# version 30
# baseline (speedup 1.0000x reference)
import contextlib

import numpy as np
import ml_dtypes
import concourse.bass as bass
import concourse.mybir as mybir
from concourse.bass_utils import run_bass_kernel_spmd

F32 = mybir.dt.float32
BF16 = mybir.dt.bfloat16
AF = mybir.ActivationFunctionType
ALU = mybir.AluOpType
AX = mybir.AxisListType

B, N, E, NH, HD, FF = 4, 4096, 1024, 16, 64, 4096
WIN = 128
TOWN = 2048
TKV = 2304
NT = TOWN // 128
NKV = TKV // 128
EC = E // 128
FC = FF // 128
EPS = 1e-5
SCL = 1.0 / 32.0
NPT = 4


def _qrange(jt):
    return max(0, jt * 128 - 256), min(TOWN, jt * 128 + 128)


def _mask_blocks(jt):
    q0, q1 = _qrange(jt)
    off = q0 - (jt * 128 - 256)
    if jt == 0:
        return [(0, q1 - q0, 0, off)]
    if jt == NKV - 1:
        return [(0, q1 - q0, 2, off)]
    out = []
    for qr_lo, qr_hi in ((0, 128), (256, 384)):
        lo = max(q0, jt * 128 - 256 + qr_lo)
        hi = min(q1, jt * 128 - 256 + qr_hi)
        if lo < hi:
            out.append((lo - q0, hi - q0, 1, lo - q0 + off))
    return out


class SemC:
    def __init__(self, h):
        self.h = h
        self.n = 0

    def bump(self, k):
        self.n += k
        return self.n


def build_nc(stop="full", dbg=()):
    nc = bass.Bass("TRN2", target_bir_lowering=False)

    dx_in = nc.dram_tensor("x", [TKV, E], F32, kind="ExternalInput").ap()
    dwq = nc.dram_tensor("wqT", [E, E], F32, kind="ExternalInput").ap()
    dwk = nc.dram_tensor("wkT", [E, E], F32, kind="ExternalInput").ap()
    dwv = nc.dram_tensor("wvT", [E, E], F32, kind="ExternalInput").ap()
    dwp = nc.dram_tensor("wpT", [E, E], F32, kind="ExternalInput").ap()
    dw1 = nc.dram_tensor("w1", [E, FF], F32, kind="ExternalInput").ap()
    dw2 = nc.dram_tensor("w2", [FF, E], F32, kind="ExternalInput").ap()
    drw = nc.dram_tensor("rw", [E, 8], F32, kind="ExternalInput").ap()
    dbq = nc.dram_tensor("bqc", [128, EC], F32, kind="ExternalInput").ap()
    dbk = nc.dram_tensor("bkc", [128, EC], F32, kind="ExternalInput").ap()
    db1 = nc.dram_tensor("b1c", [128, FC], F32, kind="ExternalInput").ap()
    dbv = nc.dram_tensor("bvb", [128, E], F32, kind="ExternalInput").ap()
    dbp = nc.dram_tensor("bpb", [128, E], F32, kind="ExternalInput").ap()
    db2 = nc.dram_tensor("b2b", [128, E], F32, kind="ExternalInput").ap()
    dmask = nc.dram_tensor("masks", [3, 128, 384], BF16, kind="ExternalInput").ap()
    dident = nc.dram_tensor("ident", [128, 128], BF16, kind="ExternalInput").ap()
    dy = nc.dram_tensor("y", [TOWN, E], F32, kind="ExternalOutput").ap()
    dx2s = nc.dram_tensor("x2s", [TOWN, E], F32, kind="Internal").ap()

    A = nc.alloc_sbuf_tensor
    xr = A("xr", [128, 2 * 1024], F32)
    x2t = A("x2t", [128, 2 * 1024], F32)
    x2r = A("x2r", [128, 2 * 256], F32)
    outr = A("outr", [128, 2 * 256], F32)
    tmpr = A("tmpr", [128, 2 * 256], F32)
    xnscr = A("xnscr", [128, 2 * 1024], BF16)
    sqscr = A("sqscr", [128, 1024], F32)
    xnfm = A("xnfm", [128, EC * TKV], BF16)
    lns = A("lns", [128, 8 * 12], F32)
    epsc = A("epsc", [128, 1], F32)
    biasb = A("biasb", [128, 1024], BF16)
    bqs = A("bqs", [128, EC], F32)
    bks = A("bks", [128, EC], F32)
    b1s = A("b1s", [128, FC], F32)
    rws = A("rws", [128, EC * 8], BF16)
    masksb = A("masksb", [128, 3 * 384], BF16)
    identb = A("identb", [128, 128], BF16)
    ptr = A("ptr", [128, NPT * 768], BF16)
    rz = A("rz", [128, 4], F32)
    scals = A("scals", [128, NT], F32)
    lgt = A("lgt", [128, 128], F32)
    rsc = A("rsc", [128, 64], F32)
    wreg = A("wreg", [128, 16 * 1024], BF16)
    w1slc = A("w1slc", [128, 2 * EC * 128], BF16)

    vbytes = NKV * 1040 * 2
    qbytes = 2 * TOWN * 2
    kbytes = 2 * TKV * 2
    wqkbytes = 4 * 1024 * 2
    atmbytes = NT * 1024 * 2
    arena_b = max(vbytes + qbytes + kbytes + wqkbytes + atmbytes,
                  FC * 1024 * 2)
    arena = A("arena", [128, arena_b], mybir.dt.uint8)
    a_addr = nc.lookup_mloc(arena).addr
    AT = nc.alloc_sbuf_tensor_at
    off = a_addr
    vp = AT("vp", [128, NKV * 1040], BF16, offset=off); off += vbytes
    slot0_off = off
    qb0 = AT("qb0", [128, TOWN], BF16, offset=off); off += TOWN * 2
    kb0 = AT("kb0", [128, TKV], BF16, offset=off); off += TKV * 2
    qb1 = AT("qb1", [128, TOWN], BF16, offset=off); off += TOWN * 2
    kb1 = AT("kb1", [128, TKV], BF16, offset=off); off += TKV * 2
    wqks_off = off
    wqks = AT("wqks", [128, 4 * 1024], BF16, offset=off); off += wqkbytes
    atm = AT("atm", [128, NT * 1024], BF16, offset=off)
    gbuf = AT("gbuf", [128, FC * 1024], BF16, offset=a_addr)
    qbs, kbs = [qb0, qb1], [kb0, kb1]
    wpts0 = AT("wpts0", [128, 4 * 1024], BF16, offset=slot0_off)
    wpts1 = AT("wpts1", [128, 4 * 1024], BF16, offset=wqks_off)
    wptsv = [wpts0, wpts1]
    w_addr = nc.lookup_mloc(wreg).addr
    wvslc = AT("wvslc", [128, 2 * EC * 512], BF16, offset=w_addr)

    stp = [nc.alloc_psum_tensor("st0", [128, 1024], F32),
           nc.alloc_psum_tensor("st1", [128, 1024], F32)]
    pp = nc.alloc_psum_tensor("pp", [128, 1024], F32)
    av = nc.alloc_psum_tensor("av", [128, 1024], F32)

    nc._dbg_handles = dict(xnfm=xnfm, vp=vp, qb=qb0, kb=kb0, atm=atm,
                           gbuf=gbuf, wpts=wpts0, wreg=wreg, ptr=ptr,
                           scals=scals, x2t=x2t, xnscr=xnscr,
                           biasb=biasb, wqks=wqks, masksb=masksb, rsc=rsc,
                           lgt=lgt, wvslc=wvslc, w1slc=w1slc, rz=rz)
    if dbg:
        dbg = [(n, nc._dbg_handles[n]) for n in dbg]
    prog = {k: [] for k in ("sync", "gpsimd", "scalar", "vector", "tensor")}

    def em(eng, f):
        prog[eng].append(f)

    stack = contextlib.ExitStack()
    sems = {}
    for name in ("dxA", "dxB", "dwA", "dwB", "dC", "dK", "stS", "stO",
                 "tMM", "aP", "vP"):
        sems[name] = SemC(stack.enter_context(nc.semaphore(name)))
    dx = [sems["dxA"], sems["dxB"]]
    dw = [sems["dwA"], sems["dwB"]]
    dC, stS, stO = sems["dC"], sems["stS"], sems["stO"]
    dK = sems["dK"]
    tMM, aP, vP = sems["tMM"], sems["aP"], sems["vP"]

    def W(eng, s, v):
        if v > 0:
            em(eng, lambda e, _h=s.h, _v=v: e.wait_ge(_h, _v))

    xnfm3 = xnfm[:].rearrange("p (c t) -> p c t", c=EC)

    def _consts_g(g):
        g.dma_start(out=masksb[:].rearrange("p (v q) -> p v q", v=3),
                    in_=dmask.rearrange("v p q -> p v q")).then_inc(dC.h, 16)
        g.dma_start(out=identb[:], in_=dident).then_inc(dC.h, 16)
        g.dma_start(out=biasb[:], in_=dbv).then_inc(dC.h, 16)
        g.dma_start(out=rws[:].rearrange("p (c x) -> p c x", c=EC),
                    in_=drw.rearrange("(c p) x -> p c x", p=128)).then_inc(dC.h, 16)
        for fh in range(2):
            g.dma_start(
                out=wvslc[:, fh * EC * 512:(fh + 1) * EC * 512]
                    .rearrange("p (c f) -> p c f", c=EC),
                in_=dwv.rearrange("(c p) f -> p c f", p=128)[:, :, fh * 512:(fh + 1) * 512],
            ).then_inc(dC.h, 16)
        for d_ap, col in ((dwq, 0), (dwk, 2 * 1024)):
            g.dma_start(
                out=wqks[:, col:col + 1024].rearrange("p (c f) -> p c f", c=EC),
                in_=d_ap.rearrange("(c p) f -> p c f", p=128)[:, :, 0:128],
            ).then_inc(dC.h, 16)

    em("gpsimd", _consts_g)
    CONSTS = dC.bump(16 * 8)

    def _consts_s(s):
        s.dma_start(out=bqs[:], in_=dbq).then_inc(dK.h, 16)
        s.dma_start(out=bks[:], in_=dbk).then_inc(dK.h, 16)
        s.dma_start(out=b1s[:], in_=db1).then_inc(dK.h, 16)

    em("sync", _consts_s)
    CONSTS2 = dK.bump(16 * 3)


    def _vones(v):
        ap = vp[:].rearrange("p (t h o) -> p t h o", t=NKV, h=NH)[:, :, :, 64:65]
        v.memset(ap, 1.0)
        v.memset(epsc[:], EPS)

    em("vector", _vones)

    LNG = 6
    lnstate = {"rdy": 0}

    def _lncol(f, slot, k, n=1):
        base = f * 12 + slot * LNG + k
        return lns[:, base:base + n]

    def emit_stats(src_ap, slot, k):
        if lnstate["rdy"]:
            W("scalar", aP, lnstate["rdy"])
        em("scalar", lambda e, a=src_ap, o=_lncol(1, slot, k): e.activation(
            sqscr[:], a, AF.Square, accum_out=o).then_inc(aP.h, 1))
        a_sq = aP.bump(1)
        W("scalar", aP, a_sq)
        em("scalar", lambda e, a=src_ap, o=_lncol(0, slot, k): e.activation(
            sqscr[:], a, AF.Identity, accum_out=o).then_inc(aP.h, 1))
        a_st = aP.bump(1)
        lnstate["rdy"] = a_st
        return [(aP, a_st)]

    def emit_ladder(slot, n, war_a=0):
        R = _lncol(0, slot, 0, n)
        S = _lncol(1, slot, 0, n)
        MT = _lncol(2, slot, 0, n)
        MR = _lncol(3, slot, 0, n)
        V_ = _lncol(4, slot, 0, n)
        L_ = _lncol(5, slot, 0, n)
        RS = _lncol(6, slot, 0, n)
        NB = _lncol(7, slot, 0, n)
        W("vector", aP, lnstate["rdy"])
        if war_a:
            W("vector", aP, war_a)
        em("vector", lambda e: e.tensor_tensor(MT, R, R, ALU.mult).then_inc(vP.h, 1))
        W("vector", vP, vP.bump(1))
        em("vector", lambda e: e.tensor_scalar(
            MR, MT, 1.0 / (E * E), -EPS, ALU.mult, ALU.add).then_inc(vP.h, 1))
        W("vector", vP, vP.bump(1))
        em("vector", lambda e: e.scalar_tensor_tensor(
            V_, S, 1.0 / E, MR, ALU.mult, ALU.subtract).then_inc(vP.h, 1))
        e5 = vP.bump(1)
        W("scalar", vP, e5)
        em("scalar", lambda e: e.activation(
            L_, V_, AF.Ln, bias=epsc[:], scale=1.0).then_inc(aP.h, 1))
        a_ln = aP.bump(1)
        W("scalar", aP, a_ln)
        em("scalar", lambda e: e.activation(
            RS, L_, AF.Exp, bias=0.0, scale=-0.5).then_inc(aP.h, 1))
        a_rstd = aP.bump(1)
        W("vector", aP, a_rstd)
        em("vector", lambda e: e.tensor_tensor(NB, R, RS, ALU.mult).then_inc(vP.h, 1))
        W("vector", vP, vP.bump(1))
        em("vector", lambda e: e.tensor_single_scalar(
            NB, NB, -1.0 / E, ALU.mult).then_inc(vP.h, 1))
        return vP.bump(1)

    ln = {"ttr": {}}

    def emit_xn(src_ap, dst_slot, slot, k, tile_id):
        if tile_id - 2 in ln["ttr"]:
            W("scalar", tMM, ln["ttr"][tile_id - 2])
        dsta = xnscr[:, dst_slot * 1024:(dst_slot + 1) * 1024]
        em("scalar", lambda e, a=src_ap, o=dsta, r=_lncol(6, slot, k),
                     b=_lncol(7, slot, k): e.activation(
            o, a, AF.Identity, bias=b, scale=r).then_inc(aP.h, 1))
        return aP.bump(1)

    tr = {"par": 0, "lastv": [0, 0], "first": True}

    def emit_transpose(src_ap, dst3_ap, wait_a, extra_t_wait=0):
        par = tr["par"]
        tr["par"] ^= 1
        avb = av[:, par * 512:(par + 1) * 512].bitcast(BF16)
        if tr["first"]:
            W("tensor", dC, CONSTS)
            tr["first"] = False
        W("tensor", aP, wait_a)
        if extra_t_wait:
            W("tensor", aP, extra_t_wait)
        W("tensor", vP, tr["lastv"][par])

        def _tr(t, s=src_ap, o=avb):
            for c in range(8):
                ins = t.transpose(o[:, c * 128:(c + 1) * 128],
                                  s[:, c * 128:(c + 1) * 128], identb[:])
                if c == 7:
                    ins.then_inc(tMM.h, 1)

        em("tensor", _tr)
        t_tr = tMM.bump(1)
        W("vector", tMM, t_tr)
        em("vector", lambda e, i0=avb.rearrange("p (c t) -> p c t", c=8),
                     o=dst3_ap: e.tensor_copy(o, i0).then_inc(vP.h, 1))
        tr["lastv"][par] = vP.bump(1)
        return t_tr, tr["lastv"][par]

    ppctr = [0]
    pp_guard = [("vP", 0), ("vP", 0)]

    def pp_acquire():
        par = ppctr[0] % 2
        ppctr[0] += 1
        gs, gv = pp_guard[par]
        W("tensor", sems[gs], gv)
        return par

    xload = [0]
    xcons = {}

    def emit_xload(src_ap2, extra=()):
        i = xload[0]
        xload[0] += 1
        sl = i % 2
        for s, v in xcons.get(i - 2, []):
            W("sync", s, v)
        for s, v in extra:
            W("sync", s, v)
        W("sync", dx[sl], dx[sl].n)
        dst = xr[:, sl * 1024:(sl + 1) * 1024]
        em("sync", lambda e, s=src_ap2, d=dst: e.dma_start(out=d, in_=s)
           .then_inc(dx[sl].h, 16))
        return i, sl, dx[sl].bump(16)

    xn_a = {}
    tV_groups = []
    agroups = [(0, LNG), (LNG, LNG), (2 * LNG, NKV - 2 * LNG)]
    ladder_rdy = {}
    xn_group_last = {}
    for gi in range(len(agroups) + 1):
        if gi < len(agroups):
            t0g, ng = agroups[gi]
            for k in range(ng):
                i = t0g + k
                ser, sl, dxv = emit_xload(dx_in[i * 128:(i + 1) * 128, :])
                W("scalar", dx[sl], dxv)
                xcons[ser] = emit_stats(xr[:, sl * 1024:(sl + 1) * 1024], gi % 2, k)
            ladder_rdy[gi] = emit_ladder(gi % 2, ng,
                                         war_a=xn_group_last.get(gi - 2, 0))
        if gi >= 1:
            t0g, ng = agroups[gi - 1]
            W("scalar", vP, ladder_rdy[gi - 1])
            rl = {0: emit_xload(dx_in[t0g * 128:(t0g + 1) * 128, :])}
            for k in range(ng):
                i = t0g + k
                if k + 1 < ng:
                    rl[k + 1] = emit_xload(
                        dx_in[(i + 1) * 128:(i + 2) * 128, :])
                ser, sl, dxv = rl[k]
                W("scalar", dx[sl], dxv)
                xn_a[i] = emit_xn(xr[:, sl * 1024:(sl + 1) * 1024], sl,
                                  (gi - 1) % 2, k, i)
                xcons[ser] = [(aP, xn_a[i])]
                t_tr, vtr_j = emit_transpose(
                    xnscr[:, sl * 1024:(sl + 1) * 1024],
                    xnfm3[:, :, i * 128:(i + 1) * 128], xn_a[i])
                ln["ttr"][i] = t_tr
                W("tensor", dC, CONSTS)
                W("tensor", vP, vtr_j)
                for fh in range(2):
                    par = pp_acquire()

                    def _vproj(t, i=i, fh=fh, par=par):
                        for ec in range(EC):
                            ins = t.matmul(
                                pp[:, par * 512:(par + 1) * 512],
                                xnfm3[:, ec, i * 128:(i + 1) * 128],
                                wvslc[:, (fh * EC + ec) * 512:(fh * EC + ec + 1) * 512],
                                start=(ec == 0), stop=(ec == EC - 1))
                            if ec == EC - 1:
                                ins.then_inc(tMM.h, 1)

                    em("tensor", _vproj)
                    t_v = tMM.bump(1)
                    tV_groups.append(t_v)
                    W("vector", tMM, t_v)
                    W("vector", dC, CONSTS)
                    dst = vp[:].rearrange("p (t h o) -> p t h o", t=NKV, h=NH)[
                        :, i, fh * 8:(fh + 1) * 8, 0:64]
                    srcp = pp[:, par * 512:(par + 1) * 512].rearrange(
                        "p (h d) -> p h d", h=8)
                    bsl = biasb[:, fh * 512:(fh + 1) * 512].rearrange(
                        "p (h d) -> p h d", h=8)
                    em("vector", lambda e, o=dst, i0=srcp, i1=bsl: e.tensor_tensor(
                        o, i0, i1, ALU.add).then_inc(vP.h, 1))
                    pp_guard[par] = ("vP", vP.bump(1))
            xn_group_last[gi - 1] = xn_a[t0g + ng - 1]

    if stop == "A":
        return _finish(nc, prog, em, W, sems, dbg, stack,
                       dict(aP=aP, vP=vP, tMM=tMM, stO=stO, stS=stS))

    qk_w_ready = {0: CONSTS}
    qk_evac_end = {}
    attn_mm_end = {}
    av_evac = {}
    avmm = {}
    qkproj_mm_end = 0
    last_st_exp = 0

    def load_qk_slices(c, wslot):
        W("gpsimd", dw[wslot], dw[wslot].n)

        def _ld(g, c=c, wslot=wslot):
            for d_ap, col in ((dwq, wslot * 1024), (dwk, (2 + wslot) * 1024)):
                g.dma_start(
                    out=wqks[:, col:col + 1024].rearrange("p (e f) -> p e f", e=EC),
                    in_=d_ap.rearrange("(e p) f -> p e f", p=128)[:, :, c * 128:(c + 1) * 128],
                ).then_inc(dw[wslot].h, 16)
        em("gpsimd", _ld)
        return dw[wslot].bump(32)

    qkproj_mm_end_m = [0]

    for it in range(EC + 1):
        c, ca = it, it - 1
        if c < EC and c + 1 < EC:
            if c - 1 >= 0:
                W("gpsimd", tMM, qkproj_mm_end_m[0])
            qk_w_ready[c + 1] = load_qk_slices(c + 1, (c + 1) % 2)

        proj_jobs = []
        if c < EC:
            for kind in ("q", "k"):
                tlen = TOWN if kind == "q" else TKV
                for s in range((tlen + 511) // 512):
                    proj_jobs.append((kind, s * 512, min(512, tlen - s * 512)))
        pj = [0]
        first_evac = [True]

        def emit_proj_group(c=c, proj_jobs=proj_jobs, pj=pj, first_evac=first_evac):
            if pj[0] >= len(proj_jobs):
                return
            kind, lo, ncol = proj_jobs[pj[0]]
            pj[0] += 1
            par = pp_acquire()
            if pj[0] == 1:
                if c == 0:
                    W("tensor", dC, CONSTS)
                else:
                    W("tensor", dw[c % 2], qk_w_ready[c])

            def _proj(t, kind=kind, c=c, lo=lo, ncol=ncol, par=par):
                wcol = (0 if kind == "q" else 2 * 1024) + (c % 2) * 1024
                src_lo = (128 + lo) if kind == "q" else lo
                for ec in range(EC):
                    ins = t.matmul(
                        pp[:, par * 512:par * 512 + ncol],
                        wqks[:, wcol + ec * 128:wcol + (ec + 1) * 128],
                        xnfm3[:, ec, src_lo:src_lo + ncol],
                        start=(ec == 0), stop=(ec == EC - 1))
                    if ec == EC - 1:
                        ins.then_inc(tMM.h, 1)

            em("tensor", _proj)
            t_g = tMM.bump(1)
            qkproj_mm_end_m[0] = t_g
            if first_evac[0]:
                if c == 0:
                    W("scalar", dK, CONSTS2)
                if c - 2 >= 0:
                    W("scalar", tMM, attn_mm_end[c - 2])
                first_evac[0] = False
            W("scalar", tMM, t_g)
            dstb = qbs[c % 2] if kind == "q" else kbs[c % 2]
            bias = (bqs if kind == "q" else bks)[:, c:c + 1]
            dsta = dstb[:, lo:lo + ncol]
            em("scalar", lambda e, o=dsta, p=par, ncol=ncol, b=bias:
               e.activation(o, pp[:, p * 512:p * 512 + ncol], AF.Identity,
                            bias=b, scale=1.0).then_inc(aP.h, 1))
            pp_guard[par] = ("aP", aP.bump(1))

        if ca < 0:
            while pj[0] < len(proj_jobs):
                emit_proj_group()
            qk_evac_end[c] = aP.n
            continue

        prev_exp_end = last_st_exp
        st_exp = {}
        mask_done = {}
        qba, kba = qbs[ca % 2], kbs[ca % 2]
        first = True
        for jt in range(NKV + 1):
            qi = jt - 3
            if 0 <= qi < NT:
                avpar = qi % 2
                if (ca, qi - 2) in av_evac:
                    W("tensor", aP, av_evac[(ca, qi - 2)])
                elif qi < 2 and ca >= 1:
                    W("tensor", aP, av_evac[(ca - 1, NT - 1)])
                W("tensor", vP, mask_done[qi + 2])

                def _av(t, qi=qi, avpar=avpar, ca=ca):
                    for hi in range(2):
                        habs = 2 * ca + hi
                        for c3 in range(3):
                            js = qi + c3
                            sl2 = js % NPT
                            o2 = qi * 128 - _qrange(js)[0]
                            ins = t.matmul(
                                av[:, avpar * 512 + hi * 65:
                                   avpar * 512 + hi * 65 + 65],
                                ptr[:, sl2 * 768 + hi * 384 + o2:
                                    sl2 * 768 + hi * 384 + o2 + 128],
                                vp[:, js * 1040 + habs * 65:
                                   js * 1040 + habs * 65 + 65],
                                start=(c3 == 0), stop=(c3 == 2))
                            if hi == 1 and c3 == 2:
                                ins.then_inc(tMM.h, 1)

                em("tensor", _av)
                avmm[(ca, qi)] = tMM.bump(1)
                W("vector", tMM, avmm[(ca, qi)])
                if (ca, qi - 2) in av_evac:
                    W("vector", aP, av_evac[(ca, qi - 2)])
                elif qi < 2 and ca >= 1:
                    W("vector", aP, av_evac[(ca - 1, NT - 1)])
                rzs = 2 * (qi % 2)

                def _rz(e, avpar=avpar, rzs=rzs):
                    src = av[:, avpar * 512:avpar * 512 + 130].rearrange(
                        "p (h o) -> p h o", h=2)[:, :, 64:65]
                    e.reciprocal(
                        rz[:, rzs:rzs + 2].rearrange("p (h o) -> p h o", h=2),
                        src).then_inc(vP.h, 1)

                em("vector", _rz)
                v_rz = vP.bump(1)
                W("scalar", vP, v_rz)
                for hi in range(2):
                    habs = 2 * ca + hi
                    o = atm[:, qi * 1024 + habs * 64:qi * 1024 + habs * 64 + 64]
                    i0 = av[:, avpar * 512 + hi * 65:avpar * 512 + hi * 65 + 64]
                    em("scalar", lambda e, o=o, i0=i0, hi=hi, rzs=rzs:
                       e.activation(o, i0, AF.Identity, bias=0.0,
                                    scale=rz[:, rzs + hi:rzs + hi + 1])
                       .then_inc(aP.h, 1))
                av_evac[(ca, qi)] = aP.bump(2)
            if jt < NKV:
                q0, q1 = _qrange(jt)
                nq = q1 - q0
                stpar = jt % 2
                if first:
                    W("tensor", aP, qk_evac_end[ca])
                    first = False
                if jt - 2 in st_exp:
                    W("tensor", aP, st_exp[jt - 2])
                elif ca >= 1:
                    W("tensor", aP, prev_exp_end)

                def _st(t, jt=jt, q0=q0, q1=q1, nq=nq, stpar=stpar, kba=kba, qba=qba):
                    for hi in range(2):
                        ins = t.matmul(
                            stp[stpar][:, hi * 512:hi * 512 + nq],
                            kba[hi * 64:hi * 64 + 64, jt * 128:(jt + 1) * 128],
                            qba[hi * 64:hi * 64 + 64, q0:q1],
                            start=True, stop=True)
                        if hi == 1:
                            ins.then_inc(tMM.h, 1)

                em("tensor", _st)
                t_st = tMM.bump(1)
                slot = jt % NPT
                W("scalar", tMM, t_st)
                if jt - NPT >= 0 and (ca, jt - NPT) in avmm:
                    W("scalar", tMM, avmm[(ca, jt - NPT)])
                elif jt < NPT and ca >= 1:
                    W("scalar", tMM, attn_mm_end[ca - 1])
                srce = stp[stpar][:].rearrange("p (h q) -> p h q", h=2)[:, :, 0:nq]
                dpt = ptr[:, slot * 768:(slot + 1) * 768].rearrange(
                    "p (h q) -> p h q", h=2)[:, :, 0:nq]
                em("scalar", lambda e, o=dpt, i0=srce: e.activation(
                    o, i0, AF.Exp, bias=0.0, scale=SCL).then_inc(aP.h, 1))
                st_exp[jt] = aP.bump(1)
                last_st_exp = st_exp[jt]
                W("vector", aP, st_exp[jt])
                W("vector", dC, CONSTS)
                nm = 0
                for lo, hi_, var, qr in _mask_blocks(jt):
                    for hh in range(2):
                        o = ptr[:, slot * 768 + hh * 384 + lo:
                                slot * 768 + hh * 384 + hi_]
                        m = masksb[:, var * 384 + qr:var * 384 + qr + hi_ - lo]
                        em("vector", lambda e, o=o, m=m: e.tensor_tensor(
                            o, o, m, ALU.mult).then_inc(vP.h, 1))
                        nm += 1
                mask_done[jt] = vP.bump(nm)
            if jt % 2 == 1:
                emit_proj_group()
        while pj[0] < len(proj_jobs):
            emit_proj_group()
        if c < EC:
            qk_evac_end[c] = aP.n
        attn_mm_end[ca] = tMM.n

    if stop == "B":
        return _finish(nc, prog, em, W, sems, dbg, stack,
                       dict(aP=aP, vP=vP, tMM=tMM, stO=stO, stS=stS))
    W("gpsimd", vP, vP.n)
    em("gpsimd", lambda g: g.dma_start(out=biasb[:], in_=dbp).then_inc(dC.h, 16))
    BP_RDY = dC.bump(16)
    W("gpsimd", tMM, attn_mm_end[EC - 2])
    W("gpsimd", aP, qk_evac_end[EC - 1])

    def _ldwp0(g):
        g.dma_start(out=wpts0[:].rearrange("p (e f) -> p e f", e=4),
                    in_=dwp.rearrange("(e p) f -> p e f", p=128)[:, 0:4, :]
                    ).then_inc(dw[0].h, 16)

    em("gpsimd", _ldwp0)
    W("gpsimd", tMM, qkproj_mm_end_m[0])

    def _ldwp1(g):
        g.dma_start(out=wpts1[:].rearrange("p (e f) -> p e f", e=4),
                    in_=dwp.rearrange("(e p) f -> p e f", p=128)[:, 4:8, :]
                    ).then_inc(dw[0].h, 16)

    em("gpsimd", _ldwp1)
    WPT_RDY = dw[0].bump(32)

    afm3 = wreg[:, 0:EC * 2048].rearrange("p (e t) -> p e t", e=EC)
    atr_done = {}
    for qi in range(NT):
        wait_a = av_evac[(EC - 1, NT - 1)] if qi < 2 else av_evac[(EC - 1, qi)]
        _, atr_done[qi] = emit_transpose(
            atm[:, qi * 1024:(qi + 1) * 1024],
            afm3[:, :, qi * 128:(qi + 1) * 128], wait_a)

    xn_c = {}
    x2_built = {}
    x2_wrote = {}
    x2_acc = {}
    x2cons = {}
    t_wp_last = 0
    xn2_evac_last = 0
    cladder = {}
    slot_last_xn = {0: xn_group_last.get(2, 0), 1: xn_group_last.get(1, 0)}
    cgroups = [(0, LNG), (LNG, LNG), (2 * LNG, NT - 2 * LNG)]
    for gi in range(len(cgroups) + 1):
        if gi < len(cgroups):
            t0g, ng = cgroups[gi]
            for k in range(ng):
                ti = t0g + k
                sl = ti % 2
                for fh in range(2):
                    par = pp_acquire()
                    W("tensor", vP, atr_done[ti])
                    W("tensor", dw[0], WPT_RDY)

                    def _wp(t, ti=ti, fh=fh, par=par):
                        for ec in range(EC):
                            ins = t.matmul(
                                pp[:, par * 512:(par + 1) * 512],
                                afm3[:, ec, ti * 128:(ti + 1) * 128],
                                wptsv[ec // 4][:, (ec % 4) * 1024 + fh * 512:
                                           (ec % 4) * 1024 + (fh + 1) * 512],
                                start=(ec == 0), stop=(ec == EC - 1))
                            if ec == EC - 1:
                                ins.then_inc(tMM.h, 1)

                    em("tensor", _wp)
                    t_wp_last = tMM.bump(1)
                    W("vector", tMM, t_wp_last)
                    W("vector", dC, BP_RDY)
                    for s, v in x2cons.get(ti - 2, []):
                        W("vector", s, v)
                    if ti - 2 >= 0:
                        W("vector", stS, stS.n)
                    o = x2t[:, sl * 1024 + fh * 512:sl * 1024 + (fh + 1) * 512]
                    em("vector", lambda e, o=o, p=par,
                                 i1=biasb[:, fh * 512:(fh + 1) * 512]:
                       e.tensor_tensor(o, pp[:, p * 512:(p + 1) * 512], i1,
                                       ALU.add).then_inc(vP.h, 1))
                    pp_guard[par] = ("vP", vP.bump(1))
                x2_built[ti] = vP.n
                W("gpsimd", vP, x2_built[ti])
                em("gpsimd", lambda g, ti=ti, sl=sl: g.dma_start(
                    out=x2t[:, sl * 1024:(sl + 1) * 1024],
                    in_=dx_in[128 + ti * 128:256 + ti * 128, :],
                    accum_op=ALU.add).then_inc(dw[sl].h, 16))
                x2_acc[ti] = dw[sl].bump(16)
                W("gpsimd", dw[sl], x2_acc[ti])
                W("gpsimd", stS, stS.n)
                em("gpsimd", lambda e, ti=ti, sl=sl: e.dma_start(
                    out=dx2s[ti * 128:(ti + 1) * 128, :],
                    in_=x2t[:, sl * 1024:(sl + 1) * 1024]).then_inc(stS.h, 16))
                x2_wrote[ti] = stS.bump(16)
                W("scalar", dw[sl], x2_acc[ti])
                x2cons[ti] = emit_stats(
                    x2t[:, sl * 1024:(sl + 1) * 1024], gi % 2, k)

            cladder[gi] = emit_ladder(gi % 2, ng, war_a=slot_last_xn[gi % 2])
        if gi >= 1:
            t0g, ng = cgroups[gi - 1]
            W("scalar", vP, cladder[gi - 1])
            jj0 = t0g
            rl = {0: emit_xload(dx2s[jj0 * 128:(jj0 + 1) * 128, :],
                                extra=[(stS, stS.n)])}
            for k in range(ng):
                j = t0g + k
                if k + 1 < ng:
                    jj = j + 1
                    rl[k + 1] = emit_xload(dx2s[jj * 128:(jj + 1) * 128, :],
                                           extra=[(stS, stS.n)])
                ser, sl, dxv = rl[k]
                W("scalar", dx[sl], dxv)
                xn_c[j] = emit_xn(xr[:, sl * 1024:(sl + 1) * 1024], sl,
                                  (gi - 1) % 2, k, NKV + j)
                xcons[ser] = [(aP, xn_c[j])]
                t_tr, ev = emit_transpose(
                    xnscr[:, sl * 1024:(sl + 1) * 1024],
                    xnfm3[:, :, j * 128:(j + 1) * 128], xn_c[j])
                ln["ttr"][NKV + j] = t_tr
                xn2_evac_last = ev
                W("tensor", dC, CONSTS)
                W("tensor", vP, ev)
                if j == 0:
                    W("tensor", aP, last_st_exp)

                def _rt(t, j=j):
                    for ec in range(EC):
                        ins = t.matmul(stp[0][0:128, j * 8:(j + 1) * 8],
                                       xnfm3[:, ec, j * 128:(j + 1) * 128],
                                       rws[:, ec * 8:(ec + 1) * 8],
                                       start=(ec == 0), stop=(ec == EC - 1))
                        if ec == EC - 1:
                            ins.then_inc(tMM.h, 1)

                em("tensor", _rt)
                rt_mm_last = tMM.bump(1)
            slot_last_xn[(gi - 1) % 2] = xn_c[t0g + ng - 1]

    W("vector", tMM, rt_mm_last)
    em("vector", lambda e: e.tensor_copy(lgt[:], stp[0][0:128, 0:128])
       .then_inc(vP.h, 1))
    W("vector", vP, vP.bump(1))
    W("scalar", vP, vP.n)
    W("scalar", aP, aP.n)
    lg2 = sqscr
    em("scalar", lambda e: e.activation(lg2[:, 0:128], lgt[:], AF.Exp,
                                        bias=0.0, scale=1.0).then_inc(aP.h, 1))
    W("vector", aP, aP.bump(1))
    lg3 = lg2[:, 0:128].rearrange("p (a b) -> p a b", a=NT)
    lgm = lgt[:].rearrange("p (a b) -> p a b", a=NT)

    def _ve2(f):
        em("vector", lambda e, f=f: f(e).then_inc(vP.h, 1))
        W("vector", vP, vP.bump(1))

    e1v = rsc[:, 0:16]
    e2v = rsc[:, 16:32]
    s12 = rsc[:, 32:48]
    e1b = e1v.rearrange("p (a b) -> p a b", b=1).broadcast_to([128, NT, 8])
    _ve2(lambda e: e.reduce_max(e1v, lg3, axis=AX.X))
    _ve2(lambda e: e.tensor_tensor(lgm, lg3, e1b, ALU.is_equal))
    _ve2(lambda e: e.scalar_tensor_tensor(lgm, lgm, -1e30, lg3, ALU.mult, ALU.add))
    _ve2(lambda e: e.reduce_max(e2v, lgm, axis=AX.X))
    _ve2(lambda e: e.tensor_tensor(s12, e1v, e2v, ALU.add))
    _ve2(lambda e: e.reciprocal(s12, s12))
    _ve2(lambda e: e.tensor_tensor(e1v, e1v, s12, ALU.mult))
    _ve2(lambda e: e.tensor_tensor(e2v, e2v, s12, ALU.mult))
    _ve2(lambda e: e.tensor_tensor(scals[:], e1v, e2v, ALU.add))

    if stop == "C":
        return _finish(nc, prog, em, W, sems, dbg, stack,
                       dict(aP=aP, vP=vP, tMM=tMM, stO=stO, stS=stS))
    ffn_regions = [pp[:, 0:512], pp[:, 512:1024],
                   stp[1][:, 0:512], stp[1][:, 512:1024]]
    ffn_guard = [("vP", 0), ("vP", 0), ("aP", last_st_exp), ("aP", last_st_exp)]
    ffn_ctr = [0]

    def ffn_acquire():
        q = ffn_ctr[0] % 4
        ffn_ctr[0] += 1
        gs, gv = ffn_guard[q]
        W("tensor", sems[gs], gv)
        return q


    W("gpsimd", vP, x2_built[NT - 1])
    em("gpsimd", lambda g: g.dma_start(out=biasb[:], in_=db2).then_inc(dC.h, 16))
    B2_RDY = dC.bump(16)
    g3 = gbuf[:].rearrange("p (f t) -> p f t", f=FC)
    gelu_mm = {}
    gelu_done = {}
    w1_rdy = {}
    w2_rdy = {}
    ffn2_mm = {}
    out_done = {}
    eserial = [0]

    def _emit_w2_load(th, hq, wslot):
        W("gpsimd", dw[wslot], dw[wslot].n)

        def _ldw2(g, hq=hq, wslot=wslot):
            for pc in range(4):
                g.dma_start(
                    out=wreg[:, wslot * FC * 256 + pc * 8 * 256:
                             wslot * FC * 256 + (pc + 1) * 8 * 256]
                        .rearrange("p (f h) -> p f h", f=8),
                    in_=dw2.rearrange("(f p) h -> p f h", p=128)[
                        :, pc * 8:(pc + 1) * 8, hq * 256:(hq + 1) * 256],
                ).then_inc(dw[wslot].h, 16)

        em("gpsimd", _ldw2)
        w2_rdy[(th, hq)] = dw[wslot].bump(64)

    for th in range(2):
        for fc in range(FC):
            wslot = fc % 2
            if fc - 2 >= 0:
                W("gpsimd", tMM, gelu_mm[(th, fc - 2)])
            elif th == 1:
                W("gpsimd", tMM, gelu_mm[(0, FC - 2 + fc)])
            elif fc == 0:
                W("gpsimd", dw[0], dw[0].n)
                W("gpsimd", dw[1], dw[1].n)

            W("gpsimd", dw[wslot], dw[wslot].n)

            def _ldw1(g, fc=fc, wslot=wslot):
                g.dma_start(
                    out=w1slc[:, wslot * EC * 128:(wslot + 1) * EC * 128]
                        .rearrange("p (e f) -> p e f", e=EC),
                    in_=dw1.rearrange("(e p) f -> p e f", p=128)[
                        :, :, fc * 128:(fc + 1) * 128],
                ).then_inc(dw[wslot].h, 16)

            em("gpsimd", _ldw1)
            w1_rdy[(th, fc)] = dw[wslot].bump(16)
            if fc in (8, 20):
                hq0 = 0 if fc == 8 else 1
                if th == 0:
                    W("gpsimd", tMM, t_wp_last)
                else:
                    W("gpsimd", tMM, ffn2_mm[(0, hq0 + 2, 7)])
                _emit_w2_load(th, hq0, hq0 % 2)
            for s in range(2):
                par = ffn_acquire()
                W("tensor", dw[wslot], w1_rdy[(th, fc)])
                if th == 0 and fc == 0 and s == 0:
                    W("tensor", vP, xn2_evac_last)

                def _f1(t, th=th, fc=fc, s=s, par=par, wslot=wslot):
                    for ec in range(EC):
                        ins = t.matmul(
                            ffn_regions[par],
                            w1slc[:, wslot * EC * 128 + ec * 128:
                                  wslot * EC * 128 + (ec + 1) * 128],
                            xnfm3[:, ec, th * 1024 + s * 512:
                                  th * 1024 + (s + 1) * 512],
                            start=(ec == 0), stop=(ec == EC - 1))
                        if ec == EC - 1:
                            ins.then_inc(tMM.h, 1)

                em("tensor", _f1)
                t_f1 = tMM.bump(1)
                if s == 1:
                    gelu_mm[(th, fc)] = t_f1
                W("scalar", tMM, t_f1)
                if th == 0 and fc == 0 and s == 0:
                    W("scalar", dK, CONSTS2)
                    W("scalar", tMM, attn_mm_end[EC - 1])
                    W("scalar", vP, atr_done[NT - 1])
                    W("scalar", tMM, t_wp_last)
                if th == 1 and fc == 0 and s == 0:
                    W("scalar", tMM, ffn2_mm[(0, 3, 7)])
                o = g3[:, fc, s * 512:(s + 1) * 512]
                em("scalar", lambda e, o=o, p=par, b=b1s[:, fc:fc + 1]:
                   e.activation(o, ffn_regions[p],
                                AF.Gelu_apprx_tanh, bias=b,
                                scale=1.0).then_inc(aP.h, 1))
                ffn_guard[par] = ("aP", aP.bump(1))
        gelu_done[th] = aP.n

        for hq in range(4):
            wslot = hq % 2
            if hq - 2 >= 0:
                W("gpsimd", tMM, ffn2_mm[(th, hq - 2, 7)])
                _emit_w2_load(th, hq, wslot)
            for ti in range(8):
                absti = th * 8 + ti
                par = ffn_acquire()
                W("tensor", dw[wslot], w2_rdy[(th, hq)])
                W("tensor", aP, gelu_done[th])

                def _f2(t, ti=ti, par=par, wslot=wslot):
                    for fc in range(FC):
                        ins = t.matmul(
                            ffn_regions[par][:, 0:256],
                            g3[:, fc, ti * 128:(ti + 1) * 128],
                            wreg[:, wslot * FC * 256 + fc * 256:
                                 wslot * FC * 256 + (fc + 1) * 256],
                            start=(fc == 0), stop=(fc == FC - 1))
                        if fc == FC - 1:
                            ins.then_inc(tMM.h, 1)

                em("tensor", _f2)
                ffn2_mm[(th, hq, ti)] = tMM.bump(1)
                ser = eserial[0]
                eserial[0] += 1
                xsl = ser % 2
                if ser - 2 >= 0:
                    W("sync", vP, out_done[ser - 2])
                W("sync", stS, stS.n)
                W("sync", dx[xsl], dx[xsl].n)
                em("sync", lambda e, absti=absti, hq=hq, xsl=xsl: e.dma_start(
                    out=x2r[:, xsl * 256:(xsl + 1) * 256],
                    in_=dx2s[absti * 128:(absti + 1) * 128,
                             hq * 256:(hq + 1) * 256]).then_inc(dx[xsl].h, 16))
                dxv = dx[xsl].bump(16)
                W("vector", tMM, ffn2_mm[(th, hq, ti)])
                W("vector", dx[xsl], dxv)
                W("vector", dC, B2_RDY)
                osl = ser % 2
                em("vector", lambda e, osl=osl, p=par, hq=hq: e.tensor_tensor(
                    tmpr[:, osl * 256:(osl + 1) * 256],
                    ffn_regions[p][:, 0:256],
                    biasb[:, hq * 256:(hq + 1) * 256],
                    ALU.add).then_inc(vP.h, 1))
                ffn_guard[par] = ("vP", vP.bump(1))
                W("vector", vP, ffn_guard[par][1])
                if ser >= 1:
                    W("vector", stO, 16 * (ser - 1))
                em("vector", lambda e, osl=osl, xsl=xsl, absti=absti:
                   e.scalar_tensor_tensor(
                       outr[:, osl * 256:(osl + 1) * 256],
                       tmpr[:, osl * 256:(osl + 1) * 256],
                       scals[:, absti:absti + 1],
                       x2r[:, xsl * 256:(xsl + 1) * 256],
                       ALU.mult, ALU.add).then_inc(vP.h, 1))
                out_done[ser] = vP.bump(1)
                W("gpsimd", vP, out_done[ser])
                W("gpsimd", stO, stO.n)
                em("gpsimd", lambda e, osl=osl, absti=absti, hq=hq: e.dma_start(
                    out=dy[absti * 128:(absti + 1) * 128, hq * 256:(hq + 1) * 256],
                    in_=outr[:, osl * 256:(osl + 1) * 256]).then_inc(stO.h, 16))
                stO.bump(16)

    W("gpsimd", stO, stO.n)
    W("sync", stO, stO.n)
    if dbg:
        return _finish(nc, prog, em, W, sems, dbg, stack,
                       dict(aP=aP, vP=vP, tMM=tMM, stO=stO, stS=stS))

    return _replay(nc, prog, stack)


def _replay(nc, prog, stack):
    with nc.Block() as blk:
        @blk.sync
        def _(e):
            for f in prog["sync"]:
                f(e)

        @blk.gpsimd
        def _(e):
            for f in prog["gpsimd"]:
                f(e)

        @blk.scalar
        def _(e):
            for f in prog["scalar"]:
                f(e)

        @blk.vector
        def _(e):
            for f in prog["vector"]:
                f(e)

        @blk.tensor
        def _(e):
            for f in prog["tensor"]:
                f(e)
    stack.close()
    return nc


def _finish(nc, prog, em, W, sems, dbg, stack, s):
    aP, vP, tMM, stO = s["aP"], s["vP"], s["tMM"], s["stO"]
    W("sync", aP, aP.n)
    W("sync", vP, vP.n)
    W("sync", tMM, tMM.n)
    ndma = 0
    for name, hh in dbg:
        shp = [hh.shape[0], hh.shape[1]]
        d = nc.dram_tensor("dbg_" + name, shp, hh.dtype, kind="ExternalOutput").ap()
        em("sync", lambda e, d=d, hh=hh: e.dma_start(out=d[:, :], in_=hh[:])
           .then_inc(stO.h, 16))
        ndma += 1
        stO.bump(16)
    W("sync", stO, stO.n)
    return _replay(nc, prog, stack)


def _host_inputs(inputs):
    x = np.asarray(inputs["L0_attn_x"], dtype=np.float32)
    wqT = np.ascontiguousarray(np.asarray(inputs["L0_attn_Wq"], np.float32).T)
    wkT = np.ascontiguousarray(np.asarray(inputs["L0_attn_Wk"], np.float32).T)
    wvT = np.ascontiguousarray(np.asarray(inputs["L0_attn_Wv"], np.float32).T)
    wpT = np.ascontiguousarray(np.asarray(inputs["L0_attn_Wp"], np.float32).T)
    w1 = np.ascontiguousarray(np.asarray(inputs["L0_ffn_w1"], np.float32))
    w2 = np.ascontiguousarray(np.asarray(inputs["L0_ffn_w2"], np.float32))
    rw = np.ascontiguousarray(np.asarray(inputs["L0_ffn_router_w"], np.float32))
    bq = np.asarray(inputs["L0_attn_bq"], np.float32)
    bk = np.asarray(inputs["L0_attn_bk"], np.float32)
    bv = np.asarray(inputs["L0_attn_bv"], np.float32)
    bp = np.asarray(inputs["L0_attn_bp"], np.float32)
    b1 = np.asarray(inputs["L0_ffn_b1"], np.float32)
    b2 = np.asarray(inputs["L0_ffn_b2"], np.float32)

    bqc = np.ascontiguousarray(bq.reshape(EC, 128).T)
    bkc = np.ascontiguousarray(bk.reshape(EC, 128).T)
    b1c = np.ascontiguousarray(b1.reshape(FC, 128).T)
    bvb = np.ascontiguousarray(np.broadcast_to(bv, (128, E)))
    bpb = np.ascontiguousarray(np.broadcast_to(bp, (128, E)))
    b2b = np.ascontiguousarray(np.broadcast_to(b2, (128, E)))
    ident = np.eye(128, dtype=ml_dtypes.bfloat16)

    jr = np.arange(128)[:, None]
    qr = np.arange(384)[None, :]
    interior = ((qr >= jr) & (qr <= jr + 256)).astype(np.float32)

    in_maps = []
    for c in range(8):
        b, half = c // 2, c % 2
        start = half * TOWN
        lo, hi = start - WIN, start + TOWN + WIN
        xin = np.zeros((TKV, E), np.float32)
        vlo, vhi = max(lo, 0), min(hi, N)
        xin[vlo - lo:vhi - lo] = x[b, vlo:vhi]
        v0 = interior if half == 1 else np.zeros_like(interior)
        v2 = interior if half == 0 else np.zeros_like(interior)
        masks = np.stack([v0, interior, v2]).astype(ml_dtypes.bfloat16)
        in_maps.append({
            "x": xin, "wqT": wqT, "wkT": wkT, "wvT": wvT, "wpT": wpT,
            "w1": w1, "w2": w2, "rw": rw,
            "bqc": bqc, "bkc": bkc, "b1c": b1c,
            "bvb": bvb, "bpb": bpb, "b2b": b2b,
            "masks": masks, "ident": ident,
        })
    return in_maps


def kernel(**inputs):
    in_maps = _host_inputs(inputs)
    nc = build_nc()
    res = run_bass_kernel_spmd(nc, in_maps, core_ids=list(range(8)))
    y = np.empty((B, N, E), np.float32)
    for c in range(8):
        b, half = c // 2, c % 2
        y[b, half * TOWN:(half + 1) * TOWN] = res.results[c]["y"]
    return y


# version 31
# speedup vs baseline: 1.1305x; 1.1305x over previous
import contextlib

import numpy as np
import ml_dtypes
import concourse.bass as bass
import concourse.mybir as mybir
from concourse.bass_utils import run_bass_kernel_spmd

F32 = mybir.dt.float32
BF16 = mybir.dt.bfloat16
AF = mybir.ActivationFunctionType
ALU = mybir.AluOpType
AX = mybir.AxisListType

B, N, E, NH, HD, FF = 4, 4096, 1024, 16, 64, 4096
WIN = 128
TOWN = 2048
TKV = 2304
NT = TOWN // 128
NKV = TKV // 128
EC = E // 128
FC = FF // 128
EPS = 1e-5
SCL = 1.0 / 32.0
NPT = 4


def _qrange(jt):
    return max(0, jt * 128 - 256), min(TOWN, jt * 128 + 128)


def _mask_blocks(jt):
    q0, q1 = _qrange(jt)
    off = q0 - (jt * 128 - 256)
    if jt == 0:
        return [(0, q1 - q0, 0, off)]
    if jt == NKV - 1:
        return [(0, q1 - q0, 2, off)]
    out = []
    for qr_lo, qr_hi in ((0, 128), (256, 384)):
        lo = max(q0, jt * 128 - 256 + qr_lo)
        hi = min(q1, jt * 128 - 256 + qr_hi)
        if lo < hi:
            out.append((lo - q0, hi - q0, 1, lo - q0 + off))
    return out


class SemC:
    def __init__(self, h):
        self.h = h
        self.n = 0

    def bump(self, k):
        self.n += k
        return self.n


def build_nc(stop="full", dbg=()):
    nc = bass.Bass("TRN2", target_bir_lowering=False)

    dx_in = nc.dram_tensor("x", [TKV, E], F32, kind="ExternalInput").ap()
    dwq = nc.dram_tensor("wqT", [E, E], F32, kind="ExternalInput").ap()
    dwk = nc.dram_tensor("wkT", [E, E], F32, kind="ExternalInput").ap()
    dwv = nc.dram_tensor("wvT", [E, E], F32, kind="ExternalInput").ap()
    dwp = nc.dram_tensor("wpT", [E, E], F32, kind="ExternalInput").ap()
    dw1 = nc.dram_tensor("w1", [E, FF], F32, kind="ExternalInput").ap()
    dw2 = nc.dram_tensor("w2", [FF, E], F32, kind="ExternalInput").ap()
    drw = nc.dram_tensor("rw", [E, 8], F32, kind="ExternalInput").ap()
    dbq = nc.dram_tensor("bqc", [128, EC], F32, kind="ExternalInput").ap()
    dbk = nc.dram_tensor("bkc", [128, EC], F32, kind="ExternalInput").ap()
    db1 = nc.dram_tensor("b1c", [128, FC], F32, kind="ExternalInput").ap()
    dbv = nc.dram_tensor("bvb", [128, E], F32, kind="ExternalInput").ap()
    dbp = nc.dram_tensor("bpb", [128, E], F32, kind="ExternalInput").ap()
    db2 = nc.dram_tensor("b2b", [128, E], F32, kind="ExternalInput").ap()
    dmask = nc.dram_tensor("masks", [3, 128, 384], BF16, kind="ExternalInput").ap()
    dident = nc.dram_tensor("ident", [128, 128], BF16, kind="ExternalInput").ap()
    dy = nc.dram_tensor("y", [TOWN, E], F32, kind="ExternalOutput").ap()
    dx2s = nc.dram_tensor("x2s", [TOWN, E], F32, kind="Internal").ap()

    A = nc.alloc_sbuf_tensor
    xr = A("xr", [128, 2 * 1024], F32)
    x2t = A("x2t", [128, 2 * 1024], F32)
    x2r = A("x2r", [128, 2 * 256], F32)
    outr = A("outr", [128, 2 * 256], F32)
    tmpr = A("tmpr", [128, 2 * 256], F32)
    xnscr = A("xnscr", [128, 2 * 1024], BF16)
    sqscr = A("sqscr", [128, 1024], F32)
    xnfm = A("xnfm", [128, EC * TKV], BF16)
    lns = A("lns", [128, 8 * 12], F32)
    epsc = A("epsc", [128, 1], F32)
    biasb = A("biasb", [128, 1024], BF16)
    bqs = A("bqs", [128, EC], F32)
    bks = A("bks", [128, EC], F32)
    b1s = A("b1s", [128, FC], F32)
    rws = A("rws", [128, EC * 8], BF16)
    masksb = A("masksb", [128, 3 * 384], BF16)
    identb = A("identb", [128, 128], BF16)
    ptr = A("ptr", [128, NPT * 768], BF16)
    rz = A("rz", [128, 4], F32)
    scals = A("scals", [128, NT], F32)
    lgt = A("lgt", [128, 128], F32)
    rsc = A("rsc", [128, 64], F32)
    wreg = A("wreg", [128, 16 * 1024], BF16)
    w1slc = A("w1slc", [128, 2 * EC * 128], BF16)

    vbytes = NKV * 1040 * 2
    qbytes = 2 * TOWN * 2
    kbytes = 2 * TKV * 2
    wqkbytes = 4 * 1024 * 2
    atmbytes = NT * 1024 * 2
    arena_b = max(vbytes + qbytes + kbytes + wqkbytes + atmbytes,
                  FC * 1024 * 2)
    arena = A("arena", [128, arena_b], mybir.dt.uint8)
    a_addr = nc.lookup_mloc(arena).addr
    AT = nc.alloc_sbuf_tensor_at
    off = a_addr
    vp = AT("vp", [128, NKV * 1040], BF16, offset=off); off += vbytes
    slot0_off = off
    qb0 = AT("qb0", [128, TOWN], BF16, offset=off); off += TOWN * 2
    kb0 = AT("kb0", [128, TKV], BF16, offset=off); off += TKV * 2
    qb1 = AT("qb1", [128, TOWN], BF16, offset=off); off += TOWN * 2
    kb1 = AT("kb1", [128, TKV], BF16, offset=off); off += TKV * 2
    wqks_off = off
    wqks = AT("wqks", [128, 4 * 1024], BF16, offset=off); off += wqkbytes
    atm = AT("atm", [128, NT * 1024], BF16, offset=off)
    gbuf = AT("gbuf", [128, FC * 1024], BF16, offset=a_addr)
    xpre = AT("xpre", [128, 4 * 1024], F32, offset=a_addr + FC * 1024 * 2)
    qbs, kbs = [qb0, qb1], [kb0, kb1]
    wpts0 = AT("wpts0", [128, 4 * 1024], BF16, offset=slot0_off)
    wpts1 = AT("wpts1", [128, 4 * 1024], BF16, offset=wqks_off)
    wptsv = [wpts0, wpts1]
    w_addr = nc.lookup_mloc(wreg).addr
    wvslc = AT("wvslc", [128, 2 * EC * 512], BF16, offset=w_addr)

    stp = [nc.alloc_psum_tensor("st0", [128, 1024], F32),
           nc.alloc_psum_tensor("st1", [128, 1024], F32)]
    pp = nc.alloc_psum_tensor("pp", [128, 1024], F32)
    av = nc.alloc_psum_tensor("av", [128, 1024], F32)

    nc._dbg_handles = dict(xnfm=xnfm, vp=vp, qb=qb0, kb=kb0, atm=atm,
                           gbuf=gbuf, wpts=wpts0, wreg=wreg, ptr=ptr,
                           scals=scals, x2t=x2t, xnscr=xnscr,
                           biasb=biasb, wqks=wqks, masksb=masksb, rsc=rsc,
                           lgt=lgt, wvslc=wvslc, w1slc=w1slc, rz=rz)
    if dbg:
        dbg = [(n, nc._dbg_handles[n]) for n in dbg]
    prog = {k: [] for k in ("sync", "gpsimd", "scalar", "vector", "tensor")}

    def em(eng, f):
        prog[eng].append(f)

    stack = contextlib.ExitStack()
    sems = {}
    for name in ("dxA", "dxB", "dwA", "dwB", "dC", "dK", "stS", "stO",
                 "dP", "dQ", "tMM", "aP", "vP"):
        sems[name] = SemC(stack.enter_context(nc.semaphore(name)))
    dx = [sems["dxA"], sems["dxB"]]
    dw = [sems["dwA"], sems["dwB"]]
    dC, stS, stO = sems["dC"], sems["stS"], sems["stO"]
    dK = sems["dK"]
    dP, dQ = sems["dP"], sems["dQ"]
    tMM, aP, vP = sems["tMM"], sems["aP"], sems["vP"]

    def W(eng, s, v):
        if v > 0:
            em(eng, lambda e, _h=s.h, _v=v: e.wait_ge(_h, _v))

    xnfm3 = xnfm[:].rearrange("p (c t) -> p c t", c=EC)

    def _consts_g(g):
        g.dma_start(out=masksb[:].rearrange("p (v q) -> p v q", v=3),
                    in_=dmask.rearrange("v p q -> p v q")).then_inc(dC.h, 16)
        g.dma_start(out=identb[:], in_=dident).then_inc(dC.h, 16)
        g.dma_start(out=biasb[:], in_=dbv).then_inc(dC.h, 16)
        g.dma_start(out=rws[:].rearrange("p (c x) -> p c x", c=EC),
                    in_=drw.rearrange("(c p) x -> p c x", p=128)).then_inc(dC.h, 16)
        for fh in range(2):
            g.dma_start(
                out=wvslc[:, fh * EC * 512:(fh + 1) * EC * 512]
                    .rearrange("p (c f) -> p c f", c=EC),
                in_=dwv.rearrange("(c p) f -> p c f", p=128)[:, :, fh * 512:(fh + 1) * 512],
            ).then_inc(dC.h, 16)
        for d_ap, col in ((dwq, 0), (dwk, 2 * 1024)):
            g.dma_start(
                out=wqks[:, col:col + 1024].rearrange("p (c f) -> p c f", c=EC),
                in_=d_ap.rearrange("(c p) f -> p c f", p=128)[:, :, 0:128],
            ).then_inc(dC.h, 16)

    em("gpsimd", _consts_g)
    CONSTS = dC.bump(16 * 8)

    def _consts_s(s):
        s.dma_start(out=bqs[:], in_=dbq).then_inc(dK.h, 16)
        s.dma_start(out=bks[:], in_=dbk).then_inc(dK.h, 16)
        s.dma_start(out=b1s[:], in_=db1).then_inc(dK.h, 16)

    em("sync", _consts_s)
    CONSTS2 = dK.bump(16 * 3)


    def _vones(v):
        ap = vp[:].rearrange("p (t h o) -> p t h o", t=NKV, h=NH)[:, :, :, 64:65]
        v.memset(ap, 1.0)
        v.memset(epsc[:], EPS)

    em("vector", _vones)

    LNG = 6
    lnstate = {"rdy": 0}

    def _lncol(f, slot, k, n=1):
        base = f * 12 + slot * LNG + k
        return lns[:, base:base + n]

    def emit_stats(src_ap, slot, k):
        if lnstate["rdy"]:
            W("scalar", aP, lnstate["rdy"])
        em("scalar", lambda e, a=src_ap, o=_lncol(1, slot, k): e.activation(
            sqscr[:], a, AF.Square, accum_out=o).then_inc(aP.h, 1))
        a_sq = aP.bump(1)
        W("scalar", aP, a_sq)
        em("scalar", lambda e, a=src_ap, o=_lncol(0, slot, k): e.activation(
            sqscr[:], a, AF.Identity, accum_out=o).then_inc(aP.h, 1))
        a_st = aP.bump(1)
        lnstate["rdy"] = a_st
        return [(aP, a_st)]

    def emit_ladder(slot, n, war_a=0):
        R = _lncol(0, slot, 0, n)
        S = _lncol(1, slot, 0, n)
        MT = _lncol(2, slot, 0, n)
        MR = _lncol(3, slot, 0, n)
        V_ = _lncol(4, slot, 0, n)
        L_ = _lncol(5, slot, 0, n)
        RS = _lncol(6, slot, 0, n)
        NB = _lncol(7, slot, 0, n)
        W("vector", aP, lnstate["rdy"])
        if war_a:
            W("vector", aP, war_a)
        em("vector", lambda e: e.tensor_tensor(MT, R, R, ALU.mult).then_inc(vP.h, 1))
        W("vector", vP, vP.bump(1))
        em("vector", lambda e: e.tensor_scalar(
            MR, MT, 1.0 / (E * E), -EPS, ALU.mult, ALU.add).then_inc(vP.h, 1))
        W("vector", vP, vP.bump(1))
        em("vector", lambda e: e.scalar_tensor_tensor(
            V_, S, 1.0 / E, MR, ALU.mult, ALU.subtract).then_inc(vP.h, 1))
        e5 = vP.bump(1)
        W("scalar", vP, e5)
        em("scalar", lambda e: e.activation(
            L_, V_, AF.Ln, bias=epsc[:], scale=1.0).then_inc(aP.h, 1))
        a_ln = aP.bump(1)
        W("scalar", aP, a_ln)
        em("scalar", lambda e: e.activation(
            RS, L_, AF.Exp, bias=0.0, scale=-0.5).then_inc(aP.h, 1))
        a_rstd = aP.bump(1)
        W("vector", aP, a_rstd)
        em("vector", lambda e: e.tensor_tensor(NB, R, RS, ALU.mult).then_inc(vP.h, 1))
        W("vector", vP, vP.bump(1))
        em("vector", lambda e: e.tensor_single_scalar(
            NB, NB, -1.0 / E, ALU.mult).then_inc(vP.h, 1))
        return vP.bump(1)

    ln = {"ttr": {}}

    def emit_xn(src_ap, dst_slot, slot, k, tile_id):
        if tile_id - 2 in ln["ttr"]:
            W("scalar", tMM, ln["ttr"][tile_id - 2])
        dsta = xnscr[:, dst_slot * 1024:(dst_slot + 1) * 1024]
        em("scalar", lambda e, a=src_ap, o=dsta, r=_lncol(6, slot, k),
                     b=_lncol(7, slot, k): e.activation(
            o, a, AF.Identity, bias=b, scale=r).then_inc(aP.h, 1))
        return aP.bump(1)

    tr = {"par": 0, "lastv": [0, 0], "first": True}

    def emit_transpose(src_ap, dst3_ap, wait_a, extra_t_wait=0):
        par = tr["par"]
        tr["par"] ^= 1
        avb = av[:, par * 512:(par + 1) * 512].bitcast(BF16)
        if tr["first"]:
            W("tensor", dC, CONSTS)
            tr["first"] = False
        W("tensor", aP, wait_a)
        if extra_t_wait:
            W("tensor", aP, extra_t_wait)
        W("tensor", vP, tr["lastv"][par])

        def _tr(t, s=src_ap, o=avb):
            for c in range(8):
                ins = t.transpose(o[:, c * 128:(c + 1) * 128],
                                  s[:, c * 128:(c + 1) * 128], identb[:])
                if c == 7:
                    ins.then_inc(tMM.h, 1)

        em("tensor", _tr)
        t_tr = tMM.bump(1)
        W("vector", tMM, t_tr)
        em("vector", lambda e, i0=avb.rearrange("p (c t) -> p c t", c=8),
                     o=dst3_ap: e.tensor_copy(o, i0).then_inc(vP.h, 1))
        tr["lastv"][par] = vP.bump(1)
        return t_tr, tr["lastv"][par]

    ppctr = [0]
    pp_guard = [("vP", 0), ("vP", 0)]

    def pp_acquire():
        par = ppctr[0] % 2
        ppctr[0] += 1
        gs, gv = pp_guard[par]
        W("tensor", sems[gs], gv)
        return par

    xload = [0]
    xcons = {}

    def emit_xload(src_ap2, extra=()):
        i = xload[0]
        xload[0] += 1
        sl = i % 2
        for s, v in xcons.get(i - 2, []):
            W("sync", s, v)
        for s, v in extra:
            W("sync", s, v)
        W("sync", dx[sl], dx[sl].n)
        dst = xr[:, sl * 1024:(sl + 1) * 1024]
        em("sync", lambda e, s=src_ap2, d=dst: e.dma_start(out=d, in_=s)
           .then_inc(dx[sl].h, 16))
        return i, sl, dx[sl].bump(16)

    xn_a = {}
    tV_groups = []
    agroups = [(0, LNG), (LNG, LNG), (2 * LNG, NKV - 2 * LNG)]
    ladder_rdy = {}
    xn_group_last = {}
    for gi in range(len(agroups) + 1):
        if gi < len(agroups):
            t0g, ng = agroups[gi]
            for k in range(ng):
                i = t0g + k
                ser, sl, dxv = emit_xload(dx_in[i * 128:(i + 1) * 128, :])
                W("scalar", dx[sl], dxv)
                xcons[ser] = emit_stats(xr[:, sl * 1024:(sl + 1) * 1024], gi % 2, k)
            ladder_rdy[gi] = emit_ladder(gi % 2, ng,
                                         war_a=xn_group_last.get(gi - 2, 0))
        if gi >= 1:
            t0g, ng = agroups[gi - 1]
            W("scalar", vP, ladder_rdy[gi - 1])
            rl = {0: emit_xload(dx_in[t0g * 128:(t0g + 1) * 128, :])}
            for k in range(ng):
                i = t0g + k
                if k + 1 < ng:
                    rl[k + 1] = emit_xload(
                        dx_in[(i + 1) * 128:(i + 2) * 128, :])
                ser, sl, dxv = rl[k]
                W("scalar", dx[sl], dxv)
                xn_a[i] = emit_xn(xr[:, sl * 1024:(sl + 1) * 1024], sl,
                                  (gi - 1) % 2, k, i)
                xcons[ser] = [(aP, xn_a[i])]
                t_tr, vtr_j = emit_transpose(
                    xnscr[:, sl * 1024:(sl + 1) * 1024],
                    xnfm3[:, :, i * 128:(i + 1) * 128], xn_a[i])
                ln["ttr"][i] = t_tr
                W("tensor", dC, CONSTS)
                W("tensor", vP, vtr_j)
                for fh in range(2):
                    par = pp_acquire()

                    def _vproj(t, i=i, fh=fh, par=par):
                        for ec in range(EC):
                            ins = t.matmul(
                                pp[:, par * 512:(par + 1) * 512],
                                xnfm3[:, ec, i * 128:(i + 1) * 128],
                                wvslc[:, (fh * EC + ec) * 512:(fh * EC + ec + 1) * 512],
                                start=(ec == 0), stop=(ec == EC - 1))
                            if ec == EC - 1:
                                ins.then_inc(tMM.h, 1)

                    em("tensor", _vproj)
                    t_v = tMM.bump(1)
                    tV_groups.append(t_v)
                    W("vector", tMM, t_v)
                    W("vector", dC, CONSTS)
                    dst = vp[:].rearrange("p (t h o) -> p t h o", t=NKV, h=NH)[
                        :, i, fh * 8:(fh + 1) * 8, 0:64]
                    srcp = pp[:, par * 512:(par + 1) * 512].rearrange(
                        "p (h d) -> p h d", h=8)
                    bsl = biasb[:, fh * 512:(fh + 1) * 512].rearrange(
                        "p (h d) -> p h d", h=8)
                    em("vector", lambda e, o=dst, i0=srcp, i1=bsl: e.tensor_tensor(
                        o, i0, i1, ALU.add).then_inc(vP.h, 1))
                    pp_guard[par] = ("vP", vP.bump(1))
            xn_group_last[gi - 1] = xn_a[t0g + ng - 1]

    if stop == "A":
        return _finish(nc, prog, em, W, sems, dbg, stack,
                       dict(aP=aP, vP=vP, tMM=tMM, stO=stO, stS=stS))

    qk_w_ready = {0: CONSTS}
    qk_evac_end = {}
    attn_mm_end = {}
    av_evac = {}
    avmm = {}
    qkproj_mm_end = 0
    last_st_exp = 0

    def load_qk_slices(c, wslot):
        W("gpsimd", dw[wslot], dw[wslot].n)

        def _ld(g, c=c, wslot=wslot):
            for d_ap, col in ((dwq, wslot * 1024), (dwk, (2 + wslot) * 1024)):
                g.dma_start(
                    out=wqks[:, col:col + 1024].rearrange("p (e f) -> p e f", e=EC),
                    in_=d_ap.rearrange("(e p) f -> p e f", p=128)[:, :, c * 128:(c + 1) * 128],
                ).then_inc(dw[wslot].h, 16)
        em("gpsimd", _ld)
        return dw[wslot].bump(32)

    qkproj_mm_end_m = [0]

    for it in range(EC + 1):
        c, ca = it, it - 1
        if c < EC and c + 1 < EC:
            if c - 1 >= 0:
                W("gpsimd", tMM, qkproj_mm_end_m[0])
            qk_w_ready[c + 1] = load_qk_slices(c + 1, (c + 1) % 2)

        proj_jobs = []
        if c < EC:
            for kind in ("q", "k"):
                tlen = TOWN if kind == "q" else TKV
                for s in range((tlen + 511) // 512):
                    proj_jobs.append((kind, s * 512, min(512, tlen - s * 512)))
        pj = [0]
        first_evac = [True]

        def emit_proj_group(c=c, proj_jobs=proj_jobs, pj=pj, first_evac=first_evac):
            if pj[0] >= len(proj_jobs):
                return
            kind, lo, ncol = proj_jobs[pj[0]]
            pj[0] += 1
            par = pp_acquire()
            if pj[0] == 1:
                if c == 0:
                    W("tensor", dC, CONSTS)
                else:
                    W("tensor", dw[c % 2], qk_w_ready[c])

            def _proj(t, kind=kind, c=c, lo=lo, ncol=ncol, par=par):
                wcol = (0 if kind == "q" else 2 * 1024) + (c % 2) * 1024
                src_lo = (128 + lo) if kind == "q" else lo
                for ec in range(EC):
                    ins = t.matmul(
                        pp[:, par * 512:par * 512 + ncol],
                        wqks[:, wcol + ec * 128:wcol + (ec + 1) * 128],
                        xnfm3[:, ec, src_lo:src_lo + ncol],
                        start=(ec == 0), stop=(ec == EC - 1))
                    if ec == EC - 1:
                        ins.then_inc(tMM.h, 1)

            em("tensor", _proj)
            t_g = tMM.bump(1)
            qkproj_mm_end_m[0] = t_g
            if first_evac[0]:
                if c == 0:
                    W("scalar", dK, CONSTS2)
                if c - 2 >= 0:
                    W("scalar", tMM, attn_mm_end[c - 2])
                first_evac[0] = False
            W("scalar", tMM, t_g)
            dstb = qbs[c % 2] if kind == "q" else kbs[c % 2]
            bias = (bqs if kind == "q" else bks)[:, c:c + 1]
            dsta = dstb[:, lo:lo + ncol]
            em("scalar", lambda e, o=dsta, p=par, ncol=ncol, b=bias:
               e.activation(o, pp[:, p * 512:p * 512 + ncol], AF.Identity,
                            bias=b, scale=1.0).then_inc(aP.h, 1))
            pp_guard[par] = ("aP", aP.bump(1))

        if ca < 0:
            while pj[0] < len(proj_jobs):
                emit_proj_group()
            qk_evac_end[c] = aP.n
            continue

        prev_exp_end = last_st_exp
        st_exp = {}
        mask_done = {}
        qba, kba = qbs[ca % 2], kbs[ca % 2]
        first = True
        for jt in range(NKV + 1):
            if jt < NKV:
                q0, q1 = _qrange(jt)
                nq = q1 - q0
                stpar = jt % 2
                if first:
                    W("tensor", aP, qk_evac_end[ca])
                    first = False
                if jt - 2 in st_exp:
                    W("tensor", aP, st_exp[jt - 2])
                elif ca >= 1:
                    W("tensor", aP, prev_exp_end)

                def _st(t, jt=jt, q0=q0, q1=q1, nq=nq, stpar=stpar, kba=kba, qba=qba):
                    for hi in range(2):
                        ins = t.matmul(
                            stp[stpar][:, hi * 512:hi * 512 + nq],
                            kba[hi * 64:hi * 64 + 64, jt * 128:(jt + 1) * 128],
                            qba[hi * 64:hi * 64 + 64, q0:q1],
                            start=True, stop=True)
                        if hi == 1:
                            ins.then_inc(tMM.h, 1)

                em("tensor", _st)
                t_st = tMM.bump(1)
                slot = jt % NPT
                W("scalar", tMM, t_st)
                if jt - NPT >= 0 and (ca, jt - NPT) in avmm:
                    W("scalar", tMM, avmm[(ca, jt - NPT)])
                elif jt < NPT and ca >= 1:
                    W("scalar", tMM, attn_mm_end[ca - 1])
                srce = stp[stpar][:].rearrange("p (h q) -> p h q", h=2)[:, :, 0:nq]
                dpt = ptr[:, slot * 768:(slot + 1) * 768].rearrange(
                    "p (h q) -> p h q", h=2)[:, :, 0:nq]
                em("scalar", lambda e, o=dpt, i0=srce: e.activation(
                    o, i0, AF.Exp, bias=0.0, scale=SCL).then_inc(aP.h, 1))
                st_exp[jt] = aP.bump(1)
                last_st_exp = st_exp[jt]
                W("vector", aP, st_exp[jt])
                W("vector", dC, CONSTS)
                nm = 0
                for lo, hi_, var, qr in _mask_blocks(jt):
                    for hh in range(2):
                        o = ptr[:, slot * 768 + hh * 384 + lo:
                                slot * 768 + hh * 384 + hi_]
                        m = masksb[:, var * 384 + qr:var * 384 + qr + hi_ - lo]
                        em("vector", lambda e, o=o, m=m: e.tensor_tensor(
                            o, o, m, ALU.mult).then_inc(vP.h, 1))
                        nm += 1
                mask_done[jt] = vP.bump(nm)
            if jt % 2 == 1:
                emit_proj_group()
            qi = jt - 3
            if 0 <= qi < NT:
                avpar = qi % 2
                if (ca, qi - 2) in av_evac:
                    W("tensor", aP, av_evac[(ca, qi - 2)])
                elif qi < 2 and ca >= 1:
                    W("tensor", aP, av_evac[(ca - 1, NT - 1)])
                W("tensor", vP, mask_done[qi + 2])

                def _av(t, qi=qi, avpar=avpar, ca=ca):
                    for hi in range(2):
                        habs = 2 * ca + hi
                        for c3 in range(3):
                            js = qi + c3
                            sl2 = js % NPT
                            o2 = qi * 128 - _qrange(js)[0]
                            ins = t.matmul(
                                av[:, avpar * 512 + hi * 65:
                                   avpar * 512 + hi * 65 + 65],
                                ptr[:, sl2 * 768 + hi * 384 + o2:
                                    sl2 * 768 + hi * 384 + o2 + 128],
                                vp[:, js * 1040 + habs * 65:
                                   js * 1040 + habs * 65 + 65],
                                start=(c3 == 0), stop=(c3 == 2))
                            if hi == 1 and c3 == 2:
                                ins.then_inc(tMM.h, 1)

                em("tensor", _av)
                avmm[(ca, qi)] = tMM.bump(1)
                W("vector", tMM, avmm[(ca, qi)])
                if (ca, qi - 2) in av_evac:
                    W("vector", aP, av_evac[(ca, qi - 2)])
                elif qi < 2 and ca >= 1:
                    W("vector", aP, av_evac[(ca - 1, NT - 1)])
                rzs = 2 * (qi % 2)

                def _rz(e, avpar=avpar, rzs=rzs):
                    src = av[:, avpar * 512:avpar * 512 + 130].rearrange(
                        "p (h o) -> p h o", h=2)[:, :, 64:65]
                    e.reciprocal(
                        rz[:, rzs:rzs + 2].rearrange("p (h o) -> p h o", h=2),
                        src).then_inc(vP.h, 1)

                em("vector", _rz)
                v_rz = vP.bump(1)
                W("scalar", vP, v_rz)
                for hi in range(2):
                    habs = 2 * ca + hi
                    o = atm[:, qi * 1024 + habs * 64:qi * 1024 + habs * 64 + 64]
                    i0 = av[:, avpar * 512 + hi * 65:avpar * 512 + hi * 65 + 64]
                    em("scalar", lambda e, o=o, i0=i0, hi=hi, rzs=rzs:
                       e.activation(o, i0, AF.Identity, bias=0.0,
                                    scale=rz[:, rzs + hi:rzs + hi + 1])
                       .then_inc(aP.h, 1))
                av_evac[(ca, qi)] = aP.bump(2)
        while pj[0] < len(proj_jobs):
            emit_proj_group()
        if c < EC:
            qk_evac_end[c] = aP.n
        attn_mm_end[ca] = tMM.n

    if stop == "B":
        return _finish(nc, prog, em, W, sems, dbg, stack,
                       dict(aP=aP, vP=vP, tMM=tMM, stO=stO, stS=stS))
    W("gpsimd", vP, vP.n)
    em("gpsimd", lambda g: g.dma_start(out=biasb[:], in_=dbp).then_inc(dC.h, 16))
    BP_RDY = dC.bump(16)
    W("gpsimd", tMM, attn_mm_end[EC - 2])
    W("gpsimd", aP, qk_evac_end[EC - 1])

    def _ldwp0(g):
        g.dma_start(out=wpts0[:].rearrange("p (e f) -> p e f", e=4),
                    in_=dwp.rearrange("(e p) f -> p e f", p=128)[:, 0:4, :]
                    ).then_inc(dw[0].h, 16)

    em("gpsimd", _ldwp0)
    W("gpsimd", tMM, qkproj_mm_end_m[0])

    def _ldwp1(g):
        g.dma_start(out=wpts1[:].rearrange("p (e f) -> p e f", e=4),
                    in_=dwp.rearrange("(e p) f -> p e f", p=128)[:, 4:8, :]
                    ).then_inc(dw[0].h, 16)

    em("gpsimd", _ldwp1)
    WPT_RDY = dw[0].bump(32)

    afm3 = wreg[:, 0:EC * 2048].rearrange("p (e t) -> p e t", e=EC)
    atr_done = {}
    for qi in range(NT):
        wait_a = av_evac[(EC - 1, NT - 1)] if qi < 2 else av_evac[(EC - 1, qi)]
        t_atr, atr_done[qi] = emit_transpose(
            atm[:, qi * 1024:(qi + 1) * 1024],
            afm3[:, :, qi * 128:(qi + 1) * 128], wait_a)
        if qi == 9:
            atr_t9 = t_atr

    xpre_rdy = {}
    W("sync", tMM, atr_t9)
    for s in range(2):
        W("sync", dQ, dQ.n)
        em("sync", lambda e, s=s: e.dma_start(
            out=xpre[:, s * 1024:(s + 1) * 1024],
            in_=dx_in[128 + s * 128:256 + s * 128, :]).then_inc(dQ.h, 16))
        xpre_rdy[s] = (dQ, dQ.bump(16))

    xn_c = {}
    x2_built = {}
    x2_wrote = {}
    x2_add = {}
    x2cons = {}
    t_wp_last = 0
    xn2_evac_last = 0
    cladder = {}
    slot_last_xn = {0: xn_group_last.get(2, 0), 1: xn_group_last.get(1, 0)}
    cgroups = [(0, LNG), (LNG, LNG), (2 * LNG, NT - 2 * LNG)]
    for gi in range(len(cgroups) + 1):
        if gi < len(cgroups):
            t0g, ng = cgroups[gi]
            for k in range(ng):
                ti = t0g + k
                sl = ti % 2
                for fh in range(2):
                    par = pp_acquire()
                    W("tensor", vP, atr_done[ti])
                    W("tensor", dw[0], WPT_RDY)

                    def _wp(t, ti=ti, fh=fh, par=par):
                        for ec in range(EC):
                            ins = t.matmul(
                                pp[:, par * 512:(par + 1) * 512],
                                afm3[:, ec, ti * 128:(ti + 1) * 128],
                                wptsv[ec // 4][:, (ec % 4) * 1024 + fh * 512:
                                           (ec % 4) * 1024 + (fh + 1) * 512],
                                start=(ec == 0), stop=(ec == EC - 1))
                            if ec == EC - 1:
                                ins.then_inc(tMM.h, 1)

                    em("tensor", _wp)
                    t_wp_last = tMM.bump(1)
                    W("vector", tMM, t_wp_last)
                    W("vector", dC, BP_RDY)
                    for s, v in x2cons.get(ti - 2, []):
                        W("vector", s, v)
                    if ti - 2 >= 0:
                        W("vector", stS, stS.n)
                    o = x2t[:, sl * 1024 + fh * 512:sl * 1024 + (fh + 1) * 512]
                    em("vector", lambda e, o=o, p=par,
                                 i1=biasb[:, fh * 512:(fh + 1) * 512]:
                       e.tensor_tensor(o, pp[:, p * 512:(p + 1) * 512], i1,
                                       ALU.add).then_inc(vP.h, 1))
                    pp_guard[par] = ("vP", vP.bump(1))
                x2_built[ti] = vP.n
                W("vector", vP, x2_built[ti])
                ps, pv = xpre_rdy[ti]
                W("vector", ps, pv)
                em("vector", lambda e, ti=ti, sl=sl: e.tensor_tensor(
                    x2t[:, sl * 1024:(sl + 1) * 1024],
                    x2t[:, sl * 1024:(sl + 1) * 1024],
                    xpre[:, (ti % 4) * 1024:((ti % 4) + 1) * 1024],
                    ALU.add).then_inc(vP.h, 1))
                x2_add[ti] = vP.bump(1)
                W("gpsimd", vP, x2_add[ti])
                W("gpsimd", stS, stS.n)
                em("gpsimd", lambda e, ti=ti, sl=sl: e.dma_start(
                    out=dx2s[ti * 128:(ti + 1) * 128, :],
                    in_=x2t[:, sl * 1024:(sl + 1) * 1024]).then_inc(stS.h, 16))
                x2_wrote[ti] = stS.bump(16)
                tj = ti + 2
                if tj <= NT - 1:
                    if tj - 4 >= 0:
                        W("gpsimd", vP, x2_add[tj - 4])
                    else:
                        W("gpsimd", tMM, atr_t9)
                    W("gpsimd", dP, dP.n)
                    em("gpsimd", lambda e, tj=tj: e.dma_start(
                        out=xpre[:, (tj % 4) * 1024:((tj % 4) + 1) * 1024],
                        in_=dx_in[128 + tj * 128:256 + tj * 128, :])
                       .then_inc(dP.h, 16))
                    xpre_rdy[tj] = (dP, dP.bump(16))
                W("scalar", vP, x2_add[ti])
                x2cons[ti] = emit_stats(
                    x2t[:, sl * 1024:(sl + 1) * 1024], gi % 2, k)

            cladder[gi] = emit_ladder(gi % 2, ng, war_a=slot_last_xn[gi % 2])
        if gi >= 1:
            t0g, ng = cgroups[gi - 1]
            W("scalar", vP, cladder[gi - 1])
            jj0 = t0g
            rl = {0: emit_xload(dx2s[jj0 * 128:(jj0 + 1) * 128, :],
                                extra=[(stS, stS.n)])}
            for k in range(ng):
                j = t0g + k
                if k + 1 < ng:
                    jj = j + 1
                    rl[k + 1] = emit_xload(dx2s[jj * 128:(jj + 1) * 128, :],
                                           extra=[(stS, stS.n)])
                ser, sl, dxv = rl[k]
                W("scalar", dx[sl], dxv)
                xn_c[j] = emit_xn(xr[:, sl * 1024:(sl + 1) * 1024], sl,
                                  (gi - 1) % 2, k, NKV + j)
                xcons[ser] = [(aP, xn_c[j])]
                t_tr, ev = emit_transpose(
                    xnscr[:, sl * 1024:(sl + 1) * 1024],
                    xnfm3[:, :, j * 128:(j + 1) * 128], xn_c[j])
                ln["ttr"][NKV + j] = t_tr
                xn2_evac_last = ev
                W("tensor", dC, CONSTS)
                W("tensor", vP, ev)
                if j == 0:
                    W("tensor", aP, last_st_exp)

                def _rt(t, j=j):
                    for ec in range(EC):
                        ins = t.matmul(stp[0][0:128, j * 8:(j + 1) * 8],
                                       xnfm3[:, ec, j * 128:(j + 1) * 128],
                                       rws[:, ec * 8:(ec + 1) * 8],
                                       start=(ec == 0), stop=(ec == EC - 1))
                        if ec == EC - 1:
                            ins.then_inc(tMM.h, 1)

                em("tensor", _rt)
                rt_mm_last = tMM.bump(1)
            slot_last_xn[(gi - 1) % 2] = xn_c[t0g + ng - 1]

    W("vector", tMM, rt_mm_last)
    em("vector", lambda e: e.tensor_copy(lgt[:], stp[0][0:128, 0:128])
       .then_inc(vP.h, 1))
    W("vector", vP, vP.bump(1))
    W("scalar", vP, vP.n)
    W("scalar", aP, aP.n)
    lg2 = sqscr
    em("scalar", lambda e: e.activation(lg2[:, 0:128], lgt[:], AF.Exp,
                                        bias=0.0, scale=1.0).then_inc(aP.h, 1))
    W("vector", aP, aP.bump(1))
    lg3 = lg2[:, 0:128].rearrange("p (a b) -> p a b", a=NT)
    lgm = lgt[:].rearrange("p (a b) -> p a b", a=NT)

    def _ve2(f):
        em("vector", lambda e, f=f: f(e).then_inc(vP.h, 1))
        W("vector", vP, vP.bump(1))

    e1v = rsc[:, 0:16]
    e2v = rsc[:, 16:32]
    s12 = rsc[:, 32:48]
    e1b = e1v.rearrange("p (a b) -> p a b", b=1).broadcast_to([128, NT, 8])
    _ve2(lambda e: e.reduce_max(e1v, lg3, axis=AX.X))
    _ve2(lambda e: e.tensor_tensor(lgm, lg3, e1b, ALU.is_equal))
    _ve2(lambda e: e.scalar_tensor_tensor(lgm, lgm, -1e30, lg3, ALU.mult, ALU.add))
    _ve2(lambda e: e.reduce_max(e2v, lgm, axis=AX.X))
    _ve2(lambda e: e.tensor_tensor(s12, e1v, e2v, ALU.add))
    _ve2(lambda e: e.reciprocal(s12, s12))
    _ve2(lambda e: e.tensor_tensor(e1v, e1v, s12, ALU.mult))
    _ve2(lambda e: e.tensor_tensor(e2v, e2v, s12, ALU.mult))
    _ve2(lambda e: e.tensor_tensor(scals[:], e1v, e2v, ALU.add))

    if stop == "C":
        return _finish(nc, prog, em, W, sems, dbg, stack,
                       dict(aP=aP, vP=vP, tMM=tMM, stO=stO, stS=stS))
    ffn_regions = [pp[:, 0:512], pp[:, 512:1024],
                   stp[1][:, 0:512], stp[1][:, 512:1024]]
    ffn_guard = [("vP", 0), ("vP", 0), ("aP", last_st_exp), ("aP", last_st_exp)]
    ffn_ctr = [0]

    def ffn_acquire():
        q = ffn_ctr[0] % 4
        ffn_ctr[0] += 1
        gs, gv = ffn_guard[q]
        W("tensor", sems[gs], gv)
        return q


    W("gpsimd", vP, x2_built[NT - 1])
    em("gpsimd", lambda g: g.dma_start(out=biasb[:], in_=db2).then_inc(dC.h, 16))
    B2_RDY = dC.bump(16)
    g3 = gbuf[:].rearrange("p (f t) -> p f t", f=FC)
    gelu_mm = {}
    gelu_done = {}
    w1_rdy = {}
    w2_rdy = {}
    ffn2_mm = {}
    out_done = {}
    eserial = [0]

    def _emit_w2_load(th, hq, wslot):
        W("gpsimd", dw[wslot], dw[wslot].n)

        def _ldw2(g, hq=hq, wslot=wslot):
            for pc in range(4):
                g.dma_start(
                    out=wreg[:, wslot * FC * 256 + pc * 8 * 256:
                             wslot * FC * 256 + (pc + 1) * 8 * 256]
                        .rearrange("p (f h) -> p f h", f=8),
                    in_=dw2.rearrange("(f p) h -> p f h", p=128)[
                        :, pc * 8:(pc + 1) * 8, hq * 256:(hq + 1) * 256],
                ).then_inc(dw[wslot].h, 16)

        em("gpsimd", _ldw2)
        w2_rdy[(th, hq)] = dw[wslot].bump(64)

    for th in range(2):
        for fc in range(FC):
            wslot = fc % 2
            if fc - 2 >= 0:
                W("gpsimd", tMM, gelu_mm[(th, fc - 2)])
            elif th == 1:
                W("gpsimd", tMM, gelu_mm[(0, FC - 2 + fc)])
            elif fc == 0:
                W("gpsimd", dw[0], dw[0].n)
                W("gpsimd", dw[1], dw[1].n)

            W("gpsimd", dw[wslot], dw[wslot].n)

            def _ldw1(g, fc=fc, wslot=wslot):
                g.dma_start(
                    out=w1slc[:, wslot * EC * 128:(wslot + 1) * EC * 128]
                        .rearrange("p (e f) -> p e f", e=EC),
                    in_=dw1.rearrange("(e p) f -> p e f", p=128)[
                        :, :, fc * 128:(fc + 1) * 128],
                ).then_inc(dw[wslot].h, 16)

            em("gpsimd", _ldw1)
            w1_rdy[(th, fc)] = dw[wslot].bump(16)
            if fc in (8, 20):
                hq0 = 0 if fc == 8 else 1
                if th == 0:
                    W("gpsimd", tMM, t_wp_last)
                else:
                    W("gpsimd", tMM, ffn2_mm[(0, hq0 + 2, 7)])
                _emit_w2_load(th, hq0, hq0 % 2)
            for s in range(2):
                par = ffn_acquire()
                W("tensor", dw[wslot], w1_rdy[(th, fc)])
                if th == 0 and fc == 0 and s == 0:
                    W("tensor", vP, xn2_evac_last)

                def _f1(t, th=th, fc=fc, s=s, par=par, wslot=wslot):
                    for ec in range(EC):
                        ins = t.matmul(
                            ffn_regions[par],
                            w1slc[:, wslot * EC * 128 + ec * 128:
                                  wslot * EC * 128 + (ec + 1) * 128],
                            xnfm3[:, ec, th * 1024 + s * 512:
                                  th * 1024 + (s + 1) * 512],
                            start=(ec == 0), stop=(ec == EC - 1))
                        if ec == EC - 1:
                            ins.then_inc(tMM.h, 1)

                em("tensor", _f1)
                t_f1 = tMM.bump(1)
                if s == 1:
                    gelu_mm[(th, fc)] = t_f1
                W("scalar", tMM, t_f1)
                if th == 0 and fc == 0 and s == 0:
                    W("scalar", dK, CONSTS2)
                    W("scalar", tMM, attn_mm_end[EC - 1])
                    W("scalar", vP, atr_done[NT - 1])
                    W("scalar", tMM, t_wp_last)
                if th == 1 and fc == 0 and s == 0:
                    W("scalar", tMM, ffn2_mm[(0, 3, 7)])
                o = g3[:, fc, s * 512:(s + 1) * 512]
                em("scalar", lambda e, o=o, p=par, b=b1s[:, fc:fc + 1]:
                   e.activation(o, ffn_regions[p],
                                AF.Gelu_apprx_tanh, bias=b,
                                scale=1.0).then_inc(aP.h, 1))
                ffn_guard[par] = ("aP", aP.bump(1))
        gelu_done[th] = aP.n

        for hq in range(4):
            wslot = hq % 2
            if hq - 2 >= 0:
                W("gpsimd", tMM, ffn2_mm[(th, hq - 2, 7)])
                _emit_w2_load(th, hq, wslot)
            for ti in range(8):
                absti = th * 8 + ti
                par = ffn_acquire()
                W("tensor", dw[wslot], w2_rdy[(th, hq)])
                W("tensor", aP, gelu_done[th])

                def _f2(t, ti=ti, par=par, wslot=wslot):
                    for fc in range(FC):
                        ins = t.matmul(
                            ffn_regions[par][:, 0:256],
                            g3[:, fc, ti * 128:(ti + 1) * 128],
                            wreg[:, wslot * FC * 256 + fc * 256:
                                 wslot * FC * 256 + (fc + 1) * 256],
                            start=(fc == 0), stop=(fc == FC - 1))
                        if fc == FC - 1:
                            ins.then_inc(tMM.h, 1)

                em("tensor", _f2)
                ffn2_mm[(th, hq, ti)] = tMM.bump(1)
                ser = eserial[0]
                eserial[0] += 1
                xsl = ser % 2
                if ser - 2 >= 0:
                    W("sync", vP, out_done[ser - 2])
                W("sync", stS, stS.n)
                W("sync", dx[xsl], dx[xsl].n)
                em("sync", lambda e, absti=absti, hq=hq, xsl=xsl: e.dma_start(
                    out=x2r[:, xsl * 256:(xsl + 1) * 256],
                    in_=dx2s[absti * 128:(absti + 1) * 128,
                             hq * 256:(hq + 1) * 256]).then_inc(dx[xsl].h, 16))
                dxv = dx[xsl].bump(16)
                W("vector", tMM, ffn2_mm[(th, hq, ti)])
                W("vector", dx[xsl], dxv)
                W("vector", dC, B2_RDY)
                osl = ser % 2
                em("vector", lambda e, osl=osl, p=par, hq=hq: e.tensor_tensor(
                    tmpr[:, osl * 256:(osl + 1) * 256],
                    ffn_regions[p][:, 0:256],
                    biasb[:, hq * 256:(hq + 1) * 256],
                    ALU.add).then_inc(vP.h, 1))
                ffn_guard[par] = ("vP", vP.bump(1))
                W("vector", vP, ffn_guard[par][1])
                if ser >= 1:
                    W("vector", stO, 16 * (ser - 1))
                em("vector", lambda e, osl=osl, xsl=xsl, absti=absti:
                   e.scalar_tensor_tensor(
                       outr[:, osl * 256:(osl + 1) * 256],
                       tmpr[:, osl * 256:(osl + 1) * 256],
                       scals[:, absti:absti + 1],
                       x2r[:, xsl * 256:(xsl + 1) * 256],
                       ALU.mult, ALU.add).then_inc(vP.h, 1))
                out_done[ser] = vP.bump(1)
                W("gpsimd", vP, out_done[ser])
                W("gpsimd", stO, stO.n)
                em("gpsimd", lambda e, osl=osl, absti=absti, hq=hq: e.dma_start(
                    out=dy[absti * 128:(absti + 1) * 128, hq * 256:(hq + 1) * 256],
                    in_=outr[:, osl * 256:(osl + 1) * 256]).then_inc(stO.h, 16))
                stO.bump(16)

    W("gpsimd", stO, stO.n)
    W("sync", stO, stO.n)
    if dbg:
        return _finish(nc, prog, em, W, sems, dbg, stack,
                       dict(aP=aP, vP=vP, tMM=tMM, stO=stO, stS=stS))

    return _replay(nc, prog, stack)


def _replay(nc, prog, stack):
    with nc.Block() as blk:
        @blk.sync
        def _(e):
            for f in prog["sync"]:
                f(e)

        @blk.gpsimd
        def _(e):
            for f in prog["gpsimd"]:
                f(e)

        @blk.scalar
        def _(e):
            for f in prog["scalar"]:
                f(e)

        @blk.vector
        def _(e):
            for f in prog["vector"]:
                f(e)

        @blk.tensor
        def _(e):
            for f in prog["tensor"]:
                f(e)
    stack.close()
    return nc


def _finish(nc, prog, em, W, sems, dbg, stack, s):
    aP, vP, tMM, stO = s["aP"], s["vP"], s["tMM"], s["stO"]
    W("sync", aP, aP.n)
    W("sync", vP, vP.n)
    W("sync", tMM, tMM.n)
    ndma = 0
    for name, hh in dbg:
        shp = [hh.shape[0], hh.shape[1]]
        d = nc.dram_tensor("dbg_" + name, shp, hh.dtype, kind="ExternalOutput").ap()
        em("sync", lambda e, d=d, hh=hh: e.dma_start(out=d[:, :], in_=hh[:])
           .then_inc(stO.h, 16))
        ndma += 1
        stO.bump(16)
    W("sync", stO, stO.n)
    return _replay(nc, prog, stack)


def _host_inputs(inputs):
    x = np.asarray(inputs["L0_attn_x"], dtype=np.float32)
    wqT = np.ascontiguousarray(np.asarray(inputs["L0_attn_Wq"], np.float32).T)
    wkT = np.ascontiguousarray(np.asarray(inputs["L0_attn_Wk"], np.float32).T)
    wvT = np.ascontiguousarray(np.asarray(inputs["L0_attn_Wv"], np.float32).T)
    wpT = np.ascontiguousarray(np.asarray(inputs["L0_attn_Wp"], np.float32).T)
    w1 = np.ascontiguousarray(np.asarray(inputs["L0_ffn_w1"], np.float32))
    w2 = np.ascontiguousarray(np.asarray(inputs["L0_ffn_w2"], np.float32))
    rw = np.ascontiguousarray(np.asarray(inputs["L0_ffn_router_w"], np.float32))
    bq = np.asarray(inputs["L0_attn_bq"], np.float32)
    bk = np.asarray(inputs["L0_attn_bk"], np.float32)
    bv = np.asarray(inputs["L0_attn_bv"], np.float32)
    bp = np.asarray(inputs["L0_attn_bp"], np.float32)
    b1 = np.asarray(inputs["L0_ffn_b1"], np.float32)
    b2 = np.asarray(inputs["L0_ffn_b2"], np.float32)

    bqc = np.ascontiguousarray(bq.reshape(EC, 128).T)
    bkc = np.ascontiguousarray(bk.reshape(EC, 128).T)
    b1c = np.ascontiguousarray(b1.reshape(FC, 128).T)
    bvb = np.ascontiguousarray(np.broadcast_to(bv, (128, E)))
    bpb = np.ascontiguousarray(np.broadcast_to(bp, (128, E)))
    b2b = np.ascontiguousarray(np.broadcast_to(b2, (128, E)))
    ident = np.eye(128, dtype=ml_dtypes.bfloat16)

    jr = np.arange(128)[:, None]
    qr = np.arange(384)[None, :]
    interior = ((qr >= jr) & (qr <= jr + 256)).astype(np.float32)

    in_maps = []
    for c in range(8):
        b, half = c // 2, c % 2
        start = half * TOWN
        lo, hi = start - WIN, start + TOWN + WIN
        xin = np.zeros((TKV, E), np.float32)
        vlo, vhi = max(lo, 0), min(hi, N)
        xin[vlo - lo:vhi - lo] = x[b, vlo:vhi]
        v0 = interior if half == 1 else np.zeros_like(interior)
        v2 = interior if half == 0 else np.zeros_like(interior)
        masks = np.stack([v0, interior, v2]).astype(ml_dtypes.bfloat16)
        in_maps.append({
            "x": xin, "wqT": wqT, "wkT": wkT, "wvT": wvT, "wpT": wpT,
            "w1": w1, "w2": w2, "rw": rw,
            "bqc": bqc, "bkc": bkc, "b1c": b1c,
            "bvb": bvb, "bpb": bpb, "b2b": b2b,
            "masks": masks, "ident": ident,
        })
    return in_maps


def kernel(**inputs):
    in_maps = _host_inputs(inputs)
    nc = build_nc()
    res = run_bass_kernel_spmd(nc, in_maps, core_ids=list(range(8)))
    y = np.empty((B, N, E), np.float32)
    for c in range(8):
        b, half = c // 2, c % 2
        y[b, half * TOWN:(half + 1) * TOWN] = res.results[c]["y"]
    return y


# version 32
# speedup vs baseline: 1.2086x; 1.0691x over previous
import contextlib

import numpy as np
import ml_dtypes
import concourse.bass as bass
import concourse.mybir as mybir
from concourse.bass_utils import run_bass_kernel_spmd

F32 = mybir.dt.float32
BF16 = mybir.dt.bfloat16
AF = mybir.ActivationFunctionType
ALU = mybir.AluOpType
AX = mybir.AxisListType

B, N, E, NH, HD, FF = 4, 4096, 1024, 16, 64, 4096
WIN = 128
TOWN = 2048
TKV = 2304
NT = TOWN // 128
NKV = TKV // 128
EC = E // 128
FC = FF // 128
EPS = 1e-5
SCL = 1.0 / 32.0
NPT = 4


def _qrange(jt):
    return max(0, jt * 128 - 256), min(TOWN, jt * 128 + 128)


def _mask_blocks(jt):
    q0, q1 = _qrange(jt)
    off = q0 - (jt * 128 - 256)
    if jt == 0:
        return [(0, q1 - q0, 0, off)]
    if jt == NKV - 1:
        return [(0, q1 - q0, 2, off)]
    out = []
    for qr_lo, qr_hi in ((0, 128), (256, 384)):
        lo = max(q0, jt * 128 - 256 + qr_lo)
        hi = min(q1, jt * 128 - 256 + qr_hi)
        if lo < hi:
            out.append((lo - q0, hi - q0, 1, lo - q0 + off))
    return out


class SemC:
    def __init__(self, h):
        self.h = h
        self.n = 0

    def bump(self, k):
        self.n += k
        return self.n


def build_nc(stop="full", dbg=()):
    nc = bass.Bass("TRN2", target_bir_lowering=False)

    dx_in = nc.dram_tensor("x", [TKV, E], F32, kind="ExternalInput").ap()
    dwq = nc.dram_tensor("wqT", [E, E], F32, kind="ExternalInput").ap()
    dwk = nc.dram_tensor("wkT", [E, E], F32, kind="ExternalInput").ap()
    dwv = nc.dram_tensor("wvT", [E, E], F32, kind="ExternalInput").ap()
    dwp = nc.dram_tensor("wpT", [E, E], F32, kind="ExternalInput").ap()
    dw1 = nc.dram_tensor("w1", [E, FF], F32, kind="ExternalInput").ap()
    dw2 = nc.dram_tensor("w2", [FF, E], F32, kind="ExternalInput").ap()
    drw = nc.dram_tensor("rw", [E, 8], F32, kind="ExternalInput").ap()
    dbq = nc.dram_tensor("bqc", [128, EC], F32, kind="ExternalInput").ap()
    dbk = nc.dram_tensor("bkc", [128, EC], F32, kind="ExternalInput").ap()
    db1 = nc.dram_tensor("b1c", [128, FC], F32, kind="ExternalInput").ap()
    dbv = nc.dram_tensor("bvb", [128, E], F32, kind="ExternalInput").ap()
    dbp = nc.dram_tensor("bpb", [128, E], F32, kind="ExternalInput").ap()
    db2 = nc.dram_tensor("b2b", [128, E], F32, kind="ExternalInput").ap()
    dmask = nc.dram_tensor("masks", [3, 128, 384], BF16, kind="ExternalInput").ap()
    dident = nc.dram_tensor("ident", [128, 128], BF16, kind="ExternalInput").ap()
    dy = nc.dram_tensor("y", [TOWN, E], F32, kind="ExternalOutput").ap()
    dx2s = nc.dram_tensor("x2s", [TOWN, E], F32, kind="Internal").ap()

    A = nc.alloc_sbuf_tensor
    xr = A("xr", [128, 2 * 1024], F32)
    x2t = A("x2t", [128, 2 * 1024], F32)
    x2r = A("x2r", [128, 2 * 256], F32)
    outr = A("outr", [128, 2 * 256], F32)
    tmpr = A("tmpr", [128, 2 * 256], F32)
    xnscr = A("xnscr", [128, 2 * 1024], BF16)
    sqscr = A("sqscr", [128, 1024], F32)
    xnfm = A("xnfm", [128, EC * TKV], BF16)
    lns = A("lns", [128, 8 * 12], F32)
    epsc = A("epsc", [128, 1], F32)
    biasb = A("biasb", [128, 1024], BF16)
    bqs = A("bqs", [128, EC], F32)
    bks = A("bks", [128, EC], F32)
    b1s = A("b1s", [128, FC], F32)
    rws = A("rws", [128, EC * 8], BF16)
    masksb = A("masksb", [128, 3 * 384], BF16)
    identb = A("identb", [128, 128], BF16)
    ptr = A("ptr", [128, NPT * 768], BF16)
    rz = A("rz", [128, 4], F32)
    scals = A("scals", [128, NT], F32)
    lgt = A("lgt", [128, 128], F32)
    rsc = A("rsc", [128, 64], F32)
    wreg = A("wreg", [128, 16 * 1024], BF16)
    w1slc = A("w1slc", [128, 2 * EC * 128], BF16)

    vbytes = NKV * 1040 * 2
    qbytes = 2 * TOWN * 2
    kbytes = 2 * TKV * 2
    wqkbytes = 4 * 1024 * 2
    atmbytes = NT * 1024 * 2
    arena_b = max(vbytes + qbytes + kbytes + wqkbytes + atmbytes,
                  FC * 1024 * 2)
    arena = A("arena", [128, arena_b], mybir.dt.uint8)
    a_addr = nc.lookup_mloc(arena).addr
    AT = nc.alloc_sbuf_tensor_at
    off = a_addr
    vp = AT("vp", [128, NKV * 1040], BF16, offset=off); off += vbytes
    slot0_off = off
    qb0 = AT("qb0", [128, TOWN], BF16, offset=off); off += TOWN * 2
    kb0 = AT("kb0", [128, TKV], BF16, offset=off); off += TKV * 2
    qb1 = AT("qb1", [128, TOWN], BF16, offset=off); off += TOWN * 2
    kb1 = AT("kb1", [128, TKV], BF16, offset=off); off += TKV * 2
    wqks_off = off
    wqks = AT("wqks", [128, 4 * 1024], BF16, offset=off); off += wqkbytes
    atm = AT("atm", [128, NT * 1024], BF16, offset=off)
    gbuf = AT("gbuf", [128, FC * 1024], BF16, offset=a_addr)
    xpre = AT("xpre", [128, 4 * 1024], F32, offset=a_addr + FC * 1024 * 2)
    w2s2 = AT("w2s2", [128, FC * 256], BF16, offset=a_addr + FC * 1024 * 2)
    qbs, kbs = [qb0, qb1], [kb0, kb1]
    wpts0 = AT("wpts0", [128, 4 * 1024], BF16, offset=slot0_off)
    wpts1 = AT("wpts1", [128, 4 * 1024], BF16, offset=wqks_off)
    wptsv = [wpts0, wpts1]
    w_addr = nc.lookup_mloc(wreg).addr
    wvslc = AT("wvslc", [128, 2 * EC * 512], BF16, offset=w_addr)

    stp = [nc.alloc_psum_tensor("st0", [128, 1024], F32),
           nc.alloc_psum_tensor("st1", [128, 1024], F32)]
    pp = nc.alloc_psum_tensor("pp", [128, 1024], F32)
    av = nc.alloc_psum_tensor("av", [128, 1024], F32)

    nc._dbg_handles = dict(xnfm=xnfm, vp=vp, qb=qb0, kb=kb0, atm=atm,
                           gbuf=gbuf, wpts=wpts0, wreg=wreg, ptr=ptr,
                           scals=scals, x2t=x2t, xnscr=xnscr,
                           biasb=biasb, wqks=wqks, masksb=masksb, rsc=rsc,
                           lgt=lgt, wvslc=wvslc, w1slc=w1slc, rz=rz)
    if dbg:
        dbg = [(n, nc._dbg_handles[n]) for n in dbg]
    prog = {k: [] for k in ("sync", "gpsimd", "scalar", "vector", "tensor")}

    def em(eng, f):
        prog[eng].append(f)

    stack = contextlib.ExitStack()
    sems = {}
    for name in ("dxA", "dxB", "dwA", "dwB", "dC", "dK", "stS", "stO",
                 "dP", "dQ", "tMM", "aP", "vP"):
        sems[name] = SemC(stack.enter_context(nc.semaphore(name)))
    dx = [sems["dxA"], sems["dxB"]]
    dw = [sems["dwA"], sems["dwB"]]
    dC, stS, stO = sems["dC"], sems["stS"], sems["stO"]
    dK = sems["dK"]
    dP, dQ = sems["dP"], sems["dQ"]
    tMM, aP, vP = sems["tMM"], sems["aP"], sems["vP"]

    def W(eng, s, v):
        if v > 0:
            em(eng, lambda e, _h=s.h, _v=v: e.wait_ge(_h, _v))

    xnfm3 = xnfm[:].rearrange("p (c t) -> p c t", c=EC)

    def _consts_g(g):
        g.dma_start(out=masksb[:].rearrange("p (v q) -> p v q", v=3),
                    in_=dmask.rearrange("v p q -> p v q")).then_inc(dC.h, 16)
        g.dma_start(out=identb[:], in_=dident).then_inc(dC.h, 16)
        g.dma_start(out=biasb[:], in_=dbv).then_inc(dC.h, 16)
        g.dma_start(out=rws[:].rearrange("p (c x) -> p c x", c=EC),
                    in_=drw.rearrange("(c p) x -> p c x", p=128)).then_inc(dC.h, 16)
        for fh in range(2):
            g.dma_start(
                out=wvslc[:, fh * EC * 512:(fh + 1) * EC * 512]
                    .rearrange("p (c f) -> p c f", c=EC),
                in_=dwv.rearrange("(c p) f -> p c f", p=128)[:, :, fh * 512:(fh + 1) * 512],
            ).then_inc(dC.h, 16)
        for d_ap, col in ((dwq, 0), (dwk, 2 * 1024)):
            g.dma_start(
                out=wqks[:, col:col + 1024].rearrange("p (c f) -> p c f", c=EC),
                in_=d_ap.rearrange("(c p) f -> p c f", p=128)[:, :, 0:128],
            ).then_inc(dC.h, 16)

    em("gpsimd", _consts_g)
    CONSTS = dC.bump(16 * 8)

    def _consts_s(s):
        s.dma_start(out=bqs[:], in_=dbq).then_inc(dK.h, 16)
        s.dma_start(out=bks[:], in_=dbk).then_inc(dK.h, 16)
        s.dma_start(out=b1s[:], in_=db1).then_inc(dK.h, 16)

    em("sync", _consts_s)
    CONSTS2 = dK.bump(16 * 3)


    def _vones(v):
        ap = vp[:].rearrange("p (t h o) -> p t h o", t=NKV, h=NH)[:, :, :, 64:65]
        v.memset(ap, 1.0)
        v.memset(epsc[:], EPS)

    em("vector", _vones)

    LNG = 6
    lnstate = {"rdy": 0}

    def _lncol(f, slot, k, n=1):
        base = f * 12 + slot * LNG + k
        return lns[:, base:base + n]

    def emit_stats(src_ap, slot, k):
        if lnstate["rdy"]:
            W("scalar", aP, lnstate["rdy"])
        em("scalar", lambda e, a=src_ap, o=_lncol(1, slot, k): e.activation(
            sqscr[:], a, AF.Square, accum_out=o).then_inc(aP.h, 1))
        a_sq = aP.bump(1)
        W("scalar", aP, a_sq)
        em("scalar", lambda e, a=src_ap, o=_lncol(0, slot, k): e.activation(
            sqscr[:], a, AF.Identity, accum_out=o).then_inc(aP.h, 1))
        a_st = aP.bump(1)
        lnstate["rdy"] = a_st
        return [(aP, a_st)]

    def emit_ladder(slot, n, war_a=0):
        R = _lncol(0, slot, 0, n)
        S = _lncol(1, slot, 0, n)
        MT = _lncol(2, slot, 0, n)
        MR = _lncol(3, slot, 0, n)
        V_ = _lncol(4, slot, 0, n)
        L_ = _lncol(5, slot, 0, n)
        RS = _lncol(6, slot, 0, n)
        NB = _lncol(7, slot, 0, n)
        W("vector", aP, lnstate["rdy"])
        if war_a:
            W("vector", aP, war_a)
        em("vector", lambda e: e.tensor_tensor(MT, R, R, ALU.mult).then_inc(vP.h, 1))
        W("vector", vP, vP.bump(1))
        em("vector", lambda e: e.tensor_scalar(
            MR, MT, 1.0 / (E * E), -EPS, ALU.mult, ALU.add).then_inc(vP.h, 1))
        W("vector", vP, vP.bump(1))
        em("vector", lambda e: e.scalar_tensor_tensor(
            V_, S, 1.0 / E, MR, ALU.mult, ALU.subtract).then_inc(vP.h, 1))
        e5 = vP.bump(1)
        W("scalar", vP, e5)
        em("scalar", lambda e: e.activation(
            L_, V_, AF.Ln, bias=epsc[:], scale=1.0).then_inc(aP.h, 1))
        a_ln = aP.bump(1)
        W("scalar", aP, a_ln)
        em("scalar", lambda e: e.activation(
            RS, L_, AF.Exp, bias=0.0, scale=-0.5).then_inc(aP.h, 1))
        a_rstd = aP.bump(1)
        W("vector", aP, a_rstd)
        em("vector", lambda e: e.tensor_tensor(NB, R, RS, ALU.mult).then_inc(vP.h, 1))
        W("vector", vP, vP.bump(1))
        em("vector", lambda e: e.tensor_single_scalar(
            NB, NB, -1.0 / E, ALU.mult).then_inc(vP.h, 1))
        return vP.bump(1)

    ln = {"ttr": {}}

    def emit_xn(src_ap, dst_slot, slot, k, tile_id):
        if tile_id - 2 in ln["ttr"]:
            W("scalar", tMM, ln["ttr"][tile_id - 2])
        dsta = xnscr[:, dst_slot * 1024:(dst_slot + 1) * 1024]
        em("scalar", lambda e, a=src_ap, o=dsta, r=_lncol(6, slot, k),
                     b=_lncol(7, slot, k): e.activation(
            o, a, AF.Identity, bias=b, scale=r).then_inc(aP.h, 1))
        return aP.bump(1)

    tr = {"par": 0, "lastv": [0, 0], "first": True}

    def emit_transpose(src_ap, dst3_ap, wait_a, extra_t_wait=0):
        par = tr["par"]
        tr["par"] ^= 1
        avb = av[:, par * 512:(par + 1) * 512].bitcast(BF16)
        if tr["first"]:
            W("tensor", dC, CONSTS)
            tr["first"] = False
        W("tensor", aP, wait_a)
        if extra_t_wait:
            W("tensor", aP, extra_t_wait)
        W("tensor", vP, tr["lastv"][par])

        def _tr(t, s=src_ap, o=avb):
            for c in range(8):
                ins = t.transpose(o[:, c * 128:(c + 1) * 128],
                                  s[:, c * 128:(c + 1) * 128], identb[:])
                if c == 7:
                    ins.then_inc(tMM.h, 1)

        em("tensor", _tr)
        t_tr = tMM.bump(1)
        W("vector", tMM, t_tr)
        em("vector", lambda e, i0=avb.rearrange("p (c t) -> p c t", c=8),
                     o=dst3_ap: e.tensor_copy(o, i0).then_inc(vP.h, 1))
        tr["lastv"][par] = vP.bump(1)
        return t_tr, tr["lastv"][par]

    ppctr = [0]
    pp_guard = [("vP", 0), ("vP", 0)]

    def pp_acquire():
        par = ppctr[0] % 2
        ppctr[0] += 1
        gs, gv = pp_guard[par]
        W("tensor", sems[gs], gv)
        return par

    xload = [0]
    xcons = {}

    def emit_xload(src_ap2, extra=()):
        i = xload[0]
        xload[0] += 1
        sl = i % 2
        for s, v in xcons.get(i - 2, []):
            W("sync", s, v)
        for s, v in extra:
            W("sync", s, v)
        W("sync", dx[sl], dx[sl].n)
        dst = xr[:, sl * 1024:(sl + 1) * 1024]
        em("sync", lambda e, s=src_ap2, d=dst: e.dma_start(out=d, in_=s)
           .then_inc(dx[sl].h, 16))
        return i, sl, dx[sl].bump(16)

    xn_a = {}
    tV_groups = []
    agroups = [(0, LNG), (LNG, LNG), (2 * LNG, NKV - 2 * LNG)]
    ladder_rdy = {}
    xn_group_last = {}
    for gi in range(len(agroups) + 1):
        if gi < len(agroups):
            t0g, ng = agroups[gi]
            for k in range(ng):
                i = t0g + k
                ser, sl, dxv = emit_xload(dx_in[i * 128:(i + 1) * 128, :])
                W("scalar", dx[sl], dxv)
                xcons[ser] = emit_stats(xr[:, sl * 1024:(sl + 1) * 1024], gi % 2, k)
            ladder_rdy[gi] = emit_ladder(gi % 2, ng,
                                         war_a=xn_group_last.get(gi - 2, 0))
        if gi >= 1:
            t0g, ng = agroups[gi - 1]
            W("scalar", vP, ladder_rdy[gi - 1])
            rl = {0: emit_xload(dx_in[t0g * 128:(t0g + 1) * 128, :])}
            for k in range(ng):
                i = t0g + k
                if k + 1 < ng:
                    rl[k + 1] = emit_xload(
                        dx_in[(i + 1) * 128:(i + 2) * 128, :])
                ser, sl, dxv = rl[k]
                W("scalar", dx[sl], dxv)
                xn_a[i] = emit_xn(xr[:, sl * 1024:(sl + 1) * 1024], sl,
                                  (gi - 1) % 2, k, i)
                xcons[ser] = [(aP, xn_a[i])]
                t_tr, vtr_j = emit_transpose(
                    xnscr[:, sl * 1024:(sl + 1) * 1024],
                    xnfm3[:, :, i * 128:(i + 1) * 128], xn_a[i])
                ln["ttr"][i] = t_tr
                W("tensor", dC, CONSTS)
                W("tensor", vP, vtr_j)
                for fh in range(2):
                    par = pp_acquire()

                    def _vproj(t, i=i, fh=fh, par=par):
                        for ec in range(EC):
                            ins = t.matmul(
                                pp[:, par * 512:(par + 1) * 512],
                                xnfm3[:, ec, i * 128:(i + 1) * 128],
                                wvslc[:, (fh * EC + ec) * 512:(fh * EC + ec + 1) * 512],
                                start=(ec == 0), stop=(ec == EC - 1))
                            if ec == EC - 1:
                                ins.then_inc(tMM.h, 1)

                    em("tensor", _vproj)
                    t_v = tMM.bump(1)
                    tV_groups.append(t_v)
                    W("vector", tMM, t_v)
                    W("vector", dC, CONSTS)
                    dst = vp[:].rearrange("p (t h o) -> p t h o", t=NKV, h=NH)[
                        :, i, fh * 8:(fh + 1) * 8, 0:64]
                    srcp = pp[:, par * 512:(par + 1) * 512].rearrange(
                        "p (h d) -> p h d", h=8)
                    bsl = biasb[:, fh * 512:(fh + 1) * 512].rearrange(
                        "p (h d) -> p h d", h=8)
                    em("vector", lambda e, o=dst, i0=srcp, i1=bsl: e.tensor_tensor(
                        o, i0, i1, ALU.add).then_inc(vP.h, 1))
                    pp_guard[par] = ("vP", vP.bump(1))
            xn_group_last[gi - 1] = xn_a[t0g + ng - 1]

    if stop == "A":
        return _finish(nc, prog, em, W, sems, dbg, stack,
                       dict(aP=aP, vP=vP, tMM=tMM, stO=stO, stS=stS))

    qk_w_ready = {0: CONSTS}
    qk_evac_end = {}
    attn_mm_end = {}
    av_evac = {}
    avmm = {}
    qkproj_mm_end = 0
    last_st_exp = 0

    def load_qk_slices(c, wslot):
        W("gpsimd", dw[wslot], dw[wslot].n)

        def _ld(g, c=c, wslot=wslot):
            for d_ap, col in ((dwq, wslot * 1024), (dwk, (2 + wslot) * 1024)):
                g.dma_start(
                    out=wqks[:, col:col + 1024].rearrange("p (e f) -> p e f", e=EC),
                    in_=d_ap.rearrange("(e p) f -> p e f", p=128)[:, :, c * 128:(c + 1) * 128],
                ).then_inc(dw[wslot].h, 16)
        em("gpsimd", _ld)
        return dw[wslot].bump(32)

    qkproj_mm_end_m = [0]

    for it in range(EC + 1):
        c, ca = it, it - 1
        if c < EC and c + 1 < EC:
            if c - 1 >= 0:
                W("gpsimd", tMM, qkproj_mm_end_m[0])
            qk_w_ready[c + 1] = load_qk_slices(c + 1, (c + 1) % 2)

        proj_jobs = []
        if c < EC:
            for kind in ("q", "k"):
                tlen = TOWN if kind == "q" else TKV
                for s in range((tlen + 511) // 512):
                    proj_jobs.append((kind, s * 512, min(512, tlen - s * 512)))
        pj = [0]
        first_evac = [True]

        def emit_proj_group(c=c, proj_jobs=proj_jobs, pj=pj, first_evac=first_evac):
            if pj[0] >= len(proj_jobs):
                return
            kind, lo, ncol = proj_jobs[pj[0]]
            pj[0] += 1
            par = pp_acquire()
            if pj[0] == 1:
                if c == 0:
                    W("tensor", dC, CONSTS)
                else:
                    W("tensor", dw[c % 2], qk_w_ready[c])

            def _proj(t, kind=kind, c=c, lo=lo, ncol=ncol, par=par):
                wcol = (0 if kind == "q" else 2 * 1024) + (c % 2) * 1024
                src_lo = (128 + lo) if kind == "q" else lo
                for ec in range(EC):
                    ins = t.matmul(
                        pp[:, par * 512:par * 512 + ncol],
                        wqks[:, wcol + ec * 128:wcol + (ec + 1) * 128],
                        xnfm3[:, ec, src_lo:src_lo + ncol],
                        start=(ec == 0), stop=(ec == EC - 1))
                    if ec == EC - 1:
                        ins.then_inc(tMM.h, 1)

            em("tensor", _proj)
            t_g = tMM.bump(1)
            qkproj_mm_end_m[0] = t_g
            if first_evac[0]:
                if c == 0:
                    W("scalar", dK, CONSTS2)
                if c - 2 >= 0:
                    W("scalar", tMM, attn_mm_end[c - 2])
                first_evac[0] = False
            W("scalar", tMM, t_g)
            dstb = qbs[c % 2] if kind == "q" else kbs[c % 2]
            bias = (bqs if kind == "q" else bks)[:, c:c + 1]
            dsta = dstb[:, lo:lo + ncol]
            em("scalar", lambda e, o=dsta, p=par, ncol=ncol, b=bias:
               e.activation(o, pp[:, p * 512:p * 512 + ncol], AF.Identity,
                            bias=b, scale=1.0).then_inc(aP.h, 1))
            pp_guard[par] = ("aP", aP.bump(1))

        if ca < 0:
            while pj[0] < len(proj_jobs):
                emit_proj_group()
            qk_evac_end[c] = aP.n
            continue

        prev_exp_end = last_st_exp
        st_exp = {}
        mask_done = {}
        qba, kba = qbs[ca % 2], kbs[ca % 2]
        first = True
        for jt in range(NKV + 1):
            if jt < NKV:
                q0, q1 = _qrange(jt)
                nq = q1 - q0
                stpar = jt % 2
                if first:
                    W("tensor", aP, qk_evac_end[ca])
                    first = False
                if jt - 2 in st_exp:
                    W("tensor", aP, st_exp[jt - 2])
                elif ca >= 1:
                    W("tensor", aP, prev_exp_end)

                def _st(t, jt=jt, q0=q0, q1=q1, nq=nq, stpar=stpar, kba=kba, qba=qba):
                    for hi in range(2):
                        ins = t.matmul(
                            stp[stpar][:, hi * 512:hi * 512 + nq],
                            kba[hi * 64:hi * 64 + 64, jt * 128:(jt + 1) * 128],
                            qba[hi * 64:hi * 64 + 64, q0:q1],
                            start=True, stop=True)
                        if hi == 1:
                            ins.then_inc(tMM.h, 1)

                em("tensor", _st)
                t_st = tMM.bump(1)
                slot = jt % NPT
                W("scalar", tMM, t_st)
                if jt - NPT >= 0 and (ca, jt - NPT) in avmm:
                    W("scalar", tMM, avmm[(ca, jt - NPT)])
                elif jt < NPT and ca >= 1:
                    W("scalar", tMM, attn_mm_end[ca - 1])
                srce = stp[stpar][:].rearrange("p (h q) -> p h q", h=2)[:, :, 0:nq]
                dpt = ptr[:, slot * 768:(slot + 1) * 768].rearrange(
                    "p (h q) -> p h q", h=2)[:, :, 0:nq]
                em("scalar", lambda e, o=dpt, i0=srce: e.activation(
                    o, i0, AF.Exp, bias=0.0, scale=SCL).then_inc(aP.h, 1))
                st_exp[jt] = aP.bump(1)
                last_st_exp = st_exp[jt]
                W("vector", aP, st_exp[jt])
                W("vector", dC, CONSTS)
                nm = 0
                for lo, hi_, var, qr in _mask_blocks(jt):
                    for hh in range(2):
                        o = ptr[:, slot * 768 + hh * 384 + lo:
                                slot * 768 + hh * 384 + hi_]
                        m = masksb[:, var * 384 + qr:var * 384 + qr + hi_ - lo]
                        em("vector", lambda e, o=o, m=m: e.tensor_tensor(
                            o, o, m, ALU.mult).then_inc(vP.h, 1))
                        nm += 1
                mask_done[jt] = vP.bump(nm)
            if jt % 2 == 1:
                emit_proj_group()
            qi = jt - 3
            if 0 <= qi < NT:
                avpar = qi % 2
                if (ca, qi - 2) in av_evac:
                    W("tensor", aP, av_evac[(ca, qi - 2)])
                elif qi < 2 and ca >= 1:
                    W("tensor", aP, av_evac[(ca - 1, NT - 1)])
                W("tensor", vP, mask_done[qi + 2])

                def _av(t, qi=qi, avpar=avpar, ca=ca):
                    for hi in range(2):
                        habs = 2 * ca + hi
                        for c3 in range(3):
                            js = qi + c3
                            sl2 = js % NPT
                            o2 = qi * 128 - _qrange(js)[0]
                            ins = t.matmul(
                                av[:, avpar * 512 + hi * 65:
                                   avpar * 512 + hi * 65 + 65],
                                ptr[:, sl2 * 768 + hi * 384 + o2:
                                    sl2 * 768 + hi * 384 + o2 + 128],
                                vp[:, js * 1040 + habs * 65:
                                   js * 1040 + habs * 65 + 65],
                                start=(c3 == 0), stop=(c3 == 2))
                            if hi == 1 and c3 == 2:
                                ins.then_inc(tMM.h, 1)

                em("tensor", _av)
                avmm[(ca, qi)] = tMM.bump(1)
                W("vector", tMM, avmm[(ca, qi)])
                if (ca, qi - 2) in av_evac:
                    W("vector", aP, av_evac[(ca, qi - 2)])
                elif qi < 2 and ca >= 1:
                    W("vector", aP, av_evac[(ca - 1, NT - 1)])
                rzs = 2 * (qi % 2)

                def _rz(e, avpar=avpar, rzs=rzs):
                    src = av[:, avpar * 512:avpar * 512 + 130].rearrange(
                        "p (h o) -> p h o", h=2)[:, :, 64:65]
                    e.reciprocal(
                        rz[:, rzs:rzs + 2].rearrange("p (h o) -> p h o", h=2),
                        src).then_inc(vP.h, 1)

                em("vector", _rz)
                v_rz = vP.bump(1)
                W("scalar", vP, v_rz)
                for hi in range(2):
                    habs = 2 * ca + hi
                    o = atm[:, qi * 1024 + habs * 64:qi * 1024 + habs * 64 + 64]
                    i0 = av[:, avpar * 512 + hi * 65:avpar * 512 + hi * 65 + 64]
                    em("scalar", lambda e, o=o, i0=i0, hi=hi, rzs=rzs:
                       e.activation(o, i0, AF.Identity, bias=0.0,
                                    scale=rz[:, rzs + hi:rzs + hi + 1])
                       .then_inc(aP.h, 1))
                av_evac[(ca, qi)] = aP.bump(2)
        while pj[0] < len(proj_jobs):
            emit_proj_group()
        if c < EC:
            qk_evac_end[c] = aP.n
        attn_mm_end[ca] = tMM.n

    if stop == "B":
        return _finish(nc, prog, em, W, sems, dbg, stack,
                       dict(aP=aP, vP=vP, tMM=tMM, stO=stO, stS=stS))
    W("gpsimd", vP, vP.n)
    em("gpsimd", lambda g: g.dma_start(out=biasb[:], in_=dbp).then_inc(dC.h, 16))
    BP_RDY = dC.bump(16)
    W("gpsimd", tMM, attn_mm_end[EC - 2])
    W("gpsimd", aP, qk_evac_end[EC - 1])

    def _ldwp0(g):
        g.dma_start(out=wpts0[:].rearrange("p (e f) -> p e f", e=4),
                    in_=dwp.rearrange("(e p) f -> p e f", p=128)[:, 0:4, :]
                    ).then_inc(dw[0].h, 16)

    em("gpsimd", _ldwp0)
    W("gpsimd", tMM, qkproj_mm_end_m[0])

    def _ldwp1(g):
        g.dma_start(out=wpts1[:].rearrange("p (e f) -> p e f", e=4),
                    in_=dwp.rearrange("(e p) f -> p e f", p=128)[:, 4:8, :]
                    ).then_inc(dw[0].h, 16)

    em("gpsimd", _ldwp1)
    WPT_RDY = dw[0].bump(32)

    afm3 = wreg[:, 0:EC * 2048].rearrange("p (e t) -> p e t", e=EC)
    atr_done = {}
    for qi in range(NT):
        wait_a = av_evac[(EC - 1, NT - 1)] if qi < 2 else av_evac[(EC - 1, qi)]
        t_atr, atr_done[qi] = emit_transpose(
            atm[:, qi * 1024:(qi + 1) * 1024],
            afm3[:, :, qi * 128:(qi + 1) * 128], wait_a)
        if qi == 9:
            atr_t9 = t_atr

    xpre_rdy = {}
    W("sync", tMM, atr_t9)
    for s in range(2):
        W("sync", dQ, dQ.n)
        em("sync", lambda e, s=s: e.dma_start(
            out=xpre[:, s * 1024:(s + 1) * 1024],
            in_=dx_in[128 + s * 128:256 + s * 128, :]).then_inc(dQ.h, 16))
        xpre_rdy[s] = (dQ, dQ.bump(16))

    xn_c = {}
    x2_built = {}
    x2_wrote = {}
    x2_add = {}
    x2cons = {}
    t_wp_last = 0
    xn2_evac_last = 0
    cladder = {}
    slot_last_xn = {0: xn_group_last.get(2, 0), 1: xn_group_last.get(1, 0)}
    cgroups = [(0, LNG), (LNG, LNG), (2 * LNG, NT - 2 * LNG)]
    for gi in range(len(cgroups) + 1):
        if gi < len(cgroups):
            t0g, ng = cgroups[gi]
            for k in range(ng):
                ti = t0g + k
                sl = ti % 2
                for fh in range(2):
                    par = pp_acquire()
                    W("tensor", vP, atr_done[ti])
                    W("tensor", dw[0], WPT_RDY)

                    def _wp(t, ti=ti, fh=fh, par=par):
                        for ec in range(EC):
                            ins = t.matmul(
                                pp[:, par * 512:(par + 1) * 512],
                                afm3[:, ec, ti * 128:(ti + 1) * 128],
                                wptsv[ec // 4][:, (ec % 4) * 1024 + fh * 512:
                                           (ec % 4) * 1024 + (fh + 1) * 512],
                                start=(ec == 0), stop=(ec == EC - 1))
                            if ec == EC - 1:
                                ins.then_inc(tMM.h, 1)

                    em("tensor", _wp)
                    t_wp_last = tMM.bump(1)
                    W("vector", tMM, t_wp_last)
                    W("vector", dC, BP_RDY)
                    for s, v in x2cons.get(ti - 2, []):
                        W("vector", s, v)
                    if ti - 2 >= 0:
                        W("vector", stS, stS.n)
                    o = x2t[:, sl * 1024 + fh * 512:sl * 1024 + (fh + 1) * 512]
                    em("vector", lambda e, o=o, p=par,
                                 i1=biasb[:, fh * 512:(fh + 1) * 512]:
                       e.tensor_tensor(o, pp[:, p * 512:(p + 1) * 512], i1,
                                       ALU.add).then_inc(vP.h, 1))
                    pp_guard[par] = ("vP", vP.bump(1))
                x2_built[ti] = vP.n
                W("vector", vP, x2_built[ti])
                ps, pv = xpre_rdy[ti]
                W("vector", ps, pv)
                em("vector", lambda e, ti=ti, sl=sl: e.tensor_tensor(
                    x2t[:, sl * 1024:(sl + 1) * 1024],
                    x2t[:, sl * 1024:(sl + 1) * 1024],
                    xpre[:, (ti % 4) * 1024:((ti % 4) + 1) * 1024],
                    ALU.add).then_inc(vP.h, 1))
                x2_add[ti] = vP.bump(1)
                W("gpsimd", vP, x2_add[ti])
                W("gpsimd", stS, stS.n)
                em("gpsimd", lambda e, ti=ti, sl=sl: e.dma_start(
                    out=dx2s[ti * 128:(ti + 1) * 128, :],
                    in_=x2t[:, sl * 1024:(sl + 1) * 1024]).then_inc(stS.h, 16))
                x2_wrote[ti] = stS.bump(16)
                tj = ti + 2
                if tj <= NT - 1:
                    if tj - 4 >= 0:
                        W("gpsimd", vP, x2_add[tj - 4])
                    else:
                        W("gpsimd", tMM, atr_t9)
                    W("gpsimd", dP, dP.n)
                    em("gpsimd", lambda e, tj=tj: e.dma_start(
                        out=xpre[:, (tj % 4) * 1024:((tj % 4) + 1) * 1024],
                        in_=dx_in[128 + tj * 128:256 + tj * 128, :])
                       .then_inc(dP.h, 16))
                    xpre_rdy[tj] = (dP, dP.bump(16))
                W("scalar", vP, x2_add[ti])
                x2cons[ti] = emit_stats(
                    x2t[:, sl * 1024:(sl + 1) * 1024], gi % 2, k)

            cladder[gi] = emit_ladder(gi % 2, ng, war_a=slot_last_xn[gi % 2])
        if gi >= 1:
            t0g, ng = cgroups[gi - 1]
            W("scalar", vP, cladder[gi - 1])
            jj0 = t0g
            rl = {0: emit_xload(dx2s[jj0 * 128:(jj0 + 1) * 128, :],
                                extra=[(stS, stS.n)])}
            for k in range(ng):
                j = t0g + k
                if k + 1 < ng:
                    jj = j + 1
                    rl[k + 1] = emit_xload(dx2s[jj * 128:(jj + 1) * 128, :],
                                           extra=[(stS, stS.n)])
                ser, sl, dxv = rl[k]
                W("scalar", dx[sl], dxv)
                xn_c[j] = emit_xn(xr[:, sl * 1024:(sl + 1) * 1024], sl,
                                  (gi - 1) % 2, k, NKV + j)
                xcons[ser] = [(aP, xn_c[j])]
                t_tr, ev = emit_transpose(
                    xnscr[:, sl * 1024:(sl + 1) * 1024],
                    xnfm3[:, :, j * 128:(j + 1) * 128], xn_c[j])
                ln["ttr"][NKV + j] = t_tr
                xn2_evac_last = ev
                W("tensor", dC, CONSTS)
                W("tensor", vP, ev)
                if j == 0:
                    W("tensor", aP, last_st_exp)

                def _rt(t, j=j):
                    for ec in range(EC):
                        ins = t.matmul(stp[0][0:128, j * 8:(j + 1) * 8],
                                       xnfm3[:, ec, j * 128:(j + 1) * 128],
                                       rws[:, ec * 8:(ec + 1) * 8],
                                       start=(ec == 0), stop=(ec == EC - 1))
                        if ec == EC - 1:
                            ins.then_inc(tMM.h, 1)

                em("tensor", _rt)
                rt_mm_last = tMM.bump(1)
            slot_last_xn[(gi - 1) % 2] = xn_c[t0g + ng - 1]

    W("vector", tMM, rt_mm_last)
    em("vector", lambda e: e.tensor_copy(lgt[:], stp[0][0:128, 0:128])
       .then_inc(vP.h, 1))
    W("vector", vP, vP.bump(1))
    W("scalar", vP, vP.n)
    W("scalar", aP, aP.n)
    lg2 = sqscr
    em("scalar", lambda e: e.activation(lg2[:, 0:128], lgt[:], AF.Exp,
                                        bias=0.0, scale=1.0).then_inc(aP.h, 1))
    W("vector", aP, aP.bump(1))
    lg3 = lg2[:, 0:128].rearrange("p (a b) -> p a b", a=NT)
    lgm = lgt[:].rearrange("p (a b) -> p a b", a=NT)

    def _ve2(f):
        em("vector", lambda e, f=f: f(e).then_inc(vP.h, 1))
        W("vector", vP, vP.bump(1))

    e1v = rsc[:, 0:16]
    e2v = rsc[:, 16:32]
    s12 = rsc[:, 32:48]
    e1b = e1v.rearrange("p (a b) -> p a b", b=1).broadcast_to([128, NT, 8])
    _ve2(lambda e: e.reduce_max(e1v, lg3, axis=AX.X))
    _ve2(lambda e: e.tensor_tensor(lgm, lg3, e1b, ALU.is_equal))
    _ve2(lambda e: e.scalar_tensor_tensor(lgm, lgm, -1e30, lg3, ALU.mult, ALU.add))
    _ve2(lambda e: e.reduce_max(e2v, lgm, axis=AX.X))
    _ve2(lambda e: e.tensor_tensor(s12, e1v, e2v, ALU.add))
    _ve2(lambda e: e.reciprocal(s12, s12))
    _ve2(lambda e: e.tensor_tensor(e1v, e1v, s12, ALU.mult))
    _ve2(lambda e: e.tensor_tensor(e2v, e2v, s12, ALU.mult))
    _ve2(lambda e: e.tensor_tensor(scals[:], e1v, e2v, ALU.add))

    if stop == "C":
        return _finish(nc, prog, em, W, sems, dbg, stack,
                       dict(aP=aP, vP=vP, tMM=tMM, stO=stO, stS=stS))
    ffn_regions = [pp[:, 0:512], pp[:, 512:1024],
                   stp[1][:, 0:512], stp[1][:, 512:1024]]
    ffn_guard = [("vP", 0), ("vP", 0), ("aP", last_st_exp), ("aP", last_st_exp)]
    ffn_ctr = [0]

    def ffn_acquire():
        q = ffn_ctr[0] % 4
        ffn_ctr[0] += 1
        gs, gv = ffn_guard[q]
        W("tensor", sems[gs], gv)
        return q


    w1_rdy = {}
    w2info = {}
    x2pre = {}
    dw2r = dw2.rearrange("(f p) h -> p f h", p=128)

    def emit_w1_load(th, fc, wslot):
        def _ldw1(g, fc=fc, wslot=wslot):
            g.dma_start(
                out=w1slc[:, wslot * EC * 128:(wslot + 1) * EC * 128]
                    .rearrange("p (e f) -> p e f", e=EC),
                in_=dw1.rearrange("(e p) f -> p e f", p=128)[
                    :, :, fc * 128:(fc + 1) * 128],
            ).then_inc(dw[wslot].h, 16)

        em("gpsimd", _ldw1)
        w1_rdy[(th, fc)] = dw[wslot].bump(16)

    def w2slot(k):
        if k % 3 < 2:
            return wreg, (k % 3) * FC * 256
        return w2s2, 0

    def emit_w2_piece_g(k, pc, tmm_wait=0, vp_wait=0):
        hq = k % 4
        h, off0 = w2slot(k)
        if tmm_wait:
            W("gpsimd", tMM, tmm_wait)
        if vp_wait:
            W("gpsimd", vP, vp_wait)
        dst = h[:, off0 + pc * 1024:off0 + (pc + 1) * 1024].rearrange(
            "p (f x) -> p f x", f=4)
        em("gpsimd", lambda e, dst=dst, hq=hq, pc=pc: e.dma_start(
            out=dst, in_=dw2r[:, pc * 4:(pc + 1) * 4,
                              hq * 256:(hq + 1) * 256]).then_inc(dP.h, 16))
        w2info[k] = (dP, dP.bump(16))

    for fc0 in range(2):
        W("gpsimd", dw[fc0], dw[fc0].n)
        emit_w1_load(0, fc0, fc0)

    W("gpsimd", vP, x2_built[NT - 1])
    em("gpsimd", lambda g: g.dma_start(out=biasb[:], in_=db2).then_inc(dC.h, 16))
    B2_RDY = dC.bump(16)
    g3 = gbuf[:].rearrange("p (f t) -> p f t", f=FC)
    gelu_mm = {}
    gelu_done = {}
    ffn2_mm = {}
    out_done = {}
    eserial = [0]

    for th in range(2):
        if th == 0:
            W("sync", stS, stS.n)
            for s in range(2):
                W("sync", dx[s], dx[s].n)
                em("sync", lambda e, s=s: e.dma_start(
                    out=x2r[:, s * 256:(s + 1) * 256],
                    in_=dx2s[s * 128:(s + 1) * 128, 0:256])
                   .then_inc(dx[s].h, 16))
                x2pre[s] = dx[s].bump(16)
        else:
            for s in range(2):
                W("sync", vP, out_done[30 + s])
                W("sync", stS, stS.n)
                W("sync", dx[s], dx[s].n)
                em("sync", lambda e, s=s: e.dma_start(
                    out=x2r[:, s * 256:(s + 1) * 256],
                    in_=dx2s[(8 + s) * 128:(9 + s) * 128, 0:256])
                   .then_inc(dx[s].h, 16))
                x2pre[32 + s] = dx[s].bump(16)
        for fc in range(FC):
            wslot = fc % 2
            if fc >= 2:
                W("gpsimd", tMM, gelu_mm[(th, fc - 2)])
                W("gpsimd", dw[wslot], dw[wslot].n)
                emit_w1_load(th, fc, wslot)
            if 8 <= fc < 16:
                tw = 0
                if fc == 8:
                    tw = t_wp_last if th == 0 else ffn2_mm[(0, 2, 7)]
                emit_w2_piece_g(0 if th == 0 else 5, fc - 8, tmm_wait=tw)
            elif fc in (24, 26, 28, 30):
                tw = ffn2_mm[(0, 3, 7)] if (th == 1 and fc == 24) else 0
                emit_w2_piece_g(1 if th == 0 else 6, (fc - 24) // 2,
                                tmm_wait=tw)
            for s in range(2):
                par = ffn_acquire()
                W("tensor", dw[wslot], w1_rdy[(th, fc)])
                if th == 0 and fc == 0 and s == 0:
                    W("tensor", vP, xn2_evac_last)

                def _f1(t, th=th, fc=fc, s=s, par=par, wslot=wslot):
                    for ec in range(EC):
                        ins = t.matmul(
                            ffn_regions[par],
                            w1slc[:, wslot * EC * 128 + ec * 128:
                                  wslot * EC * 128 + (ec + 1) * 128],
                            xnfm3[:, ec, th * 1024 + s * 512:
                                  th * 1024 + (s + 1) * 512],
                            start=(ec == 0), stop=(ec == EC - 1))
                        if ec == EC - 1:
                            ins.then_inc(tMM.h, 1)

                em("tensor", _f1)
                t_f1 = tMM.bump(1)
                if s == 1:
                    gelu_mm[(th, fc)] = t_f1
                W("scalar", tMM, t_f1)
                if th == 0 and fc == 0 and s == 0:
                    W("scalar", dK, CONSTS2)
                    W("scalar", tMM, attn_mm_end[EC - 1])
                    W("scalar", vP, atr_done[NT - 1])
                    W("scalar", tMM, t_wp_last)
                if th == 1 and fc == 0 and s == 0:
                    W("scalar", tMM, ffn2_mm[(0, 3, 7)])
                o = g3[:, fc, s * 512:(s + 1) * 512]
                em("scalar", lambda e, o=o, p=par, b=b1s[:, fc:fc + 1]:
                   e.activation(o, ffn_regions[p],
                                AF.Gelu_apprx_tanh, bias=b,
                                scale=1.0).then_inc(aP.h, 1))
                ffn_guard[par] = ("aP", aP.bump(1))
        gelu_done[th] = aP.n

        for hq in range(4):
            k4 = th * 4 + hq
            h2, off2 = w2slot(k4)
            if th == 0 and hq == 3:
                W("gpsimd", tMM, gelu_mm[(0, FC - 2)])
                W("gpsimd", dw[0], dw[0].n)
                emit_w1_load(1, 0, 0)
                W("gpsimd", tMM, gelu_mm[(0, FC - 1)])
                W("gpsimd", dw[1], dw[1].n)
                emit_w1_load(1, 1, 1)
            for ti in range(8):
                if th == 0:
                    if ti < 4:
                        emit_w2_piece_g(hq + 1, 4 + ti)
                    elif hq < 3:
                        tw = ffn2_mm[(0, hq - 1, 7)] if (ti == 4 and hq >= 1) else 0
                        vw = x2_add[NT - 1] if (ti == 4 and hq == 0) else 0
                        emit_w2_piece_g(hq + 2, ti - 4, tmm_wait=tw, vp_wait=vw)
                else:
                    if hq == 0 and ti < 4:
                        emit_w2_piece_g(6, 4 + ti)
                    elif hq == 1:
                        tw = ffn2_mm[(1, 0, 7)] if ti == 0 else 0
                        emit_w2_piece_g(7, ti, tmm_wait=tw)
                absti = th * 8 + ti
                par = ffn_acquire()
                w2s_, w2v_ = w2info[k4]
                W("tensor", w2s_, w2v_)
                W("tensor", aP, gelu_done[th])

                def _f2(t, ti=ti, par=par, h2=h2, off2=off2):
                    for fc in range(FC):
                        ins = t.matmul(
                            ffn_regions[par][:, 0:256],
                            g3[:, fc, ti * 128:(ti + 1) * 128],
                            h2[:, off2 + fc * 256:off2 + (fc + 1) * 256],
                            start=(fc == 0), stop=(fc == FC - 1))
                        if fc == FC - 1:
                            ins.then_inc(tMM.h, 1)

                em("tensor", _f2)
                ffn2_mm[(th, hq, ti)] = tMM.bump(1)
                ser = eserial[0]
                eserial[0] += 1
                xsl = ser % 2
                nser = ser + 1
                if nser % 32 != 0 and nser not in x2pre:
                    lhq, lti = (nser % 32) // 8, nser % 8
                    labs = th * 8 + lti
                    if ser >= 1:
                        W("sync", vP, out_done[ser - 1])
                    W("sync", stS, stS.n)
                    W("sync", dx[nser % 2], dx[nser % 2].n)
                    em("sync", lambda e, labs=labs, lhq=lhq, nxsl=nser % 2:
                       e.dma_start(
                           out=x2r[:, nxsl * 256:(nxsl + 1) * 256],
                           in_=dx2s[labs * 128:(labs + 1) * 128,
                                    lhq * 256:(lhq + 1) * 256])
                       .then_inc(dx[nxsl].h, 16))
                    x2pre[nser] = dx[nser % 2].bump(16)
                dxv = x2pre[ser]
                W("vector", tMM, ffn2_mm[(th, hq, ti)])
                W("vector", dx[xsl], dxv)
                W("vector", dC, B2_RDY)
                osl = ser % 2
                em("vector", lambda e, osl=osl, p=par, hq=hq: e.tensor_tensor(
                    tmpr[:, osl * 256:(osl + 1) * 256],
                    ffn_regions[p][:, 0:256],
                    biasb[:, hq * 256:(hq + 1) * 256],
                    ALU.add).then_inc(vP.h, 1))
                ffn_guard[par] = ("vP", vP.bump(1))
                W("vector", vP, ffn_guard[par][1])
                if ser >= 1:
                    W("vector", stO, 16 * (ser - 1))
                em("vector", lambda e, osl=osl, xsl=xsl, absti=absti:
                   e.scalar_tensor_tensor(
                       outr[:, osl * 256:(osl + 1) * 256],
                       tmpr[:, osl * 256:(osl + 1) * 256],
                       scals[:, absti:absti + 1],
                       x2r[:, xsl * 256:(xsl + 1) * 256],
                       ALU.mult, ALU.add).then_inc(vP.h, 1))
                out_done[ser] = vP.bump(1)
                W("sync", vP, out_done[ser])
                W("sync", stO, stO.n)
                em("sync", lambda e, osl=osl, absti=absti, hq=hq: e.dma_start(
                    out=dy[absti * 128:(absti + 1) * 128, hq * 256:(hq + 1) * 256],
                    in_=outr[:, osl * 256:(osl + 1) * 256]).then_inc(stO.h, 16))
                stO.bump(16)

    W("gpsimd", stO, stO.n)
    W("sync", stO, stO.n)
    if dbg:
        return _finish(nc, prog, em, W, sems, dbg, stack,
                       dict(aP=aP, vP=vP, tMM=tMM, stO=stO, stS=stS))

    return _replay(nc, prog, stack)


def _replay(nc, prog, stack):
    with nc.Block() as blk:
        @blk.sync
        def _(e):
            for f in prog["sync"]:
                f(e)

        @blk.gpsimd
        def _(e):
            for f in prog["gpsimd"]:
                f(e)

        @blk.scalar
        def _(e):
            for f in prog["scalar"]:
                f(e)

        @blk.vector
        def _(e):
            for f in prog["vector"]:
                f(e)

        @blk.tensor
        def _(e):
            for f in prog["tensor"]:
                f(e)
    stack.close()
    return nc


def _finish(nc, prog, em, W, sems, dbg, stack, s):
    aP, vP, tMM, stO = s["aP"], s["vP"], s["tMM"], s["stO"]
    W("sync", aP, aP.n)
    W("sync", vP, vP.n)
    W("sync", tMM, tMM.n)
    ndma = 0
    for name, hh in dbg:
        shp = [hh.shape[0], hh.shape[1]]
        d = nc.dram_tensor("dbg_" + name, shp, hh.dtype, kind="ExternalOutput").ap()
        em("sync", lambda e, d=d, hh=hh: e.dma_start(out=d[:, :], in_=hh[:])
           .then_inc(stO.h, 16))
        ndma += 1
        stO.bump(16)
    W("sync", stO, stO.n)
    return _replay(nc, prog, stack)


def _host_inputs(inputs):
    x = np.asarray(inputs["L0_attn_x"], dtype=np.float32)
    wqT = np.ascontiguousarray(np.asarray(inputs["L0_attn_Wq"], np.float32).T)
    wkT = np.ascontiguousarray(np.asarray(inputs["L0_attn_Wk"], np.float32).T)
    wvT = np.ascontiguousarray(np.asarray(inputs["L0_attn_Wv"], np.float32).T)
    wpT = np.ascontiguousarray(np.asarray(inputs["L0_attn_Wp"], np.float32).T)
    w1 = np.ascontiguousarray(np.asarray(inputs["L0_ffn_w1"], np.float32))
    w2 = np.ascontiguousarray(np.asarray(inputs["L0_ffn_w2"], np.float32))
    rw = np.ascontiguousarray(np.asarray(inputs["L0_ffn_router_w"], np.float32))
    bq = np.asarray(inputs["L0_attn_bq"], np.float32)
    bk = np.asarray(inputs["L0_attn_bk"], np.float32)
    bv = np.asarray(inputs["L0_attn_bv"], np.float32)
    bp = np.asarray(inputs["L0_attn_bp"], np.float32)
    b1 = np.asarray(inputs["L0_ffn_b1"], np.float32)
    b2 = np.asarray(inputs["L0_ffn_b2"], np.float32)

    bqc = np.ascontiguousarray(bq.reshape(EC, 128).T)
    bkc = np.ascontiguousarray(bk.reshape(EC, 128).T)
    b1c = np.ascontiguousarray(b1.reshape(FC, 128).T)
    bvb = np.ascontiguousarray(np.broadcast_to(bv, (128, E)))
    bpb = np.ascontiguousarray(np.broadcast_to(bp, (128, E)))
    b2b = np.ascontiguousarray(np.broadcast_to(b2, (128, E)))
    ident = np.eye(128, dtype=ml_dtypes.bfloat16)

    jr = np.arange(128)[:, None]
    qr = np.arange(384)[None, :]
    interior = ((qr >= jr) & (qr <= jr + 256)).astype(np.float32)

    in_maps = []
    for c in range(8):
        b, half = c // 2, c % 2
        start = half * TOWN
        lo, hi = start - WIN, start + TOWN + WIN
        xin = np.zeros((TKV, E), np.float32)
        vlo, vhi = max(lo, 0), min(hi, N)
        xin[vlo - lo:vhi - lo] = x[b, vlo:vhi]
        v0 = interior if half == 1 else np.zeros_like(interior)
        v2 = interior if half == 0 else np.zeros_like(interior)
        masks = np.stack([v0, interior, v2]).astype(ml_dtypes.bfloat16)
        in_maps.append({
            "x": xin, "wqT": wqT, "wkT": wkT, "wvT": wvT, "wpT": wpT,
            "w1": w1, "w2": w2, "rw": rw,
            "bqc": bqc, "bkc": bkc, "b1c": b1c,
            "bvb": bvb, "bpb": bpb, "b2b": b2b,
            "masks": masks, "ident": ident,
        })
    return in_maps


def kernel(**inputs):
    in_maps = _host_inputs(inputs)
    nc = build_nc()
    res = run_bass_kernel_spmd(nc, in_maps, core_ids=list(range(8)))
    y = np.empty((B, N, E), np.float32)
    for c in range(8):
        b, half = c // 2, c % 2
        y[b, half * TOWN:(half + 1) * TOWN] = res.results[c]["y"]
    return y


# version 33
# speedup vs baseline: 1.3720x; 1.1352x over previous
import contextlib

import numpy as np
import ml_dtypes
import concourse.bass as bass
import concourse.mybir as mybir
from concourse.bass_utils import run_bass_kernel_spmd

F32 = mybir.dt.float32
BF16 = mybir.dt.bfloat16
AF = mybir.ActivationFunctionType
ALU = mybir.AluOpType
AX = mybir.AxisListType

B, N, E, NH, HD, FF = 4, 4096, 1024, 16, 64, 4096
WIN = 128
TOWN = 2048
TKV = 2304
NT = TOWN // 128
NKV = TKV // 128
EC = E // 128
FC = FF // 128
EPS = 1e-5
SCL = 1.0 / 32.0
NPT = 4


def _qrange(jt):
    return max(0, jt * 128 - 256), min(TOWN, jt * 128 + 128)


def _mask_blocks(jt):
    q0, q1 = _qrange(jt)
    off = q0 - (jt * 128 - 256)
    if jt == 0:
        return [(0, q1 - q0, 0, off)]
    if jt == NKV - 1:
        return [(0, q1 - q0, 2, off)]
    out = []
    for qr_lo, qr_hi in ((0, 128), (256, 384)):
        lo = max(q0, jt * 128 - 256 + qr_lo)
        hi = min(q1, jt * 128 - 256 + qr_hi)
        if lo < hi:
            out.append((lo - q0, hi - q0, 1, lo - q0 + off))
    return out


class SemC:
    def __init__(self, h):
        self.h = h
        self.n = 0

    def bump(self, k):
        self.n += k
        return self.n


def build_nc(stop="full", dbg=()):
    nc = bass.Bass("TRN2", target_bir_lowering=False)

    dx_in = nc.dram_tensor("x", [TKV, E], F32, kind="ExternalInput").ap()
    dwq = nc.dram_tensor("wqT", [E, E], F32, kind="ExternalInput").ap()
    dwk = nc.dram_tensor("wkT", [E, E], F32, kind="ExternalInput").ap()
    dwv = nc.dram_tensor("wvT", [E, E], F32, kind="ExternalInput").ap()
    dwp = nc.dram_tensor("wpT", [E, E], F32, kind="ExternalInput").ap()
    dw1 = nc.dram_tensor("w1", [E, FF], F32, kind="ExternalInput").ap()
    dw2 = nc.dram_tensor("w2", [FF, E], F32, kind="ExternalInput").ap()
    drw = nc.dram_tensor("rw", [E, 8], F32, kind="ExternalInput").ap()
    dbq = nc.dram_tensor("bqc", [128, EC], F32, kind="ExternalInput").ap()
    dbk = nc.dram_tensor("bkc", [128, EC], F32, kind="ExternalInput").ap()
    db1 = nc.dram_tensor("b1c", [128, FC], F32, kind="ExternalInput").ap()
    dbv = nc.dram_tensor("bvb", [128, E], F32, kind="ExternalInput").ap()
    dbp = nc.dram_tensor("bpb", [128, E], F32, kind="ExternalInput").ap()
    db2 = nc.dram_tensor("b2b", [128, E], F32, kind="ExternalInput").ap()
    dmask = nc.dram_tensor("masks", [3, 128, 384], BF16, kind="ExternalInput").ap()
    dident = nc.dram_tensor("ident", [128, 128], BF16, kind="ExternalInput").ap()
    dy = nc.dram_tensor("y", [TOWN, E], F32, kind="ExternalOutput").ap()
    dx2s = nc.dram_tensor("x2s", [TOWN, E], F32, kind="Internal").ap()

    A = nc.alloc_sbuf_tensor
    xr = A("xr", [128, 2 * 1024], F32)
    x2t = A("x2t", [128, 2 * 1024], F32)
    x2r = A("x2r", [128, 2 * 256], F32)
    outr = A("outr", [128, 2 * 256], F32)
    tmpr = A("tmpr", [128, 2 * 256], F32)
    xnscr = A("xnscr", [128, 2 * 1024], BF16)
    sqscr = A("sqscr", [128, 1024], F32)
    xnfm = A("xnfm", [128, EC * TKV], BF16)
    lns = A("lns", [128, 8 * 12], F32)
    epsc = A("epsc", [128, 1], F32)
    biasb = A("biasb", [128, 1024], BF16)
    bqs = A("bqs", [128, EC], F32)
    bks = A("bks", [128, EC], F32)
    b1s = A("b1s", [128, FC], F32)
    rws = A("rws", [128, EC * 8], BF16)
    masksb = A("masksb", [128, 3 * 384], BF16)
    identb = A("identb", [128, 128], BF16)
    ptr = A("ptr", [128, NPT * 768], BF16)
    rz = A("rz", [128, 4], F32)
    scals = A("scals", [128, NT], F32)
    lgt = A("lgt", [128, 128], F32)
    rsc = A("rsc", [128, 64], F32)
    wreg = A("wreg", [128, 16 * 1024], BF16)
    w1slc = A("w1slc", [128, 2 * EC * 128], BF16)

    vbytes = NKV * 1040 * 2
    qbytes = 2 * TOWN * 2
    kbytes = 2 * TKV * 2
    wqkbytes = 4 * 1024 * 2
    atmbytes = NT * 1024 * 2
    arena_b = max(vbytes + qbytes + kbytes + wqkbytes + atmbytes,
                  FC * 1024 * 2)
    arena = A("arena", [128, arena_b], mybir.dt.uint8)
    a_addr = nc.lookup_mloc(arena).addr
    AT = nc.alloc_sbuf_tensor_at
    off = a_addr
    vp = AT("vp", [128, NKV * 1040], BF16, offset=off); off += vbytes
    slot0_off = off
    qb0 = AT("qb0", [128, TOWN], BF16, offset=off); off += TOWN * 2
    kb0 = AT("kb0", [128, TKV], BF16, offset=off); off += TKV * 2
    qb1 = AT("qb1", [128, TOWN], BF16, offset=off); off += TOWN * 2
    kb1 = AT("kb1", [128, TKV], BF16, offset=off); off += TKV * 2
    wqks_off = off
    wqks = AT("wqks", [128, 4 * 1024], BF16, offset=off); off += wqkbytes
    atm = AT("atm", [128, NT * 1024], BF16, offset=off)
    gbuf = AT("gbuf", [128, FC * 1024], BF16, offset=a_addr)
    xpre = AT("xpre", [128, 4 * 1024], F32, offset=a_addr + FC * 1024 * 2)
    w2s2 = AT("w2s2", [128, FC * 256], BF16, offset=a_addr + FC * 1024 * 2)
    w1slc2 = AT("w1slc2", [128, 2 * EC * 128], BF16,
                offset=a_addr + FC * 1024 * 2 + FC * 256 * 2)
    qbs, kbs = [qb0, qb1], [kb0, kb1]
    wpts0 = AT("wpts0", [128, 4 * 1024], BF16, offset=slot0_off)
    wpts1 = AT("wpts1", [128, 4 * 1024], BF16, offset=wqks_off)
    wptsv = [wpts0, wpts1]
    w_addr = nc.lookup_mloc(wreg).addr
    wvslc = AT("wvslc", [128, 2 * EC * 512], BF16, offset=w_addr)

    stp = [nc.alloc_psum_tensor("st0", [128, 1024], F32),
           nc.alloc_psum_tensor("st1", [128, 1024], F32)]
    pp = nc.alloc_psum_tensor("pp", [128, 1024], F32)
    av = nc.alloc_psum_tensor("av", [128, 1024], F32)

    nc._dbg_handles = dict(xnfm=xnfm, vp=vp, qb=qb0, kb=kb0, atm=atm,
                           gbuf=gbuf, wpts=wpts0, wreg=wreg, ptr=ptr,
                           scals=scals, x2t=x2t, xnscr=xnscr,
                           biasb=biasb, wqks=wqks, masksb=masksb, rsc=rsc,
                           lgt=lgt, wvslc=wvslc, w1slc=w1slc, rz=rz)
    if dbg:
        dbg = [(n, nc._dbg_handles[n]) for n in dbg]
    prog = {k: [] for k in ("sync", "gpsimd", "scalar", "vector", "tensor")}

    def em(eng, f):
        prog[eng].append(f)

    stack = contextlib.ExitStack()
    sems = {}
    for name in ("dxA", "dxB", "dwA", "dwB", "dC", "dK", "stS", "stO",
                 "dP", "dQ", "tMM", "aP", "vP"):
        sems[name] = SemC(stack.enter_context(nc.semaphore(name)))
    dx = [sems["dxA"], sems["dxB"]]
    dw = [sems["dwA"], sems["dwB"]]
    dC, stS, stO = sems["dC"], sems["stS"], sems["stO"]
    dK = sems["dK"]
    dP, dQ = sems["dP"], sems["dQ"]
    tMM, aP, vP = sems["tMM"], sems["aP"], sems["vP"]

    def W(eng, s, v):
        if v > 0:
            em(eng, lambda e, _h=s.h, _v=v: e.wait_ge(_h, _v))

    xnfm3 = xnfm[:].rearrange("p (c t) -> p c t", c=EC)

    def _consts_g(g):
        g.dma_start(out=masksb[:].rearrange("p (v q) -> p v q", v=3),
                    in_=dmask.rearrange("v p q -> p v q")).then_inc(dC.h, 16)
        g.dma_start(out=identb[:], in_=dident).then_inc(dC.h, 16)
        g.dma_start(out=biasb[:], in_=dbv).then_inc(dC.h, 16)
        g.dma_start(out=rws[:].rearrange("p (c x) -> p c x", c=EC),
                    in_=drw.rearrange("(c p) x -> p c x", p=128)).then_inc(dC.h, 16)
        for fh in range(2):
            g.dma_start(
                out=wvslc[:, fh * EC * 512:(fh + 1) * EC * 512]
                    .rearrange("p (c f) -> p c f", c=EC),
                in_=dwv.rearrange("(c p) f -> p c f", p=128)[:, :, fh * 512:(fh + 1) * 512],
            ).then_inc(dC.h, 16)
        for d_ap, col in ((dwq, 0), (dwk, 2 * 1024)):
            g.dma_start(
                out=wqks[:, col:col + 1024].rearrange("p (c f) -> p c f", c=EC),
                in_=d_ap.rearrange("(c p) f -> p c f", p=128)[:, :, 0:128],
            ).then_inc(dC.h, 16)

    em("gpsimd", _consts_g)
    CONSTS = dC.bump(16 * 8)

    def _consts_s(s):
        s.dma_start(out=bqs[:], in_=dbq).then_inc(dK.h, 16)
        s.dma_start(out=bks[:], in_=dbk).then_inc(dK.h, 16)
        s.dma_start(out=b1s[:], in_=db1).then_inc(dK.h, 16)

    em("sync", _consts_s)
    CONSTS2 = dK.bump(16 * 3)


    def _vones(v):
        ap = vp[:].rearrange("p (t h o) -> p t h o", t=NKV, h=NH)[:, :, :, 64:65]
        v.memset(ap, 1.0)
        v.memset(epsc[:], EPS)

    em("vector", _vones)

    LNG = 6
    lnstate = {"rdy": 0}

    def _lncol(f, slot, k, n=1):
        base = f * 12 + slot * LNG + k
        return lns[:, base:base + n]

    def emit_stats(src_ap, slot, k):
        if lnstate["rdy"]:
            W("scalar", aP, lnstate["rdy"])
        em("scalar", lambda e, a=src_ap, o=_lncol(1, slot, k): e.activation(
            sqscr[:], a, AF.Square, accum_out=o).then_inc(aP.h, 1))
        a_sq = aP.bump(1)
        W("scalar", aP, a_sq)
        em("scalar", lambda e, a=src_ap, o=_lncol(0, slot, k): e.activation(
            sqscr[:], a, AF.Identity, accum_out=o).then_inc(aP.h, 1))
        a_st = aP.bump(1)
        lnstate["rdy"] = a_st
        return [(aP, a_st)]

    def emit_ladder(slot, n, war_a=0):
        R = _lncol(0, slot, 0, n)
        S = _lncol(1, slot, 0, n)
        MT = _lncol(2, slot, 0, n)
        MR = _lncol(3, slot, 0, n)
        V_ = _lncol(4, slot, 0, n)
        L_ = _lncol(5, slot, 0, n)
        RS = _lncol(6, slot, 0, n)
        NB = _lncol(7, slot, 0, n)
        W("vector", aP, lnstate["rdy"])
        if war_a:
            W("vector", aP, war_a)
        em("vector", lambda e: e.tensor_tensor(MT, R, R, ALU.mult).then_inc(vP.h, 1))
        W("vector", vP, vP.bump(1))
        em("vector", lambda e: e.tensor_scalar(
            MR, MT, 1.0 / (E * E), -EPS, ALU.mult, ALU.add).then_inc(vP.h, 1))
        W("vector", vP, vP.bump(1))
        em("vector", lambda e: e.scalar_tensor_tensor(
            V_, S, 1.0 / E, MR, ALU.mult, ALU.subtract).then_inc(vP.h, 1))
        e5 = vP.bump(1)
        W("scalar", vP, e5)
        em("scalar", lambda e: e.activation(
            L_, V_, AF.Ln, bias=epsc[:], scale=1.0).then_inc(aP.h, 1))
        a_ln = aP.bump(1)
        W("scalar", aP, a_ln)
        em("scalar", lambda e: e.activation(
            RS, L_, AF.Exp, bias=0.0, scale=-0.5).then_inc(aP.h, 1))
        a_rstd = aP.bump(1)
        W("vector", aP, a_rstd)
        em("vector", lambda e: e.tensor_tensor(NB, R, RS, ALU.mult).then_inc(vP.h, 1))
        W("vector", vP, vP.bump(1))
        em("vector", lambda e: e.tensor_single_scalar(
            NB, NB, -1.0 / E, ALU.mult).then_inc(vP.h, 1))
        return vP.bump(1)

    ln = {"ttr": {}}

    def emit_xn(src_ap, dst_slot, slot, k, tile_id):
        if tile_id - 2 in ln["ttr"]:
            W("scalar", tMM, ln["ttr"][tile_id - 2])
        dsta = xnscr[:, dst_slot * 1024:(dst_slot + 1) * 1024]
        em("scalar", lambda e, a=src_ap, o=dsta, r=_lncol(6, slot, k),
                     b=_lncol(7, slot, k): e.activation(
            o, a, AF.Identity, bias=b, scale=r).then_inc(aP.h, 1))
        return aP.bump(1)

    tr = {"par": 0, "lastv": [0, 0], "first": True}

    def emit_transpose(src_ap, dst3_ap, wait_a, extra_t_wait=0):
        par = tr["par"]
        tr["par"] ^= 1
        avb = av[:, par * 512:(par + 1) * 512].bitcast(BF16)
        if tr["first"]:
            W("tensor", dC, CONSTS)
            tr["first"] = False
        W("tensor", aP, wait_a)
        if extra_t_wait:
            W("tensor", aP, extra_t_wait)
        W("tensor", vP, tr["lastv"][par])

        def _tr(t, s=src_ap, o=avb):
            for c in range(8):
                ins = t.transpose(o[:, c * 128:(c + 1) * 128],
                                  s[:, c * 128:(c + 1) * 128], identb[:])
                if c == 7:
                    ins.then_inc(tMM.h, 1)

        em("tensor", _tr)
        t_tr = tMM.bump(1)
        W("vector", tMM, t_tr)
        em("vector", lambda e, i0=avb.rearrange("p (c t) -> p c t", c=8),
                     o=dst3_ap: e.tensor_copy(o, i0).then_inc(vP.h, 1))
        tr["lastv"][par] = vP.bump(1)
        return t_tr, tr["lastv"][par]

    ppctr = [0]
    pp_guard = [("vP", 0), ("vP", 0)]

    def pp_acquire():
        par = ppctr[0] % 2
        ppctr[0] += 1
        gs, gv = pp_guard[par]
        W("tensor", sems[gs], gv)
        return par

    xload = [0]
    xcons = {}

    def emit_xload(src_ap2, extra=()):
        i = xload[0]
        xload[0] += 1
        sl = i % 2
        for s, v in xcons.get(i - 2, []):
            W("sync", s, v)
        for s, v in extra:
            W("sync", s, v)
        W("sync", dx[sl], dx[sl].n)
        dst = xr[:, sl * 1024:(sl + 1) * 1024]
        em("sync", lambda e, s=src_ap2, d=dst: e.dma_start(out=d, in_=s)
           .then_inc(dx[sl].h, 16))
        return i, sl, dx[sl].bump(16)

    xn_a = {}
    tV_groups = []
    agroups = [(0, LNG), (LNG, LNG), (2 * LNG, NKV - 2 * LNG)]
    ladder_rdy = {}
    xn_group_last = {}
    for gi in range(len(agroups) + 1):
        if gi < len(agroups):
            t0g, ng = agroups[gi]
            for k in range(ng):
                i = t0g + k
                ser, sl, dxv = emit_xload(dx_in[i * 128:(i + 1) * 128, :])
                W("scalar", dx[sl], dxv)
                xcons[ser] = emit_stats(xr[:, sl * 1024:(sl + 1) * 1024], gi % 2, k)
            ladder_rdy[gi] = emit_ladder(gi % 2, ng,
                                         war_a=xn_group_last.get(gi - 2, 0))
        if gi >= 1:
            t0g, ng = agroups[gi - 1]
            W("scalar", vP, ladder_rdy[gi - 1])
            rl = {0: emit_xload(dx_in[t0g * 128:(t0g + 1) * 128, :])}
            for k in range(ng):
                i = t0g + k
                if k + 1 < ng:
                    rl[k + 1] = emit_xload(
                        dx_in[(i + 1) * 128:(i + 2) * 128, :])
                ser, sl, dxv = rl[k]
                W("scalar", dx[sl], dxv)
                xn_a[i] = emit_xn(xr[:, sl * 1024:(sl + 1) * 1024], sl,
                                  (gi - 1) % 2, k, i)
                xcons[ser] = [(aP, xn_a[i])]
                t_tr, vtr_j = emit_transpose(
                    xnscr[:, sl * 1024:(sl + 1) * 1024],
                    xnfm3[:, :, i * 128:(i + 1) * 128], xn_a[i])
                ln["ttr"][i] = t_tr
                W("tensor", dC, CONSTS)
                W("tensor", vP, vtr_j)
                for fh in range(2):
                    par = pp_acquire()

                    def _vproj(t, i=i, fh=fh, par=par):
                        for ec in range(EC):
                            ins = t.matmul(
                                pp[:, par * 512:(par + 1) * 512],
                                xnfm3[:, ec, i * 128:(i + 1) * 128],
                                wvslc[:, (fh * EC + ec) * 512:(fh * EC + ec + 1) * 512],
                                start=(ec == 0), stop=(ec == EC - 1))
                            if ec == EC - 1:
                                ins.then_inc(tMM.h, 1)

                    em("tensor", _vproj)
                    t_v = tMM.bump(1)
                    tV_groups.append(t_v)
                    W("vector", tMM, t_v)
                    W("vector", dC, CONSTS)
                    dst = vp[:].rearrange("p (t h o) -> p t h o", t=NKV, h=NH)[
                        :, i, fh * 8:(fh + 1) * 8, 0:64]
                    srcp = pp[:, par * 512:(par + 1) * 512].rearrange(
                        "p (h d) -> p h d", h=8)
                    bsl = biasb[:, fh * 512:(fh + 1) * 512].rearrange(
                        "p (h d) -> p h d", h=8)
                    em("vector", lambda e, o=dst, i0=srcp, i1=bsl: e.tensor_tensor(
                        o, i0, i1, ALU.add).then_inc(vP.h, 1))
                    pp_guard[par] = ("vP", vP.bump(1))
            xn_group_last[gi - 1] = xn_a[t0g + ng - 1]

    if stop == "A":
        return _finish(nc, prog, em, W, sems, dbg, stack,
                       dict(aP=aP, vP=vP, tMM=tMM, stO=stO, stS=stS))

    qk_w_ready = {0: CONSTS}
    qk_evac_end = {}
    attn_mm_end = {}
    av_evac = {}
    avmm = {}
    qkproj_mm_end = 0
    last_st_exp = 0

    def load_qk_slices(c, wslot):
        W("gpsimd", dw[wslot], dw[wslot].n)

        def _ld(g, c=c, wslot=wslot):
            for d_ap, col in ((dwq, wslot * 1024), (dwk, (2 + wslot) * 1024)):
                g.dma_start(
                    out=wqks[:, col:col + 1024].rearrange("p (e f) -> p e f", e=EC),
                    in_=d_ap.rearrange("(e p) f -> p e f", p=128)[:, :, c * 128:(c + 1) * 128],
                ).then_inc(dw[wslot].h, 16)
        em("gpsimd", _ld)
        return dw[wslot].bump(32)

    qkproj_mm_end_m = [0]

    for it in range(EC + 1):
        c, ca = it, it - 1
        if c < EC and c + 1 < EC:
            if c - 1 >= 0:
                W("gpsimd", tMM, qkproj_mm_end_m[0])
            qk_w_ready[c + 1] = load_qk_slices(c + 1, (c + 1) % 2)

        proj_jobs = []
        if c < EC:
            for kind in ("q", "k"):
                tlen = TOWN if kind == "q" else TKV
                for s in range((tlen + 511) // 512):
                    proj_jobs.append((kind, s * 512, min(512, tlen - s * 512)))
        pj = [0]
        first_evac = [True]

        def emit_proj_group(c=c, proj_jobs=proj_jobs, pj=pj, first_evac=first_evac):
            if pj[0] >= len(proj_jobs):
                return
            kind, lo, ncol = proj_jobs[pj[0]]
            pj[0] += 1
            par = pp_acquire()
            if pj[0] == 1:
                if c == 0:
                    W("tensor", dC, CONSTS)
                else:
                    W("tensor", dw[c % 2], qk_w_ready[c])

            def _proj(t, kind=kind, c=c, lo=lo, ncol=ncol, par=par):
                wcol = (0 if kind == "q" else 2 * 1024) + (c % 2) * 1024
                src_lo = (128 + lo) if kind == "q" else lo
                for ec in range(EC):
                    ins = t.matmul(
                        pp[:, par * 512:par * 512 + ncol],
                        wqks[:, wcol + ec * 128:wcol + (ec + 1) * 128],
                        xnfm3[:, ec, src_lo:src_lo + ncol],
                        start=(ec == 0), stop=(ec == EC - 1))
                    if ec == EC - 1:
                        ins.then_inc(tMM.h, 1)

            em("tensor", _proj)
            t_g = tMM.bump(1)
            qkproj_mm_end_m[0] = t_g
            if first_evac[0]:
                if c == 0:
                    W("scalar", dK, CONSTS2)
                if c - 2 >= 0:
                    W("scalar", tMM, attn_mm_end[c - 2])
                first_evac[0] = False
            W("scalar", tMM, t_g)
            dstb = qbs[c % 2] if kind == "q" else kbs[c % 2]
            bias = (bqs if kind == "q" else bks)[:, c:c + 1]
            dsta = dstb[:, lo:lo + ncol]
            em("scalar", lambda e, o=dsta, p=par, ncol=ncol, b=bias:
               e.activation(o, pp[:, p * 512:p * 512 + ncol], AF.Identity,
                            bias=b, scale=1.0).then_inc(aP.h, 1))
            pp_guard[par] = ("aP", aP.bump(1))

        if ca < 0:
            while pj[0] < len(proj_jobs):
                emit_proj_group()
            qk_evac_end[c] = aP.n
            continue

        prev_exp_end = last_st_exp
        st_exp = {}
        mask_done = {}
        qba, kba = qbs[ca % 2], kbs[ca % 2]
        first = True
        for jt in range(NKV + 1):
            if jt < NKV:
                q0, q1 = _qrange(jt)
                nq = q1 - q0
                stpar = jt % 2
                if first:
                    W("tensor", aP, qk_evac_end[ca])
                    first = False
                if jt - 2 in st_exp:
                    W("tensor", aP, st_exp[jt - 2])
                elif ca >= 1:
                    W("tensor", aP, prev_exp_end)

                def _st(t, jt=jt, q0=q0, q1=q1, nq=nq, stpar=stpar, kba=kba, qba=qba):
                    for hi in range(2):
                        ins = t.matmul(
                            stp[stpar][:, hi * 512:hi * 512 + nq],
                            kba[hi * 64:hi * 64 + 64, jt * 128:(jt + 1) * 128],
                            qba[hi * 64:hi * 64 + 64, q0:q1],
                            start=True, stop=True)
                        if hi == 1:
                            ins.then_inc(tMM.h, 1)

                em("tensor", _st)
                t_st = tMM.bump(1)
                slot = jt % NPT
                W("scalar", tMM, t_st)
                if jt - NPT >= 0 and (ca, jt - NPT) in avmm:
                    W("scalar", tMM, avmm[(ca, jt - NPT)])
                elif jt < NPT and ca >= 1:
                    W("scalar", tMM, attn_mm_end[ca - 1])
                srce = stp[stpar][:].rearrange("p (h q) -> p h q", h=2)[:, :, 0:nq]
                dpt = ptr[:, slot * 768:(slot + 1) * 768].rearrange(
                    "p (h q) -> p h q", h=2)[:, :, 0:nq]
                em("scalar", lambda e, o=dpt, i0=srce: e.activation(
                    o, i0, AF.Exp, bias=0.0, scale=SCL).then_inc(aP.h, 1))
                st_exp[jt] = aP.bump(1)
                last_st_exp = st_exp[jt]
                W("vector", aP, st_exp[jt])
                W("vector", dC, CONSTS)
                nm = 0
                for lo, hi_, var, qr in _mask_blocks(jt):
                    for hh in range(2):
                        o = ptr[:, slot * 768 + hh * 384 + lo:
                                slot * 768 + hh * 384 + hi_]
                        m = masksb[:, var * 384 + qr:var * 384 + qr + hi_ - lo]
                        em("vector", lambda e, o=o, m=m: e.tensor_tensor(
                            o, o, m, ALU.mult).then_inc(vP.h, 1))
                        nm += 1
                mask_done[jt] = vP.bump(nm)
            if jt % 2 == 1:
                emit_proj_group()
            qi = jt - 3
            if 0 <= qi < NT:
                avpar = qi % 2
                if (ca, qi - 2) in av_evac:
                    W("tensor", aP, av_evac[(ca, qi - 2)])
                elif qi < 2 and ca >= 1:
                    W("tensor", aP, av_evac[(ca - 1, NT - 1)])
                W("tensor", vP, mask_done[qi + 2])

                def _av(t, qi=qi, avpar=avpar, ca=ca):
                    for hi in range(2):
                        habs = 2 * ca + hi
                        for c3 in range(3):
                            js = qi + c3
                            sl2 = js % NPT
                            o2 = qi * 128 - _qrange(js)[0]
                            ins = t.matmul(
                                av[:, avpar * 512 + hi * 65:
                                   avpar * 512 + hi * 65 + 65],
                                ptr[:, sl2 * 768 + hi * 384 + o2:
                                    sl2 * 768 + hi * 384 + o2 + 128],
                                vp[:, js * 1040 + habs * 65:
                                   js * 1040 + habs * 65 + 65],
                                start=(c3 == 0), stop=(c3 == 2))
                            if hi == 1 and c3 == 2:
                                ins.then_inc(tMM.h, 1)

                em("tensor", _av)
                avmm[(ca, qi)] = tMM.bump(1)
                W("vector", tMM, avmm[(ca, qi)])
                if (ca, qi - 2) in av_evac:
                    W("vector", aP, av_evac[(ca, qi - 2)])
                elif qi < 2 and ca >= 1:
                    W("vector", aP, av_evac[(ca - 1, NT - 1)])
                rzs = 2 * (qi % 2)

                def _rz(e, avpar=avpar, rzs=rzs):
                    src = av[:, avpar * 512:avpar * 512 + 130].rearrange(
                        "p (h o) -> p h o", h=2)[:, :, 64:65]
                    e.reciprocal(
                        rz[:, rzs:rzs + 2].rearrange("p (h o) -> p h o", h=2),
                        src).then_inc(vP.h, 1)

                em("vector", _rz)
                v_rz = vP.bump(1)
                W("scalar", vP, v_rz)
                for hi in range(2):
                    habs = 2 * ca + hi
                    o = atm[:, qi * 1024 + habs * 64:qi * 1024 + habs * 64 + 64]
                    i0 = av[:, avpar * 512 + hi * 65:avpar * 512 + hi * 65 + 64]
                    em("scalar", lambda e, o=o, i0=i0, hi=hi, rzs=rzs:
                       e.activation(o, i0, AF.Identity, bias=0.0,
                                    scale=rz[:, rzs + hi:rzs + hi + 1])
                       .then_inc(aP.h, 1))
                av_evac[(ca, qi)] = aP.bump(2)
        while pj[0] < len(proj_jobs):
            emit_proj_group()
        if c < EC:
            qk_evac_end[c] = aP.n
        attn_mm_end[ca] = tMM.n

    if stop == "B":
        return _finish(nc, prog, em, W, sems, dbg, stack,
                       dict(aP=aP, vP=vP, tMM=tMM, stO=stO, stS=stS))
    W("gpsimd", vP, vP.n)
    em("gpsimd", lambda g: g.dma_start(out=biasb[:], in_=dbp).then_inc(dC.h, 16))
    BP_RDY = dC.bump(16)
    W("gpsimd", tMM, attn_mm_end[EC - 2])
    W("gpsimd", aP, qk_evac_end[EC - 1])

    def _ldwp0(g):
        g.dma_start(out=wpts0[:].rearrange("p (e f) -> p e f", e=4),
                    in_=dwp.rearrange("(e p) f -> p e f", p=128)[:, 0:4, :]
                    ).then_inc(dw[0].h, 16)

    em("gpsimd", _ldwp0)
    W("gpsimd", tMM, qkproj_mm_end_m[0])

    def _ldwp1(g):
        g.dma_start(out=wpts1[:].rearrange("p (e f) -> p e f", e=4),
                    in_=dwp.rearrange("(e p) f -> p e f", p=128)[:, 4:8, :]
                    ).then_inc(dw[0].h, 16)

    em("gpsimd", _ldwp1)
    WPT_RDY = dw[0].bump(32)

    afm3 = wreg[:, 0:EC * 2048].rearrange("p (e t) -> p e t", e=EC)
    atr_done = {}
    for qi in range(NT):
        wait_a = av_evac[(EC - 1, NT - 1)] if qi < 2 else av_evac[(EC - 1, qi)]
        t_atr, atr_done[qi] = emit_transpose(
            atm[:, qi * 1024:(qi + 1) * 1024],
            afm3[:, :, qi * 128:(qi + 1) * 128], wait_a)
        if qi == 9:
            atr_t9 = t_atr

    xpre_rdy = {}
    W("sync", tMM, atr_t9)
    for s in range(2):
        W("sync", dQ, dQ.n)
        em("sync", lambda e, s=s: e.dma_start(
            out=xpre[:, s * 1024:(s + 1) * 1024],
            in_=dx_in[128 + s * 128:256 + s * 128, :]).then_inc(dQ.h, 16))
        xpre_rdy[s] = (dQ, dQ.bump(16))

    xn_c = {}
    x2_built = {}
    x2_wrote = {}
    x2_add = {}
    x2cons = {}
    t_wp_last = 0
    xn2_evac_last = 0
    cladder = {}
    slot_last_xn = {0: xn_group_last.get(2, 0), 1: xn_group_last.get(1, 0)}
    cgroups = [(0, LNG), (LNG, LNG), (2 * LNG, NT - 2 * LNG)]
    for gi in range(len(cgroups) + 1):
        if gi < len(cgroups):
            t0g, ng = cgroups[gi]
            for k in range(ng):
                ti = t0g + k
                sl = ti % 2
                for fh in range(2):
                    par = pp_acquire()
                    W("tensor", vP, atr_done[ti])
                    W("tensor", dw[0], WPT_RDY)

                    def _wp(t, ti=ti, fh=fh, par=par):
                        for ec in range(EC):
                            ins = t.matmul(
                                pp[:, par * 512:(par + 1) * 512],
                                afm3[:, ec, ti * 128:(ti + 1) * 128],
                                wptsv[ec // 4][:, (ec % 4) * 1024 + fh * 512:
                                           (ec % 4) * 1024 + (fh + 1) * 512],
                                start=(ec == 0), stop=(ec == EC - 1))
                            if ec == EC - 1:
                                ins.then_inc(tMM.h, 1)

                    em("tensor", _wp)
                    t_wp_last = tMM.bump(1)
                    W("vector", tMM, t_wp_last)
                    W("vector", dC, BP_RDY)
                    for s, v in x2cons.get(ti - 2, []):
                        W("vector", s, v)
                    if ti - 2 >= 0:
                        W("vector", stS, stS.n)
                    o = x2t[:, sl * 1024 + fh * 512:sl * 1024 + (fh + 1) * 512]
                    em("vector", lambda e, o=o, p=par,
                                 i1=biasb[:, fh * 512:(fh + 1) * 512]:
                       e.tensor_tensor(o, pp[:, p * 512:(p + 1) * 512], i1,
                                       ALU.add).then_inc(vP.h, 1))
                    pp_guard[par] = ("vP", vP.bump(1))
                x2_built[ti] = vP.n
                W("vector", vP, x2_built[ti])
                ps, pv = xpre_rdy[ti]
                W("vector", ps, pv)
                em("vector", lambda e, ti=ti, sl=sl: e.tensor_tensor(
                    x2t[:, sl * 1024:(sl + 1) * 1024],
                    x2t[:, sl * 1024:(sl + 1) * 1024],
                    xpre[:, (ti % 4) * 1024:((ti % 4) + 1) * 1024],
                    ALU.add).then_inc(vP.h, 1))
                x2_add[ti] = vP.bump(1)
                W("gpsimd", vP, x2_add[ti])
                W("gpsimd", stS, stS.n)
                em("gpsimd", lambda e, ti=ti, sl=sl: e.dma_start(
                    out=dx2s[ti * 128:(ti + 1) * 128, :],
                    in_=x2t[:, sl * 1024:(sl + 1) * 1024]).then_inc(stS.h, 16))
                x2_wrote[ti] = stS.bump(16)
                tj = ti + 2
                if tj <= NT - 1:
                    if tj - 4 >= 0:
                        W("gpsimd", vP, x2_add[tj - 4])
                    else:
                        W("gpsimd", tMM, atr_t9)
                    W("gpsimd", dP, dP.n)
                    em("gpsimd", lambda e, tj=tj: e.dma_start(
                        out=xpre[:, (tj % 4) * 1024:((tj % 4) + 1) * 1024],
                        in_=dx_in[128 + tj * 128:256 + tj * 128, :])
                       .then_inc(dP.h, 16))
                    xpre_rdy[tj] = (dP, dP.bump(16))
                W("scalar", vP, x2_add[ti])
                x2cons[ti] = emit_stats(
                    x2t[:, sl * 1024:(sl + 1) * 1024], gi % 2, k)

            cladder[gi] = emit_ladder(gi % 2, ng, war_a=slot_last_xn[gi % 2])
        if gi >= 1:
            t0g, ng = cgroups[gi - 1]
            W("scalar", vP, cladder[gi - 1])
            jj0 = t0g
            rl = {0: emit_xload(dx2s[jj0 * 128:(jj0 + 1) * 128, :],
                                extra=[(stS, stS.n)])}
            for k in range(ng):
                j = t0g + k
                if k + 1 < ng:
                    jj = j + 1
                    rl[k + 1] = emit_xload(dx2s[jj * 128:(jj + 1) * 128, :],
                                           extra=[(stS, stS.n)])
                ser, sl, dxv = rl[k]
                W("scalar", dx[sl], dxv)
                xn_c[j] = emit_xn(xr[:, sl * 1024:(sl + 1) * 1024], sl,
                                  (gi - 1) % 2, k, NKV + j)
                xcons[ser] = [(aP, xn_c[j])]
                t_tr, ev = emit_transpose(
                    xnscr[:, sl * 1024:(sl + 1) * 1024],
                    xnfm3[:, :, j * 128:(j + 1) * 128], xn_c[j])
                ln["ttr"][NKV + j] = t_tr
                xn2_evac_last = ev
                W("tensor", dC, CONSTS)
                W("tensor", vP, ev)
                if j == 0:
                    W("tensor", aP, last_st_exp)

                def _rt(t, j=j):
                    for ec in range(EC):
                        ins = t.matmul(stp[0][0:128, j * 8:(j + 1) * 8],
                                       xnfm3[:, ec, j * 128:(j + 1) * 128],
                                       rws[:, ec * 8:(ec + 1) * 8],
                                       start=(ec == 0), stop=(ec == EC - 1))
                        if ec == EC - 1:
                            ins.then_inc(tMM.h, 1)

                em("tensor", _rt)
                rt_mm_last = tMM.bump(1)
            slot_last_xn[(gi - 1) % 2] = xn_c[t0g + ng - 1]

    W("vector", tMM, rt_mm_last)
    em("vector", lambda e: e.tensor_copy(lgt[:], stp[0][0:128, 0:128])
       .then_inc(vP.h, 1))
    W("vector", vP, vP.bump(1))
    W("scalar", vP, vP.n)
    W("scalar", aP, aP.n)
    lg2 = sqscr
    em("scalar", lambda e: e.activation(lg2[:, 0:128], lgt[:], AF.Exp,
                                        bias=0.0, scale=1.0).then_inc(aP.h, 1))
    W("vector", aP, aP.bump(1))
    lg3 = lg2[:, 0:128].rearrange("p (a b) -> p a b", a=NT)
    lgm = lgt[:].rearrange("p (a b) -> p a b", a=NT)

    def _ve2(f):
        em("vector", lambda e, f=f: f(e).then_inc(vP.h, 1))
        W("vector", vP, vP.bump(1))

    e1v = rsc[:, 0:16]
    e2v = rsc[:, 16:32]
    s12 = rsc[:, 32:48]
    e1b = e1v.rearrange("p (a b) -> p a b", b=1).broadcast_to([128, NT, 8])
    _ve2(lambda e: e.reduce_max(e1v, lg3, axis=AX.X))
    _ve2(lambda e: e.tensor_tensor(lgm, lg3, e1b, ALU.is_equal))
    _ve2(lambda e: e.scalar_tensor_tensor(lgm, lgm, -1e30, lg3, ALU.mult, ALU.add))
    _ve2(lambda e: e.reduce_max(e2v, lgm, axis=AX.X))
    _ve2(lambda e: e.tensor_tensor(s12, e1v, e2v, ALU.add))
    _ve2(lambda e: e.reciprocal(s12, s12))
    _ve2(lambda e: e.tensor_tensor(e1v, e1v, s12, ALU.mult))
    _ve2(lambda e: e.tensor_tensor(e2v, e2v, s12, ALU.mult))
    _ve2(lambda e: e.tensor_tensor(scals[:], e1v, e2v, ALU.add))

    if stop == "C":
        return _finish(nc, prog, em, W, sems, dbg, stack,
                       dict(aP=aP, vP=vP, tMM=tMM, stO=stO, stS=stS))
    ffn_regions = [pp[:, 0:512], pp[:, 512:1024],
                   stp[1][:, 0:512], stp[1][:, 512:1024]]
    ffn_guard = [("vP", 0), ("vP", 0), ("aP", last_st_exp), ("aP", last_st_exp)]
    ffn_ctr = [0]

    def ffn_acquire():
        q = ffn_ctr[0] % 4
        ffn_ctr[0] += 1
        gs, gv = ffn_guard[q]
        W("tensor", sems[gs], gv)
        return q


    w1_rdy = {}
    w2info = {}
    x2pre = {}
    dw2r = dw2.rearrange("(f p) h -> p f h", p=128)

    def w1ap(fc):
        wh = w1slc if fc % 4 < 2 else w1slc2
        off = (fc % 2) * EC * 128
        return wh[:, off:off + EC * 128]

    def emit_w1_load(th, fc):
        wsem = fc % 2

        def _ldw1(g, fc=fc, wsem=wsem):
            g.dma_start(
                out=w1ap(fc).rearrange("p (e f) -> p e f", e=EC),
                in_=dw1.rearrange("(e p) f -> p e f", p=128)[
                    :, :, fc * 128:(fc + 1) * 128],
            ).then_inc(dw[wsem].h, 16)

        em("gpsimd", _ldw1)
        w1_rdy[(th, fc)] = dw[wsem].bump(16)

    def w2slot(k):
        if k % 3 < 2:
            return wreg, (k % 3) * FC * 256
        return w2s2, 0

    def emit_w2_piece_g(k, pc, tmm_wait=0, vp_wait=0):
        hq = k % 4
        h, off0 = w2slot(k)
        if tmm_wait:
            W("gpsimd", tMM, tmm_wait)
        if vp_wait:
            W("gpsimd", vP, vp_wait)
        dst = h[:, off0 + pc * 1024:off0 + (pc + 1) * 1024].rearrange(
            "p (f x) -> p f x", f=4)
        em("gpsimd", lambda e, dst=dst, hq=hq, pc=pc: e.dma_start(
            out=dst, in_=dw2r[:, pc * 4:(pc + 1) * 4,
                              hq * 256:(hq + 1) * 256]).then_inc(dP.h, 16))
        w2info[k] = (dP, dP.bump(16))

    W("gpsimd", dw[0], dw[0].n)
    W("gpsimd", dw[1], dw[1].n)
    for fc0 in range(4):
        emit_w1_load(0, fc0)

    W("gpsimd", vP, x2_built[NT - 1])
    em("gpsimd", lambda g: g.dma_start(out=biasb[:], in_=db2).then_inc(dC.h, 16))
    B2_RDY = dC.bump(16)
    g3 = gbuf[:].rearrange("p (f t) -> p f t", f=FC)
    gelu_mm = {}
    gelu_done = {}
    ffn2_mm = {}
    out_done = {}
    eserial = [0]

    for th in range(2):
        if th == 0:
            W("sync", stS, stS.n)
            for s in range(2):
                W("sync", dx[s], dx[s].n)
                em("sync", lambda e, s=s: e.dma_start(
                    out=x2r[:, s * 256:(s + 1) * 256],
                    in_=dx2s[s * 128:(s + 1) * 128, 0:256])
                   .then_inc(dx[s].h, 16))
                x2pre[s] = dx[s].bump(16)
        else:
            for s in range(2):
                W("sync", vP, out_done[30 + s])
                W("sync", stS, stS.n)
                W("sync", dx[s], dx[s].n)
                em("sync", lambda e, s=s: e.dma_start(
                    out=x2r[:, s * 256:(s + 1) * 256],
                    in_=dx2s[(8 + s) * 128:(9 + s) * 128, 0:256])
                   .then_inc(dx[s].h, 16))
                x2pre[32 + s] = dx[s].bump(16)
        for fc in range(FC):
            if fc >= 4:
                W("gpsimd", tMM, gelu_mm[(th, fc - 4)])
                emit_w1_load(th, fc)
            if 8 <= fc < 16:
                tw = 0
                if fc == 8:
                    tw = t_wp_last if th == 0 else ffn2_mm[(0, 2, 7)]
                emit_w2_piece_g(0 if th == 0 else 5, fc - 8, tmm_wait=tw)
            elif fc in (24, 26, 28, 30):
                tw = ffn2_mm[(0, 3, 7)] if (th == 1 and fc == 24) else 0
                emit_w2_piece_g(1 if th == 0 else 6, (fc - 24) // 2,
                                tmm_wait=tw)
            for s in range(2):
                par = ffn_acquire()
                W("tensor", dw[fc % 2], w1_rdy[(th, fc)])
                if th == 0 and fc == 0 and s == 0:
                    W("tensor", vP, xn2_evac_last)

                def _f1(t, th=th, fc=fc, s=s, par=par, wap=w1ap(fc)):
                    for ec in range(EC):
                        ins = t.matmul(
                            ffn_regions[par],
                            wap[:, ec * 128:(ec + 1) * 128],
                            xnfm3[:, ec, th * 1024 + s * 512:
                                  th * 1024 + (s + 1) * 512],
                            start=(ec == 0), stop=(ec == EC - 1))
                        if ec == EC - 1:
                            ins.then_inc(tMM.h, 1)

                em("tensor", _f1)
                t_f1 = tMM.bump(1)
                if s == 1:
                    gelu_mm[(th, fc)] = t_f1
                W("scalar", tMM, t_f1)
                if th == 0 and fc == 0 and s == 0:
                    W("scalar", dK, CONSTS2)
                    W("scalar", tMM, attn_mm_end[EC - 1])
                    W("scalar", vP, atr_done[NT - 1])
                    W("scalar", tMM, t_wp_last)
                if th == 1 and fc == 0 and s == 0:
                    W("scalar", tMM, ffn2_mm[(0, 3, 7)])
                o = g3[:, fc, s * 512:(s + 1) * 512]
                em("scalar", lambda e, o=o, p=par, b=b1s[:, fc:fc + 1]:
                   e.activation(o, ffn_regions[p],
                                AF.Gelu_apprx_tanh, bias=b,
                                scale=1.0).then_inc(aP.h, 1))
                ffn_guard[par] = ("aP", aP.bump(1))
        gelu_done[th] = aP.n

        for hq in range(4):
            k4 = th * 4 + hq
            h2, off2 = w2slot(k4)
            if th == 0 and hq == 3:
                for j in range(4):
                    W("gpsimd", tMM, gelu_mm[(0, FC - 4 + j)])
                    emit_w1_load(1, j)
            for ti in range(8):
                if th == 0:
                    if ti < 4:
                        emit_w2_piece_g(hq + 1, 4 + ti)
                    elif hq < 3:
                        tw = ffn2_mm[(0, hq - 1, 7)] if (ti == 4 and hq >= 1) else 0
                        vw = x2_add[NT - 1] if (ti == 4 and hq == 0) else 0
                        emit_w2_piece_g(hq + 2, ti - 4, tmm_wait=tw, vp_wait=vw)
                else:
                    if hq == 0 and ti < 4:
                        emit_w2_piece_g(6, 4 + ti)
                    elif hq == 1:
                        tw = ffn2_mm[(1, 0, 7)] if ti == 0 else 0
                        emit_w2_piece_g(7, ti, tmm_wait=tw)
                absti = th * 8 + ti
                par = ffn_acquire()
                w2s_, w2v_ = w2info[k4]
                W("tensor", w2s_, w2v_)
                W("tensor", aP, gelu_done[th])

                def _f2(t, ti=ti, par=par, h2=h2, off2=off2):
                    for fc in range(FC):
                        ins = t.matmul(
                            ffn_regions[par][:, 0:256],
                            g3[:, fc, ti * 128:(ti + 1) * 128],
                            h2[:, off2 + fc * 256:off2 + (fc + 1) * 256],
                            start=(fc == 0), stop=(fc == FC - 1))
                        if fc == FC - 1:
                            ins.then_inc(tMM.h, 1)

                em("tensor", _f2)
                ffn2_mm[(th, hq, ti)] = tMM.bump(1)
                ser = eserial[0]
                eserial[0] += 1
                xsl = ser % 2
                nser = ser + 1
                if nser % 32 != 0 and nser not in x2pre:
                    lhq, lti = (nser % 32) // 8, nser % 8
                    labs = th * 8 + lti
                    if ser >= 1:
                        W("sync", vP, out_done[ser - 1])
                    W("sync", stS, stS.n)
                    W("sync", dx[nser % 2], dx[nser % 2].n)
                    em("sync", lambda e, labs=labs, lhq=lhq, nxsl=nser % 2:
                       e.dma_start(
                           out=x2r[:, nxsl * 256:(nxsl + 1) * 256],
                           in_=dx2s[labs * 128:(labs + 1) * 128,
                                    lhq * 256:(lhq + 1) * 256])
                       .then_inc(dx[nxsl].h, 16))
                    x2pre[nser] = dx[nser % 2].bump(16)
                dxv = x2pre[ser]
                W("vector", tMM, ffn2_mm[(th, hq, ti)])
                W("vector", dx[xsl], dxv)
                W("vector", dC, B2_RDY)
                osl = ser % 2
                em("vector", lambda e, osl=osl, p=par, hq=hq: e.tensor_tensor(
                    tmpr[:, osl * 256:(osl + 1) * 256],
                    ffn_regions[p][:, 0:256],
                    biasb[:, hq * 256:(hq + 1) * 256],
                    ALU.add).then_inc(vP.h, 1))
                ffn_guard[par] = ("vP", vP.bump(1))
                W("vector", vP, ffn_guard[par][1])
                if ser >= 1:
                    W("vector", stO, 16 * (ser - 1))
                em("vector", lambda e, osl=osl, xsl=xsl, absti=absti:
                   e.scalar_tensor_tensor(
                       outr[:, osl * 256:(osl + 1) * 256],
                       tmpr[:, osl * 256:(osl + 1) * 256],
                       scals[:, absti:absti + 1],
                       x2r[:, xsl * 256:(xsl + 1) * 256],
                       ALU.mult, ALU.add).then_inc(vP.h, 1))
                out_done[ser] = vP.bump(1)
                W("sync", vP, out_done[ser])
                W("sync", stO, stO.n)
                em("sync", lambda e, osl=osl, absti=absti, hq=hq: e.dma_start(
                    out=dy[absti * 128:(absti + 1) * 128, hq * 256:(hq + 1) * 256],
                    in_=outr[:, osl * 256:(osl + 1) * 256]).then_inc(stO.h, 16))
                stO.bump(16)

    W("gpsimd", stO, stO.n)
    W("sync", stO, stO.n)
    if dbg:
        return _finish(nc, prog, em, W, sems, dbg, stack,
                       dict(aP=aP, vP=vP, tMM=tMM, stO=stO, stS=stS))

    return _replay(nc, prog, stack)


def _replay(nc, prog, stack):
    with nc.Block() as blk:
        @blk.sync
        def _(e):
            for f in prog["sync"]:
                f(e)

        @blk.gpsimd
        def _(e):
            for f in prog["gpsimd"]:
                f(e)

        @blk.scalar
        def _(e):
            for f in prog["scalar"]:
                f(e)

        @blk.vector
        def _(e):
            for f in prog["vector"]:
                f(e)

        @blk.tensor
        def _(e):
            for f in prog["tensor"]:
                f(e)
    stack.close()
    return nc


def _finish(nc, prog, em, W, sems, dbg, stack, s):
    aP, vP, tMM, stO = s["aP"], s["vP"], s["tMM"], s["stO"]
    W("sync", aP, aP.n)
    W("sync", vP, vP.n)
    W("sync", tMM, tMM.n)
    ndma = 0
    for name, hh in dbg:
        shp = [hh.shape[0], hh.shape[1]]
        d = nc.dram_tensor("dbg_" + name, shp, hh.dtype, kind="ExternalOutput").ap()
        em("sync", lambda e, d=d, hh=hh: e.dma_start(out=d[:, :], in_=hh[:])
           .then_inc(stO.h, 16))
        ndma += 1
        stO.bump(16)
    W("sync", stO, stO.n)
    return _replay(nc, prog, stack)


def _host_inputs(inputs):
    x = np.asarray(inputs["L0_attn_x"], dtype=np.float32)
    wqT = np.ascontiguousarray(np.asarray(inputs["L0_attn_Wq"], np.float32).T)
    wkT = np.ascontiguousarray(np.asarray(inputs["L0_attn_Wk"], np.float32).T)
    wvT = np.ascontiguousarray(np.asarray(inputs["L0_attn_Wv"], np.float32).T)
    wpT = np.ascontiguousarray(np.asarray(inputs["L0_attn_Wp"], np.float32).T)
    w1 = np.ascontiguousarray(np.asarray(inputs["L0_ffn_w1"], np.float32))
    w2 = np.ascontiguousarray(np.asarray(inputs["L0_ffn_w2"], np.float32))
    rw = np.ascontiguousarray(np.asarray(inputs["L0_ffn_router_w"], np.float32))
    bq = np.asarray(inputs["L0_attn_bq"], np.float32)
    bk = np.asarray(inputs["L0_attn_bk"], np.float32)
    bv = np.asarray(inputs["L0_attn_bv"], np.float32)
    bp = np.asarray(inputs["L0_attn_bp"], np.float32)
    b1 = np.asarray(inputs["L0_ffn_b1"], np.float32)
    b2 = np.asarray(inputs["L0_ffn_b2"], np.float32)

    bqc = np.ascontiguousarray(bq.reshape(EC, 128).T)
    bkc = np.ascontiguousarray(bk.reshape(EC, 128).T)
    b1c = np.ascontiguousarray(b1.reshape(FC, 128).T)
    bvb = np.ascontiguousarray(np.broadcast_to(bv, (128, E)))
    bpb = np.ascontiguousarray(np.broadcast_to(bp, (128, E)))
    b2b = np.ascontiguousarray(np.broadcast_to(b2, (128, E)))
    ident = np.eye(128, dtype=ml_dtypes.bfloat16)

    jr = np.arange(128)[:, None]
    qr = np.arange(384)[None, :]
    interior = ((qr >= jr) & (qr <= jr + 256)).astype(np.float32)

    in_maps = []
    for c in range(8):
        b, half = c // 2, c % 2
        start = half * TOWN
        lo, hi = start - WIN, start + TOWN + WIN
        xin = np.zeros((TKV, E), np.float32)
        vlo, vhi = max(lo, 0), min(hi, N)
        xin[vlo - lo:vhi - lo] = x[b, vlo:vhi]
        v0 = interior if half == 1 else np.zeros_like(interior)
        v2 = interior if half == 0 else np.zeros_like(interior)
        masks = np.stack([v0, interior, v2]).astype(ml_dtypes.bfloat16)
        in_maps.append({
            "x": xin, "wqT": wqT, "wkT": wkT, "wvT": wvT, "wpT": wpT,
            "w1": w1, "w2": w2, "rw": rw,
            "bqc": bqc, "bkc": bkc, "b1c": b1c,
            "bvb": bvb, "bpb": bpb, "b2b": b2b,
            "masks": masks, "ident": ident,
        })
    return in_maps


def kernel(**inputs):
    in_maps = _host_inputs(inputs)
    nc = build_nc()
    res = run_bass_kernel_spmd(nc, in_maps, core_ids=list(range(8)))
    y = np.empty((B, N, E), np.float32)
    for c in range(8):
        b, half = c // 2, c % 2
        y[b, half * TOWN:(half + 1) * TOWN] = res.results[c]["y"]
    return y


# version 34 (final)
# speedup vs baseline: 1.3752x; 1.0023x over previous
import contextlib

import numpy as np
import ml_dtypes
import concourse.bass as bass
import concourse.mybir as mybir
from concourse.bass_utils import run_bass_kernel_spmd

F32 = mybir.dt.float32
BF16 = mybir.dt.bfloat16
AF = mybir.ActivationFunctionType
ALU = mybir.AluOpType
AX = mybir.AxisListType

B, N, E, NH, HD, FF = 4, 4096, 1024, 16, 64, 4096
WIN = 128
TOWN = 2048
TKV = 2304
NT = TOWN // 128
NKV = TKV // 128
EC = E // 128
FC = FF // 128
EPS = 1e-5
SCL = 1.0 / 32.0
NPT = 4


def _qrange(jt):
    return max(0, jt * 128 - 256), min(TOWN, jt * 128 + 128)


def _mask_blocks(jt):
    q0, q1 = _qrange(jt)
    off = q0 - (jt * 128 - 256)
    if jt == 0:
        return [(0, q1 - q0, 0, off)]
    if jt == NKV - 1:
        return [(0, q1 - q0, 2, off)]
    out = []
    for qr_lo, qr_hi in ((0, 128), (256, 384)):
        lo = max(q0, jt * 128 - 256 + qr_lo)
        hi = min(q1, jt * 128 - 256 + qr_hi)
        if lo < hi:
            out.append((lo - q0, hi - q0, 1, lo - q0 + off))
    return out


class SemC:
    def __init__(self, h):
        self.h = h
        self.n = 0

    def bump(self, k):
        self.n += k
        return self.n


def build_nc(stop="full", dbg=()):
    nc = bass.Bass("TRN2", target_bir_lowering=False)

    dx_in = nc.dram_tensor("x", [TKV, E], F32, kind="ExternalInput").ap()
    dwq = nc.dram_tensor("wqT", [E, E], F32, kind="ExternalInput").ap()
    dwk = nc.dram_tensor("wkT", [E, E], F32, kind="ExternalInput").ap()
    dwv = nc.dram_tensor("wvT", [E, E], F32, kind="ExternalInput").ap()
    dwp = nc.dram_tensor("wpT", [E, E], F32, kind="ExternalInput").ap()
    dw1 = nc.dram_tensor("w1", [E, FF], F32, kind="ExternalInput").ap()
    dw2 = nc.dram_tensor("w2", [FF, E], F32, kind="ExternalInput").ap()
    drw = nc.dram_tensor("rw", [E, 8], F32, kind="ExternalInput").ap()
    dbq = nc.dram_tensor("bqc", [128, EC], F32, kind="ExternalInput").ap()
    dbk = nc.dram_tensor("bkc", [128, EC], F32, kind="ExternalInput").ap()
    db1 = nc.dram_tensor("b1c", [128, FC], F32, kind="ExternalInput").ap()
    dbv = nc.dram_tensor("bvb", [128, E], F32, kind="ExternalInput").ap()
    dbp = nc.dram_tensor("bpb", [128, E], F32, kind="ExternalInput").ap()
    db2 = nc.dram_tensor("b2b", [128, E], F32, kind="ExternalInput").ap()
    dmask = nc.dram_tensor("masks", [3, 128, 384], BF16, kind="ExternalInput").ap()
    dident = nc.dram_tensor("ident", [128, 128], BF16, kind="ExternalInput").ap()
    dy = nc.dram_tensor("y", [TOWN, E], F32, kind="ExternalOutput").ap()
    dx2s = nc.dram_tensor("x2s", [TOWN, E], F32, kind="Internal").ap()

    A = nc.alloc_sbuf_tensor
    xr = A("xr", [128, 2 * 1024], F32)
    x2t = A("x2t", [128, 2 * 1024], F32)
    x2r = A("x2r", [128, 2 * 256], F32)
    outr = A("outr", [128, 2 * 256], F32)
    tmpr = A("tmpr", [128, 2 * 256], F32)
    xnscr = A("xnscr", [128, 2 * 1024], BF16)
    sqscr = A("sqscr", [128, 1024], F32)
    xnfm = A("xnfm", [128, EC * TKV], BF16)
    lns = A("lns", [128, 8 * 12], F32)
    epsc = A("epsc", [128, 1], F32)
    biasb = A("biasb", [128, 1024], BF16)
    bqs = A("bqs", [128, EC], F32)
    bks = A("bks", [128, EC], F32)
    b1s = A("b1s", [128, FC], F32)
    rws = A("rws", [128, EC * 8], BF16)
    masksb = A("masksb", [128, 3 * 384], BF16)
    identb = A("identb", [128, 128], BF16)
    ptr = A("ptr", [128, NPT * 768], BF16)
    rz = A("rz", [128, 4], F32)
    scals = A("scals", [128, NT], F32)
    lgt = A("lgt", [128, 128], F32)
    rsc = A("rsc", [128, 64], F32)
    wreg = A("wreg", [128, 16 * 1024], BF16)
    w1slc = A("w1slc", [128, 2 * EC * 128], BF16)

    vbytes = NKV * 1040 * 2
    qbytes = 2 * TOWN * 2
    kbytes = 2 * TKV * 2
    wqkbytes = 4 * 1024 * 2
    atmbytes = NT * 1024 * 2
    arena_b = max(vbytes + qbytes + kbytes + wqkbytes + atmbytes,
                  FC * 1024 * 2)
    arena = A("arena", [128, arena_b], mybir.dt.uint8)
    a_addr = nc.lookup_mloc(arena).addr
    AT = nc.alloc_sbuf_tensor_at
    off = a_addr
    vp = AT("vp", [128, NKV * 1040], BF16, offset=off); off += vbytes
    slot0_off = off
    qb0 = AT("qb0", [128, TOWN], BF16, offset=off); off += TOWN * 2
    kb0 = AT("kb0", [128, TKV], BF16, offset=off); off += TKV * 2
    qb1 = AT("qb1", [128, TOWN], BF16, offset=off); off += TOWN * 2
    kb1 = AT("kb1", [128, TKV], BF16, offset=off); off += TKV * 2
    wqks_off = off
    wqks = AT("wqks", [128, 4 * 1024], BF16, offset=off); off += wqkbytes
    atm = AT("atm", [128, NT * 1024], BF16, offset=off)
    gbuf = AT("gbuf", [128, FC * 1024], BF16, offset=a_addr)
    xpre = AT("xpre", [128, 4 * 1024], F32, offset=a_addr + FC * 1024 * 2)
    w2s2 = AT("w2s2", [128, FC * 256], BF16, offset=a_addr + FC * 1024 * 2)
    w1slc2 = AT("w1slc2", [128, 2 * EC * 128], BF16,
                offset=a_addr + FC * 1024 * 2 + FC * 256 * 2)
    qbs, kbs = [qb0, qb1], [kb0, kb1]
    wpts0 = AT("wpts0", [128, 4 * 1024], BF16, offset=slot0_off)
    wpts1 = AT("wpts1", [128, 4 * 1024], BF16, offset=wqks_off)
    wptsv = [wpts0, wpts1]
    w_addr = nc.lookup_mloc(wreg).addr
    wvslc = AT("wvslc", [128, 2 * EC * 512], BF16, offset=w_addr)

    stp = [nc.alloc_psum_tensor("st0", [128, 1024], F32),
           nc.alloc_psum_tensor("st1", [128, 1024], F32)]
    pp = nc.alloc_psum_tensor("pp", [128, 1024], F32)
    av = nc.alloc_psum_tensor("av", [128, 1024], F32)

    nc._dbg_handles = dict(xnfm=xnfm, vp=vp, qb=qb0, kb=kb0, atm=atm,
                           gbuf=gbuf, wpts=wpts0, wreg=wreg, ptr=ptr,
                           scals=scals, x2t=x2t, xnscr=xnscr,
                           biasb=biasb, wqks=wqks, masksb=masksb, rsc=rsc,
                           lgt=lgt, wvslc=wvslc, w1slc=w1slc, rz=rz)
    if dbg:
        dbg = [(n, nc._dbg_handles[n]) for n in dbg]
    prog = {k: [] for k in ("sync", "gpsimd", "scalar", "vector", "tensor")}

    def em(eng, f):
        prog[eng].append(f)

    stack = contextlib.ExitStack()
    sems = {}
    for name in ("dxA", "dxB", "dwA", "dwB", "dC", "dK", "stS", "stO",
                 "dP", "dQ", "tMM", "aP", "vP"):
        sems[name] = SemC(stack.enter_context(nc.semaphore(name)))
    dx = [sems["dxA"], sems["dxB"]]
    dw = [sems["dwA"], sems["dwB"]]
    dC, stS, stO = sems["dC"], sems["stS"], sems["stO"]
    dK = sems["dK"]
    dP, dQ = sems["dP"], sems["dQ"]
    tMM, aP, vP = sems["tMM"], sems["aP"], sems["vP"]

    def W(eng, s, v):
        if v > 0:
            em(eng, lambda e, _h=s.h, _v=v: e.wait_ge(_h, _v))

    xnfm3 = xnfm[:].rearrange("p (c t) -> p c t", c=EC)

    def _consts_g(g):
        g.dma_start(out=masksb[:].rearrange("p (v q) -> p v q", v=3),
                    in_=dmask.rearrange("v p q -> p v q")).then_inc(dC.h, 16)
        g.dma_start(out=identb[:], in_=dident).then_inc(dC.h, 16)
        g.dma_start(out=biasb[:], in_=dbv).then_inc(dC.h, 16)
        g.dma_start(out=rws[:].rearrange("p (c x) -> p c x", c=EC),
                    in_=drw.rearrange("(c p) x -> p c x", p=128)).then_inc(dC.h, 16)
        for fh in range(2):
            g.dma_start(
                out=wvslc[:, fh * EC * 512:(fh + 1) * EC * 512]
                    .rearrange("p (c f) -> p c f", c=EC),
                in_=dwv.rearrange("(c p) f -> p c f", p=128)[:, :, fh * 512:(fh + 1) * 512],
            ).then_inc(dC.h, 16)
        for d_ap, col in ((dwq, 0), (dwk, 2 * 1024)):
            g.dma_start(
                out=wqks[:, col:col + 1024].rearrange("p (c f) -> p c f", c=EC),
                in_=d_ap.rearrange("(c p) f -> p c f", p=128)[:, :, 0:128],
            ).then_inc(dC.h, 16)

    em("gpsimd", _consts_g)
    CONSTS = dC.bump(16 * 8)

    def _consts_s(s):
        s.dma_start(out=bqs[:], in_=dbq).then_inc(dK.h, 16)
        s.dma_start(out=bks[:], in_=dbk).then_inc(dK.h, 16)
        s.dma_start(out=b1s[:], in_=db1).then_inc(dK.h, 16)

    em("sync", _consts_s)
    CONSTS2 = dK.bump(16 * 3)


    def _vones(v):
        ap = vp[:].rearrange("p (t h o) -> p t h o", t=NKV, h=NH)[:, :, :, 64:65]
        v.memset(ap, 1.0)
        v.memset(epsc[:], EPS)

    em("vector", _vones)

    LNG = 6
    lnstate = {"rdy": 0}

    def _lncol(f, slot, k, n=1):
        base = f * 12 + slot * LNG + k
        return lns[:, base:base + n]

    def emit_stats(src_ap, slot, k):
        if lnstate["rdy"]:
            W("scalar", aP, lnstate["rdy"])
        em("scalar", lambda e, a=src_ap, o=_lncol(1, slot, k): e.activation(
            sqscr[:], a, AF.Square, accum_out=o).then_inc(aP.h, 1))
        a_sq = aP.bump(1)
        W("scalar", aP, a_sq)
        em("scalar", lambda e, a=src_ap, o=_lncol(0, slot, k): e.activation(
            sqscr[:], a, AF.Identity, accum_out=o).then_inc(aP.h, 1))
        a_st = aP.bump(1)
        lnstate["rdy"] = a_st
        return [(aP, a_st)]

    def emit_ladder(slot, n, war_a=0):
        R = _lncol(0, slot, 0, n)
        S = _lncol(1, slot, 0, n)
        MT = _lncol(2, slot, 0, n)
        MR = _lncol(3, slot, 0, n)
        V_ = _lncol(4, slot, 0, n)
        L_ = _lncol(5, slot, 0, n)
        RS = _lncol(6, slot, 0, n)
        NB = _lncol(7, slot, 0, n)
        W("vector", aP, lnstate["rdy"])
        if war_a:
            W("vector", aP, war_a)
        em("vector", lambda e: e.tensor_tensor(MT, R, R, ALU.mult).then_inc(vP.h, 1))
        W("vector", vP, vP.bump(1))
        em("vector", lambda e: e.tensor_scalar(
            MR, MT, 1.0 / (E * E), -EPS, ALU.mult, ALU.add).then_inc(vP.h, 1))
        W("vector", vP, vP.bump(1))
        em("vector", lambda e: e.scalar_tensor_tensor(
            V_, S, 1.0 / E, MR, ALU.mult, ALU.subtract).then_inc(vP.h, 1))
        e5 = vP.bump(1)
        W("scalar", vP, e5)
        em("scalar", lambda e: e.activation(
            L_, V_, AF.Ln, bias=epsc[:], scale=1.0).then_inc(aP.h, 1))
        a_ln = aP.bump(1)
        W("scalar", aP, a_ln)
        em("scalar", lambda e: e.activation(
            RS, L_, AF.Exp, bias=0.0, scale=-0.5).then_inc(aP.h, 1))
        a_rstd = aP.bump(1)
        W("vector", aP, a_rstd)
        em("vector", lambda e: e.tensor_tensor(NB, R, RS, ALU.mult).then_inc(vP.h, 1))
        W("vector", vP, vP.bump(1))
        em("vector", lambda e: e.tensor_single_scalar(
            NB, NB, -1.0 / E, ALU.mult).then_inc(vP.h, 1))
        return vP.bump(1)

    ln = {"ttr": {}}

    def emit_xn(src_ap, dst_slot, slot, k, tile_id):
        if tile_id - 2 in ln["ttr"]:
            W("scalar", tMM, ln["ttr"][tile_id - 2])
        dsta = xnscr[:, dst_slot * 1024:(dst_slot + 1) * 1024]
        em("scalar", lambda e, a=src_ap, o=dsta, r=_lncol(6, slot, k),
                     b=_lncol(7, slot, k): e.activation(
            o, a, AF.Identity, bias=b, scale=r).then_inc(aP.h, 1))
        return aP.bump(1)

    tr = {"par": 0, "lastv": [0, 0], "first": True}

    def emit_transpose(src_ap, dst3_ap, wait_a, extra_t_wait=0):
        par = tr["par"]
        tr["par"] ^= 1
        avb = av[:, par * 512:(par + 1) * 512].bitcast(BF16)
        if tr["first"]:
            W("tensor", dC, CONSTS)
            tr["first"] = False
        W("tensor", aP, wait_a)
        if extra_t_wait:
            W("tensor", aP, extra_t_wait)
        W("tensor", vP, tr["lastv"][par])

        def _tr(t, s=src_ap, o=avb):
            for c in range(8):
                ins = t.transpose(o[:, c * 128:(c + 1) * 128],
                                  s[:, c * 128:(c + 1) * 128], identb[:])
                if c == 7:
                    ins.then_inc(tMM.h, 1)

        em("tensor", _tr)
        t_tr = tMM.bump(1)
        W("vector", tMM, t_tr)
        em("vector", lambda e, i0=avb.rearrange("p (c t) -> p c t", c=8),
                     o=dst3_ap: e.tensor_copy(o, i0).then_inc(vP.h, 1))
        tr["lastv"][par] = vP.bump(1)
        return t_tr, tr["lastv"][par]

    ppctr = [0]
    pp_guard = [("vP", 0), ("vP", 0)]

    def pp_acquire():
        par = ppctr[0] % 2
        ppctr[0] += 1
        gs, gv = pp_guard[par]
        W("tensor", sems[gs], gv)
        return par

    xload = [0]
    xcons = {}

    def emit_xload(src_ap2, extra=()):
        i = xload[0]
        xload[0] += 1
        sl = i % 2
        for s, v in xcons.get(i - 2, []):
            W("sync", s, v)
        for s, v in extra:
            W("sync", s, v)
        W("sync", dx[sl], dx[sl].n)
        dst = xr[:, sl * 1024:(sl + 1) * 1024]
        em("sync", lambda e, s=src_ap2, d=dst: e.dma_start(out=d, in_=s)
           .then_inc(dx[sl].h, 16))
        return i, sl, dx[sl].bump(16)

    xn_a = {}
    tV_groups = []
    agroups = [(0, LNG), (LNG, LNG), (2 * LNG, NKV - 2 * LNG)]
    ladder_rdy = {}
    xn_group_last = {}
    for gi in range(len(agroups) + 1):
        if gi < len(agroups):
            t0g, ng = agroups[gi]
            for k in range(ng):
                i = t0g + k
                ser, sl, dxv = emit_xload(dx_in[i * 128:(i + 1) * 128, :])
                W("scalar", dx[sl], dxv)
                xcons[ser] = emit_stats(xr[:, sl * 1024:(sl + 1) * 1024], gi % 2, k)
            ladder_rdy[gi] = emit_ladder(gi % 2, ng,
                                         war_a=xn_group_last.get(gi - 2, 0))
        if gi >= 1:
            t0g, ng = agroups[gi - 1]
            W("scalar", vP, ladder_rdy[gi - 1])
            rl = {0: emit_xload(dx_in[t0g * 128:(t0g + 1) * 128, :])}
            for k in range(ng):
                i = t0g + k
                if k + 1 < ng:
                    rl[k + 1] = emit_xload(
                        dx_in[(i + 1) * 128:(i + 2) * 128, :])
                ser, sl, dxv = rl[k]
                W("scalar", dx[sl], dxv)
                xn_a[i] = emit_xn(xr[:, sl * 1024:(sl + 1) * 1024], sl,
                                  (gi - 1) % 2, k, i)
                xcons[ser] = [(aP, xn_a[i])]
                t_tr, vtr_j = emit_transpose(
                    xnscr[:, sl * 1024:(sl + 1) * 1024],
                    xnfm3[:, :, i * 128:(i + 1) * 128], xn_a[i])
                ln["ttr"][i] = t_tr
                W("tensor", dC, CONSTS)
                W("tensor", vP, vtr_j)
                for fh in range(2):
                    par = pp_acquire()

                    def _vproj(t, i=i, fh=fh, par=par):
                        for ec in range(EC):
                            ins = t.matmul(
                                pp[:, par * 512:(par + 1) * 512],
                                xnfm3[:, ec, i * 128:(i + 1) * 128],
                                wvslc[:, (fh * EC + ec) * 512:(fh * EC + ec + 1) * 512],
                                start=(ec == 0), stop=(ec == EC - 1))
                            if ec == EC - 1:
                                ins.then_inc(tMM.h, 1)

                    em("tensor", _vproj)
                    t_v = tMM.bump(1)
                    tV_groups.append(t_v)
                    W("vector", tMM, t_v)
                    W("vector", dC, CONSTS)
                    dst = vp[:].rearrange("p (t h o) -> p t h o", t=NKV, h=NH)[
                        :, i, fh * 8:(fh + 1) * 8, 0:64]
                    srcp = pp[:, par * 512:(par + 1) * 512].rearrange(
                        "p (h d) -> p h d", h=8)
                    bsl = biasb[:, fh * 512:(fh + 1) * 512].rearrange(
                        "p (h d) -> p h d", h=8)
                    em("vector", lambda e, o=dst, i0=srcp, i1=bsl: e.tensor_tensor(
                        o, i0, i1, ALU.add).then_inc(vP.h, 1))
                    pp_guard[par] = ("vP", vP.bump(1))
            xn_group_last[gi - 1] = xn_a[t0g + ng - 1]

    if stop == "A":
        return _finish(nc, prog, em, W, sems, dbg, stack,
                       dict(aP=aP, vP=vP, tMM=tMM, stO=stO, stS=stS))

    qk_w_ready = {0: CONSTS}
    qk_evac_end = {}
    attn_mm_end = {}
    av_evac = {}
    avmm = {}
    qkproj_mm_end = 0
    last_st_exp = 0

    def load_qk_slices(c, wslot):
        W("gpsimd", dw[wslot], dw[wslot].n)

        def _ld(g, c=c, wslot=wslot):
            for d_ap, col in ((dwq, wslot * 1024), (dwk, (2 + wslot) * 1024)):
                g.dma_start(
                    out=wqks[:, col:col + 1024].rearrange("p (e f) -> p e f", e=EC),
                    in_=d_ap.rearrange("(e p) f -> p e f", p=128)[:, :, c * 128:(c + 1) * 128],
                ).then_inc(dw[wslot].h, 16)
        em("gpsimd", _ld)
        return dw[wslot].bump(32)

    qkproj_mm_end_m = [0]

    for it in range(EC + 1):
        c, ca = it, it - 1
        if c < EC and c + 1 < EC:
            if c - 1 >= 0:
                W("gpsimd", tMM, qkproj_mm_end_m[0])
            qk_w_ready[c + 1] = load_qk_slices(c + 1, (c + 1) % 2)

        proj_jobs = []
        if c < EC:
            for kind in ("q", "k"):
                tlen = TOWN if kind == "q" else TKV
                for s in range((tlen + 511) // 512):
                    proj_jobs.append((kind, s * 512, min(512, tlen - s * 512)))
        pj = [0]
        first_evac = [True]

        def emit_proj_group(c=c, proj_jobs=proj_jobs, pj=pj, first_evac=first_evac):
            if pj[0] >= len(proj_jobs):
                return
            kind, lo, ncol = proj_jobs[pj[0]]
            pj[0] += 1
            par = pp_acquire()
            if pj[0] == 1:
                if c == 0:
                    W("tensor", dC, CONSTS)
                else:
                    W("tensor", dw[c % 2], qk_w_ready[c])

            def _proj(t, kind=kind, c=c, lo=lo, ncol=ncol, par=par):
                wcol = (0 if kind == "q" else 2 * 1024) + (c % 2) * 1024
                src_lo = (128 + lo) if kind == "q" else lo
                for ec in range(EC):
                    ins = t.matmul(
                        pp[:, par * 512:par * 512 + ncol],
                        wqks[:, wcol + ec * 128:wcol + (ec + 1) * 128],
                        xnfm3[:, ec, src_lo:src_lo + ncol],
                        start=(ec == 0), stop=(ec == EC - 1))
                    if ec == EC - 1:
                        ins.then_inc(tMM.h, 1)

            em("tensor", _proj)
            t_g = tMM.bump(1)
            qkproj_mm_end_m[0] = t_g
            if first_evac[0]:
                if c == 0:
                    W("scalar", dK, CONSTS2)
                if c - 2 >= 0:
                    W("scalar", tMM, attn_mm_end[c - 2])
                first_evac[0] = False
            W("scalar", tMM, t_g)
            dstb = qbs[c % 2] if kind == "q" else kbs[c % 2]
            bias = (bqs if kind == "q" else bks)[:, c:c + 1]
            dsta = dstb[:, lo:lo + ncol]
            em("scalar", lambda e, o=dsta, p=par, ncol=ncol, b=bias:
               e.activation(o, pp[:, p * 512:p * 512 + ncol], AF.Identity,
                            bias=b, scale=1.0).then_inc(aP.h, 1))
            pp_guard[par] = ("aP", aP.bump(1))

        if ca < 0:
            while pj[0] < len(proj_jobs):
                emit_proj_group()
            qk_evac_end[c] = aP.n
            continue

        prev_exp_end = last_st_exp
        st_exp = {}
        mask_done = {}
        qba, kba = qbs[ca % 2], kbs[ca % 2]
        first = True
        for jt in range(NKV + 1):
            if jt < NKV:
                q0, q1 = _qrange(jt)
                nq = q1 - q0
                stpar = jt % 2
                if first:
                    W("tensor", aP, qk_evac_end[ca])
                    first = False
                if jt - 2 in st_exp:
                    W("tensor", aP, st_exp[jt - 2])
                elif ca >= 1:
                    W("tensor", aP, prev_exp_end)

                def _st(t, jt=jt, q0=q0, q1=q1, nq=nq, stpar=stpar, kba=kba, qba=qba):
                    for hi in range(2):
                        ins = t.matmul(
                            stp[stpar][:, hi * 512:hi * 512 + nq],
                            kba[hi * 64:hi * 64 + 64, jt * 128:(jt + 1) * 128],
                            qba[hi * 64:hi * 64 + 64, q0:q1],
                            start=True, stop=True)
                        if hi == 1:
                            ins.then_inc(tMM.h, 1)

                em("tensor", _st)
                t_st = tMM.bump(1)
                slot = jt % NPT
                W("scalar", tMM, t_st)
                if jt - NPT >= 0 and (ca, jt - NPT) in avmm:
                    W("scalar", tMM, avmm[(ca, jt - NPT)])
                elif jt < NPT and ca >= 1:
                    W("scalar", tMM, attn_mm_end[ca - 1])
                srce = stp[stpar][:].rearrange("p (h q) -> p h q", h=2)[:, :, 0:nq]
                dpt = ptr[:, slot * 768:(slot + 1) * 768].rearrange(
                    "p (h q) -> p h q", h=2)[:, :, 0:nq]
                em("scalar", lambda e, o=dpt, i0=srce: e.activation(
                    o, i0, AF.Exp, bias=0.0, scale=SCL).then_inc(aP.h, 1))
                st_exp[jt] = aP.bump(1)
                last_st_exp = st_exp[jt]
                W("vector", aP, st_exp[jt])
                W("vector", dC, CONSTS)
                nm = 0
                for lo, hi_, var, qr in _mask_blocks(jt):
                    for hh in range(2):
                        o = ptr[:, slot * 768 + hh * 384 + lo:
                                slot * 768 + hh * 384 + hi_]
                        m = masksb[:, var * 384 + qr:var * 384 + qr + hi_ - lo]
                        em("vector", lambda e, o=o, m=m: e.tensor_tensor(
                            o, o, m, ALU.mult).then_inc(vP.h, 1))
                        nm += 1
                mask_done[jt] = vP.bump(nm)
            if jt % 2 == 1:
                emit_proj_group()
            qi = jt - 3
            if 0 <= qi < NT:
                avpar = qi % 2
                if (ca, qi - 2) in av_evac:
                    W("tensor", aP, av_evac[(ca, qi - 2)])
                elif qi < 2 and ca >= 1:
                    W("tensor", aP, av_evac[(ca - 1, NT - 1)])
                W("tensor", vP, mask_done[qi + 2])

                def _av(t, qi=qi, avpar=avpar, ca=ca):
                    for hi in range(2):
                        habs = 2 * ca + hi
                        for c3 in range(3):
                            js = qi + c3
                            sl2 = js % NPT
                            o2 = qi * 128 - _qrange(js)[0]
                            ins = t.matmul(
                                av[:, avpar * 512 + hi * 65:
                                   avpar * 512 + hi * 65 + 65],
                                ptr[:, sl2 * 768 + hi * 384 + o2:
                                    sl2 * 768 + hi * 384 + o2 + 128],
                                vp[:, js * 1040 + habs * 65:
                                   js * 1040 + habs * 65 + 65],
                                start=(c3 == 0), stop=(c3 == 2))
                            if hi == 1 and c3 == 2:
                                ins.then_inc(tMM.h, 1)

                em("tensor", _av)
                avmm[(ca, qi)] = tMM.bump(1)
                W("vector", tMM, avmm[(ca, qi)])
                if (ca, qi - 2) in av_evac:
                    W("vector", aP, av_evac[(ca, qi - 2)])
                elif qi < 2 and ca >= 1:
                    W("vector", aP, av_evac[(ca - 1, NT - 1)])
                rzs = 2 * (qi % 2)

                def _rz(e, avpar=avpar, rzs=rzs):
                    src = av[:, avpar * 512:avpar * 512 + 130].rearrange(
                        "p (h o) -> p h o", h=2)[:, :, 64:65]
                    e.reciprocal(
                        rz[:, rzs:rzs + 2].rearrange("p (h o) -> p h o", h=2),
                        src).then_inc(vP.h, 1)

                em("vector", _rz)
                v_rz = vP.bump(1)
                W("scalar", vP, v_rz)
                for hi in range(2):
                    habs = 2 * ca + hi
                    o = atm[:, qi * 1024 + habs * 64:qi * 1024 + habs * 64 + 64]
                    i0 = av[:, avpar * 512 + hi * 65:avpar * 512 + hi * 65 + 64]
                    em("scalar", lambda e, o=o, i0=i0, hi=hi, rzs=rzs:
                       e.activation(o, i0, AF.Identity, bias=0.0,
                                    scale=rz[:, rzs + hi:rzs + hi + 1])
                       .then_inc(aP.h, 1))
                av_evac[(ca, qi)] = aP.bump(2)
        while pj[0] < len(proj_jobs):
            emit_proj_group()
        if c < EC:
            qk_evac_end[c] = aP.n
        attn_mm_end[ca] = tMM.n

    if stop == "B":
        return _finish(nc, prog, em, W, sems, dbg, stack,
                       dict(aP=aP, vP=vP, tMM=tMM, stO=stO, stS=stS))
    W("gpsimd", vP, vP.n)
    em("gpsimd", lambda g: g.dma_start(out=biasb[:], in_=dbp).then_inc(dC.h, 16))
    BP_RDY = dC.bump(16)
    W("gpsimd", tMM, attn_mm_end[EC - 2])
    W("gpsimd", aP, qk_evac_end[EC - 1])

    def _ldwp0(g):
        g.dma_start(out=wpts0[:].rearrange("p (e f) -> p e f", e=4),
                    in_=dwp.rearrange("(e p) f -> p e f", p=128)[:, 0:4, :]
                    ).then_inc(dw[0].h, 16)

    em("gpsimd", _ldwp0)
    W("gpsimd", tMM, qkproj_mm_end_m[0])

    def _ldwp1(g):
        g.dma_start(out=wpts1[:].rearrange("p (e f) -> p e f", e=4),
                    in_=dwp.rearrange("(e p) f -> p e f", p=128)[:, 4:8, :]
                    ).then_inc(dw[0].h, 16)

    em("gpsimd", _ldwp1)
    WPT_RDY = dw[0].bump(32)

    afm3 = wreg[:, 0:EC * 2048].rearrange("p (e t) -> p e t", e=EC)
    atr_done = {}
    for qi in range(NT):
        wait_a = av_evac[(EC - 1, NT - 1)] if qi < 2 else av_evac[(EC - 1, qi)]
        t_atr, atr_done[qi] = emit_transpose(
            atm[:, qi * 1024:(qi + 1) * 1024],
            afm3[:, :, qi * 128:(qi + 1) * 128], wait_a)
        if qi == 9:
            atr_t9 = t_atr

    xpre_rdy = {}
    W("sync", tMM, atr_t9)
    for s in range(2):
        W("sync", dQ, dQ.n)
        em("sync", lambda e, s=s: e.dma_start(
            out=xpre[:, s * 1024:(s + 1) * 1024],
            in_=dx_in[128 + s * 128:256 + s * 128, :]).then_inc(dQ.h, 16))
        xpre_rdy[s] = (dQ, dQ.bump(16))

    xn_c = {}
    x2_built = {}
    x2_wrote = {}
    x2_add = {}
    x2cons = {}
    t_wp_last = 0
    xn2_evac_last = 0
    cladder = {}
    slot_last_xn = {0: xn_group_last.get(2, 0), 1: xn_group_last.get(1, 0)}
    cgroups = [(0, LNG), (LNG, LNG), (2 * LNG, NT - 2 * LNG)]
    for gi in range(len(cgroups) + 1):
        if gi < len(cgroups):
            t0g, ng = cgroups[gi]
            for k in range(ng):
                ti = t0g + k
                sl = ti % 2
                for fh in range(2):
                    par = pp_acquire()
                    W("tensor", vP, atr_done[ti])
                    W("tensor", dw[0], WPT_RDY)

                    def _wp(t, ti=ti, fh=fh, par=par):
                        for ec in range(EC):
                            ins = t.matmul(
                                pp[:, par * 512:(par + 1) * 512],
                                afm3[:, ec, ti * 128:(ti + 1) * 128],
                                wptsv[ec // 4][:, (ec % 4) * 1024 + fh * 512:
                                           (ec % 4) * 1024 + (fh + 1) * 512],
                                start=(ec == 0), stop=(ec == EC - 1))
                            if ec == EC - 1:
                                ins.then_inc(tMM.h, 1)

                    em("tensor", _wp)
                    t_wp_last = tMM.bump(1)
                    W("vector", tMM, t_wp_last)
                    W("vector", dC, BP_RDY)
                    for s, v in x2cons.get(ti - 2, []):
                        W("vector", s, v)
                    if ti - 2 >= 0:
                        W("vector", stS, stS.n)
                    o = x2t[:, sl * 1024 + fh * 512:sl * 1024 + (fh + 1) * 512]
                    em("vector", lambda e, o=o, p=par,
                                 i1=biasb[:, fh * 512:(fh + 1) * 512]:
                       e.tensor_tensor(o, pp[:, p * 512:(p + 1) * 512], i1,
                                       ALU.add).then_inc(vP.h, 1))
                    pp_guard[par] = ("vP", vP.bump(1))
                x2_built[ti] = vP.n
                W("vector", vP, x2_built[ti])
                ps, pv = xpre_rdy[ti]
                W("vector", ps, pv)
                em("vector", lambda e, ti=ti, sl=sl: e.tensor_tensor(
                    x2t[:, sl * 1024:(sl + 1) * 1024],
                    x2t[:, sl * 1024:(sl + 1) * 1024],
                    xpre[:, (ti % 4) * 1024:((ti % 4) + 1) * 1024],
                    ALU.add).then_inc(vP.h, 1))
                x2_add[ti] = vP.bump(1)
                W("gpsimd", vP, x2_add[ti])
                W("gpsimd", stS, stS.n)
                em("gpsimd", lambda e, ti=ti, sl=sl: e.dma_start(
                    out=dx2s[ti * 128:(ti + 1) * 128, :],
                    in_=x2t[:, sl * 1024:(sl + 1) * 1024]).then_inc(stS.h, 16))
                x2_wrote[ti] = stS.bump(16)
                tj = ti + 2
                if tj <= NT - 1:
                    if tj - 4 >= 0:
                        W("gpsimd", vP, x2_add[tj - 4])
                    else:
                        W("gpsimd", tMM, atr_t9)
                    W("gpsimd", dP, dP.n)
                    em("gpsimd", lambda e, tj=tj: e.dma_start(
                        out=xpre[:, (tj % 4) * 1024:((tj % 4) + 1) * 1024],
                        in_=dx_in[128 + tj * 128:256 + tj * 128, :])
                       .then_inc(dP.h, 16))
                    xpre_rdy[tj] = (dP, dP.bump(16))
                W("scalar", vP, x2_add[ti])
                x2cons[ti] = emit_stats(
                    x2t[:, sl * 1024:(sl + 1) * 1024], gi % 2, k)

            cladder[gi] = emit_ladder(gi % 2, ng, war_a=slot_last_xn[gi % 2])
        if gi >= 1:
            t0g, ng = cgroups[gi - 1]
            W("scalar", vP, cladder[gi - 1])
            jj0 = t0g
            rl = {0: emit_xload(dx2s[jj0 * 128:(jj0 + 1) * 128, :],
                                extra=[(stS, stS.n)])}
            for k in range(ng):
                j = t0g + k
                if k + 1 < ng:
                    jj = j + 1
                    rl[k + 1] = emit_xload(dx2s[jj * 128:(jj + 1) * 128, :],
                                           extra=[(stS, stS.n)])
                ser, sl, dxv = rl[k]
                W("scalar", dx[sl], dxv)
                xn_c[j] = emit_xn(xr[:, sl * 1024:(sl + 1) * 1024], sl,
                                  (gi - 1) % 2, k, NKV + j)
                xcons[ser] = [(aP, xn_c[j])]
                t_tr, ev = emit_transpose(
                    xnscr[:, sl * 1024:(sl + 1) * 1024],
                    xnfm3[:, :, j * 128:(j + 1) * 128], xn_c[j])
                ln["ttr"][NKV + j] = t_tr
                xn2_evac_last = ev
                W("tensor", dC, CONSTS)
                W("tensor", vP, ev)
                if j == 0:
                    W("tensor", aP, last_st_exp)

                def _rt(t, j=j):
                    for ec in range(EC):
                        ins = t.matmul(stp[0][0:128, j * 8:(j + 1) * 8],
                                       xnfm3[:, ec, j * 128:(j + 1) * 128],
                                       rws[:, ec * 8:(ec + 1) * 8],
                                       start=(ec == 0), stop=(ec == EC - 1))
                        if ec == EC - 1:
                            ins.then_inc(tMM.h, 1)

                em("tensor", _rt)
                rt_mm_last = tMM.bump(1)
            slot_last_xn[(gi - 1) % 2] = xn_c[t0g + ng - 1]

    W("vector", tMM, rt_mm_last)
    em("vector", lambda e: e.tensor_copy(lgt[:], stp[0][0:128, 0:128])
       .then_inc(vP.h, 1))
    W("vector", vP, vP.bump(1))
    W("scalar", vP, vP.n)
    W("scalar", aP, aP.n)
    lg2 = sqscr
    em("scalar", lambda e: e.activation(lg2[:, 0:128], lgt[:], AF.Exp,
                                        bias=0.0, scale=1.0).then_inc(aP.h, 1))
    W("vector", aP, aP.bump(1))
    lg3 = lg2[:, 0:128].rearrange("p (a b) -> p a b", a=NT)
    lgm = lgt[:].rearrange("p (a b) -> p a b", a=NT)

    def _ve2(f):
        em("vector", lambda e, f=f: f(e).then_inc(vP.h, 1))
        W("vector", vP, vP.bump(1))

    e1v = rsc[:, 0:16]
    e2v = rsc[:, 16:32]
    s12 = rsc[:, 32:48]
    e1b = e1v.rearrange("p (a b) -> p a b", b=1).broadcast_to([128, NT, 8])
    _ve2(lambda e: e.reduce_max(e1v, lg3, axis=AX.X))
    _ve2(lambda e: e.tensor_tensor(lgm, lg3, e1b, ALU.is_equal))
    _ve2(lambda e: e.scalar_tensor_tensor(lgm, lgm, -1e30, lg3, ALU.mult, ALU.add))
    _ve2(lambda e: e.reduce_max(e2v, lgm, axis=AX.X))
    _ve2(lambda e: e.tensor_tensor(s12, e1v, e2v, ALU.add))
    _ve2(lambda e: e.reciprocal(s12, s12))
    _ve2(lambda e: e.tensor_tensor(e1v, e1v, s12, ALU.mult))
    _ve2(lambda e: e.tensor_tensor(e2v, e2v, s12, ALU.mult))
    _ve2(lambda e: e.tensor_tensor(scals[:], e1v, e2v, ALU.add))

    if stop == "C":
        return _finish(nc, prog, em, W, sems, dbg, stack,
                       dict(aP=aP, vP=vP, tMM=tMM, stO=stO, stS=stS))
    ffn_regions = [pp[:, 0:512], pp[:, 512:1024],
                   stp[1][:, 0:512], stp[1][:, 512:1024]]
    ffn_guard = [("vP", 0), ("vP", 0), ("aP", last_st_exp), ("aP", last_st_exp)]
    ffn_ctr = [0]

    def ffn_acquire():
        q = ffn_ctr[0] % 4
        ffn_ctr[0] += 1
        gs, gv = ffn_guard[q]
        W("tensor", sems[gs], gv)
        return q


    w1_rdy = {}
    w2info = {}
    x2pre = {}
    dw2r = dw2.rearrange("(f p) h -> p f h", p=128)

    def w1ap(fc):
        wh = w1slc if fc % 4 < 2 else w1slc2
        off = (fc % 2) * EC * 128
        return wh[:, off:off + EC * 128]

    def emit_w1_load(th, fc):
        wsem = fc % 2
        W("gpsimd", dw[wsem], dw[wsem].n)

        def _ldw1(g, fc=fc, wsem=wsem):
            g.dma_start(
                out=w1ap(fc).rearrange("p (e f) -> p e f", e=EC),
                in_=dw1.rearrange("(e p) f -> p e f", p=128)[
                    :, :, fc * 128:(fc + 1) * 128],
            ).then_inc(dw[wsem].h, 16)

        em("gpsimd", _ldw1)
        w1_rdy[(th, fc)] = dw[wsem].bump(16)

    def w2slot(k):
        if k % 3 < 2:
            return wreg, (k % 3) * FC * 256
        return w2s2, 0

    def emit_w2_piece_g(k, pc, tmm_wait=0, vp_wait=0):
        hq = k % 4
        h, off0 = w2slot(k)
        if pc == 0:
            W("gpsimd", dP, dP.n)
        if tmm_wait:
            W("gpsimd", tMM, tmm_wait)
        if vp_wait:
            W("gpsimd", vP, vp_wait)
        dst = h[:, off0 + pc * 1024:off0 + (pc + 1) * 1024].rearrange(
            "p (f x) -> p f x", f=4)
        em("gpsimd", lambda e, dst=dst, hq=hq, pc=pc: e.dma_start(
            out=dst, in_=dw2r[:, pc * 4:(pc + 1) * 4,
                              hq * 256:(hq + 1) * 256]).then_inc(dP.h, 16))
        w2info[k] = (dP, dP.bump(16))

    W("gpsimd", dw[0], dw[0].n)
    W("gpsimd", dw[1], dw[1].n)
    for fc0 in range(4):
        emit_w1_load(0, fc0)

    W("gpsimd", vP, x2_built[NT - 1])
    em("gpsimd", lambda g: g.dma_start(out=biasb[:], in_=db2).then_inc(dC.h, 16))
    B2_RDY = dC.bump(16)
    g3 = gbuf[:].rearrange("p (f t) -> p f t", f=FC)
    gelu_mm = {}
    gelu_done = {}
    ffn2_mm = {}
    out_done = {}
    eserial = [0]

    for th in range(2):
        if th == 0:
            W("sync", stS, stS.n)
            for s in range(2):
                W("sync", dx[s], dx[s].n)
                em("sync", lambda e, s=s: e.dma_start(
                    out=x2r[:, s * 256:(s + 1) * 256],
                    in_=dx2s[s * 128:(s + 1) * 128, 0:256])
                   .then_inc(dx[s].h, 16))
                x2pre[s] = dx[s].bump(16)
        else:
            for s in range(2):
                W("sync", vP, out_done[30 + s])
                W("sync", stS, stS.n)
                W("sync", dx[s], dx[s].n)
                em("sync", lambda e, s=s: e.dma_start(
                    out=x2r[:, s * 256:(s + 1) * 256],
                    in_=dx2s[(8 + s) * 128:(9 + s) * 128, 0:256])
                   .then_inc(dx[s].h, 16))
                x2pre[32 + s] = dx[s].bump(16)
        for fc in range(FC):
            if fc >= 4:
                W("gpsimd", tMM, gelu_mm[(th, fc - 4)])
                emit_w1_load(th, fc)
            if 8 <= fc < 16:
                tw = 0
                if fc == 8:
                    tw = t_wp_last if th == 0 else ffn2_mm[(0, 2, 7)]
                emit_w2_piece_g(0 if th == 0 else 5, fc - 8, tmm_wait=tw)
            elif fc in (24, 26, 28, 30):
                tw = ffn2_mm[(0, 3, 7)] if (th == 1 and fc == 24) else 0
                emit_w2_piece_g(1 if th == 0 else 6, (fc - 24) // 2,
                                tmm_wait=tw)
            for s in range(2):
                par = ffn_acquire()
                W("tensor", dw[fc % 2], w1_rdy[(th, fc)])
                if th == 0 and fc == 0 and s == 0:
                    W("tensor", vP, xn2_evac_last)

                def _f1(t, th=th, fc=fc, s=s, par=par, wap=w1ap(fc)):
                    for ec in range(EC):
                        ins = t.matmul(
                            ffn_regions[par],
                            wap[:, ec * 128:(ec + 1) * 128],
                            xnfm3[:, ec, th * 1024 + s * 512:
                                  th * 1024 + (s + 1) * 512],
                            start=(ec == 0), stop=(ec == EC - 1))
                        if ec == EC - 1:
                            ins.then_inc(tMM.h, 1)

                em("tensor", _f1)
                t_f1 = tMM.bump(1)
                if s == 1:
                    gelu_mm[(th, fc)] = t_f1
                W("scalar", tMM, t_f1)
                if th == 0 and fc == 0 and s == 0:
                    W("scalar", dK, CONSTS2)
                    W("scalar", tMM, attn_mm_end[EC - 1])
                    W("scalar", vP, atr_done[NT - 1])
                    W("scalar", tMM, t_wp_last)
                if th == 1 and fc == 0 and s == 0:
                    W("scalar", tMM, ffn2_mm[(0, 3, 7)])
                o = g3[:, fc, s * 512:(s + 1) * 512]
                em("scalar", lambda e, o=o, p=par, b=b1s[:, fc:fc + 1]:
                   e.activation(o, ffn_regions[p],
                                AF.Gelu_apprx_tanh, bias=b,
                                scale=1.0).then_inc(aP.h, 1))
                ffn_guard[par] = ("aP", aP.bump(1))
        gelu_done[th] = aP.n

        for hq in range(4):
            k4 = th * 4 + hq
            h2, off2 = w2slot(k4)
            if th == 0 and hq == 3:
                for j in range(4):
                    W("gpsimd", tMM, gelu_mm[(0, FC - 4 + j)])
                    emit_w1_load(1, j)
            for ti in range(8):
                if th == 0:
                    if ti < 4:
                        emit_w2_piece_g(hq + 1, 4 + ti)
                    elif hq < 3:
                        tw = ffn2_mm[(0, hq - 1, 7)] if (ti == 4 and hq >= 1) else 0
                        vw = x2_add[NT - 1] if (ti == 4 and hq == 0) else 0
                        emit_w2_piece_g(hq + 2, ti - 4, tmm_wait=tw, vp_wait=vw)
                else:
                    if hq == 0 and ti < 4:
                        emit_w2_piece_g(6, 4 + ti)
                    elif hq == 1:
                        tw = ffn2_mm[(1, 0, 7)] if ti == 0 else 0
                        emit_w2_piece_g(7, ti, tmm_wait=tw)
                absti = th * 8 + ti
                par = ffn_acquire()
                w2s_, w2v_ = w2info[k4]
                W("tensor", w2s_, w2v_)
                W("tensor", aP, gelu_done[th])

                def _f2(t, ti=ti, par=par, h2=h2, off2=off2):
                    for fc in range(FC):
                        ins = t.matmul(
                            ffn_regions[par][:, 0:256],
                            g3[:, fc, ti * 128:(ti + 1) * 128],
                            h2[:, off2 + fc * 256:off2 + (fc + 1) * 256],
                            start=(fc == 0), stop=(fc == FC - 1))
                        if fc == FC - 1:
                            ins.then_inc(tMM.h, 1)

                em("tensor", _f2)
                ffn2_mm[(th, hq, ti)] = tMM.bump(1)
                ser = eserial[0]
                eserial[0] += 1
                xsl = ser % 2
                nser = ser + 1
                if nser % 32 != 0 and nser not in x2pre:
                    lhq, lti = (nser % 32) // 8, nser % 8
                    labs = th * 8 + lti
                    if ser >= 1:
                        W("sync", vP, out_done[ser - 1])
                    W("sync", stS, stS.n)
                    W("sync", dx[nser % 2], dx[nser % 2].n)
                    em("sync", lambda e, labs=labs, lhq=lhq, nxsl=nser % 2:
                       e.dma_start(
                           out=x2r[:, nxsl * 256:(nxsl + 1) * 256],
                           in_=dx2s[labs * 128:(labs + 1) * 128,
                                    lhq * 256:(lhq + 1) * 256])
                       .then_inc(dx[nxsl].h, 16))
                    x2pre[nser] = dx[nser % 2].bump(16)
                dxv = x2pre[ser]
                W("vector", tMM, ffn2_mm[(th, hq, ti)])
                W("vector", dx[xsl], dxv)
                W("vector", dC, B2_RDY)
                osl = ser % 2
                em("vector", lambda e, osl=osl, p=par, hq=hq: e.tensor_tensor(
                    tmpr[:, osl * 256:(osl + 1) * 256],
                    ffn_regions[p][:, 0:256],
                    biasb[:, hq * 256:(hq + 1) * 256],
                    ALU.add).then_inc(vP.h, 1))
                ffn_guard[par] = ("vP", vP.bump(1))
                W("vector", vP, ffn_guard[par][1])
                if ser >= 1:
                    W("vector", stO, 16 * (ser - 1))
                em("vector", lambda e, osl=osl, xsl=xsl, absti=absti:
                   e.scalar_tensor_tensor(
                       outr[:, osl * 256:(osl + 1) * 256],
                       tmpr[:, osl * 256:(osl + 1) * 256],
                       scals[:, absti:absti + 1],
                       x2r[:, xsl * 256:(xsl + 1) * 256],
                       ALU.mult, ALU.add).then_inc(vP.h, 1))
                out_done[ser] = vP.bump(1)
                W("sync", vP, out_done[ser])
                W("sync", stO, stO.n)
                em("sync", lambda e, osl=osl, absti=absti, hq=hq: e.dma_start(
                    out=dy[absti * 128:(absti + 1) * 128, hq * 256:(hq + 1) * 256],
                    in_=outr[:, osl * 256:(osl + 1) * 256]).then_inc(stO.h, 16))
                stO.bump(16)

    W("gpsimd", stO, stO.n)
    W("sync", stO, stO.n)
    if dbg:
        return _finish(nc, prog, em, W, sems, dbg, stack,
                       dict(aP=aP, vP=vP, tMM=tMM, stO=stO, stS=stS))

    return _replay(nc, prog, stack)


def _replay(nc, prog, stack):
    with nc.Block() as blk:
        @blk.sync
        def _(e):
            for f in prog["sync"]:
                f(e)

        @blk.gpsimd
        def _(e):
            for f in prog["gpsimd"]:
                f(e)

        @blk.scalar
        def _(e):
            for f in prog["scalar"]:
                f(e)

        @blk.vector
        def _(e):
            for f in prog["vector"]:
                f(e)

        @blk.tensor
        def _(e):
            for f in prog["tensor"]:
                f(e)
    stack.close()
    return nc


def _finish(nc, prog, em, W, sems, dbg, stack, s):
    aP, vP, tMM, stO = s["aP"], s["vP"], s["tMM"], s["stO"]
    W("sync", aP, aP.n)
    W("sync", vP, vP.n)
    W("sync", tMM, tMM.n)
    ndma = 0
    for name, hh in dbg:
        shp = [hh.shape[0], hh.shape[1]]
        d = nc.dram_tensor("dbg_" + name, shp, hh.dtype, kind="ExternalOutput").ap()
        em("sync", lambda e, d=d, hh=hh: e.dma_start(out=d[:, :], in_=hh[:])
           .then_inc(stO.h, 16))
        ndma += 1
        stO.bump(16)
    W("sync", stO, stO.n)
    return _replay(nc, prog, stack)


def _host_inputs(inputs):
    x = np.asarray(inputs["L0_attn_x"], dtype=np.float32)
    wqT = np.ascontiguousarray(np.asarray(inputs["L0_attn_Wq"], np.float32).T)
    wkT = np.ascontiguousarray(np.asarray(inputs["L0_attn_Wk"], np.float32).T)
    wvT = np.ascontiguousarray(np.asarray(inputs["L0_attn_Wv"], np.float32).T)
    wpT = np.ascontiguousarray(np.asarray(inputs["L0_attn_Wp"], np.float32).T)
    w1 = np.ascontiguousarray(np.asarray(inputs["L0_ffn_w1"], np.float32))
    w2 = np.ascontiguousarray(np.asarray(inputs["L0_ffn_w2"], np.float32))
    rw = np.ascontiguousarray(np.asarray(inputs["L0_ffn_router_w"], np.float32))
    bq = np.asarray(inputs["L0_attn_bq"], np.float32)
    bk = np.asarray(inputs["L0_attn_bk"], np.float32)
    bv = np.asarray(inputs["L0_attn_bv"], np.float32)
    bp = np.asarray(inputs["L0_attn_bp"], np.float32)
    b1 = np.asarray(inputs["L0_ffn_b1"], np.float32)
    b2 = np.asarray(inputs["L0_ffn_b2"], np.float32)

    bqc = np.ascontiguousarray(bq.reshape(EC, 128).T)
    bkc = np.ascontiguousarray(bk.reshape(EC, 128).T)
    b1c = np.ascontiguousarray(b1.reshape(FC, 128).T)
    bvb = np.ascontiguousarray(np.broadcast_to(bv, (128, E)))
    bpb = np.ascontiguousarray(np.broadcast_to(bp, (128, E)))
    b2b = np.ascontiguousarray(np.broadcast_to(b2, (128, E)))
    ident = np.eye(128, dtype=ml_dtypes.bfloat16)

    jr = np.arange(128)[:, None]
    qr = np.arange(384)[None, :]
    interior = ((qr >= jr) & (qr <= jr + 256)).astype(np.float32)

    in_maps = []
    for c in range(8):
        b, half = c // 2, c % 2
        start = half * TOWN
        lo, hi = start - WIN, start + TOWN + WIN
        xin = np.zeros((TKV, E), np.float32)
        vlo, vhi = max(lo, 0), min(hi, N)
        xin[vlo - lo:vhi - lo] = x[b, vlo:vhi]
        v0 = interior if half == 1 else np.zeros_like(interior)
        v2 = interior if half == 0 else np.zeros_like(interior)
        masks = np.stack([v0, interior, v2]).astype(ml_dtypes.bfloat16)
        in_maps.append({
            "x": xin, "wqT": wqT, "wkT": wkT, "wvT": wvT, "wpT": wpT,
            "w1": w1, "w2": w2, "rw": rw,
            "bqc": bqc, "bkc": bkc, "b1c": b1c,
            "bvb": bvb, "bpb": bpb, "b2b": b2b,
            "masks": masks, "ident": ident,
        })
    return in_maps


def kernel(**inputs):
    in_maps = _host_inputs(inputs)
    nc = build_nc()
    res = run_bass_kernel_spmd(nc, in_maps, core_ids=list(range(8)))
    y = np.empty((B, N, E), np.float32)
    for c in range(8):
        b, half = c // 2, c % 2
        y[b, half * TOWN:(half + 1) * TOWN] = res.results[c]["y"]
    return y
